# version 29
# baseline (speedup 1.0000x reference)
import os
from contextlib import ExitStack

import numpy as np
import ml_dtypes

import concourse.bass as bass
import concourse.mybir as mybir
from concourse.bass_utils import run_bass_kernel_spmd

F32 = mybir.dt.float32
BF16 = mybir.dt.bfloat16
AF = mybir.ActivationFunctionType
ALU = mybir.AluOpType
AX = mybir.AxisListType
BFNP = ml_dtypes.bfloat16

NCORES = 8
T = 2048
NT = 16
D = 1024
DI = 2048
DIP = 4384
NH = 32
MAGIC = float(np.float32(1.5 * 2 ** 23))

RACE = os.environ.get("KMK_RACE", "0") == "1"
ENGS = ("sync", "tensor", "vector", "scalar", "gpsimd")


def bcast2(ap2d, reps):
    return ap2d.unsqueeze(2).broadcast_to([ap2d.shape[0], ap2d.shape[1], reps])


def build_program():
    nc = bass.Bass("TRN2", target_bir_lowering=False,
                   detect_race_conditions=RACE)

    def din(name, shape, dt=F32):
        return nc.dram_tensor(name, list(shape), dt, kind="ExternalInput").ap()

    x0_d = din("x0", (NT + 1, 128, D))
    wint_d = din("wint", (D, DIP))
    woutt_d = din("woutt", (DI, D))
    wrt_d = din("wrt", (D, D))
    wkt_d = din("wkt", (D, D))
    wvt_d = din("wvt", (D, D))
    cdiag_d = din("cdiag", (4, 18, 128, 128))
    cbias_fm_d = din("cbias_fm", (128, 18))
    nwrep_d = din("nwrep", (128, D))
    onwrep_d = din("onwrep", (128, DI))
    murep_d = din("murep", (128, 8))
    mukrep_d = din("mukrep", (128, 8))
    dtb_d = din("dtb", (64, 1))
    alog_d = din("alog", (64, 1))
    dprep_d = din("dprep", (128, 32))
    ones1_d = din("ones1", (1, 128))
    onescol_d = din("onescol", (128, 1))
    i128_d = din("i128", (128, 128))
    u01_d = din("u01", (128, 128))
    sel8_d = din("sel8", (8, 1024))
    selprev_d = din("selprev", (8, 1))
    between_d = din("between", (8, 8))
    maskj_d = din("maskj", (8, 1))

    out_d = nc.dram_tensor("out", [T, D], F32, kind="ExternalOutput").ap()

    yp_dram = nc.dram_tensor("yp_dram", [NT, 128, DI], F32).ap()
    sz_dram = nc.dram_tensor("sz_dram", [NT, 128, DI], BF16).ap()
    c_dram = nc.dram_tensor("c_dram", [NT, 128, 128], BF16).ap()
    acs_dram = nc.dram_tensor("acs_dram", [2, 32, 128], F32).ap()
    w8_dram = nc.dram_tensor("w8_dram", [8, 32], F32).ap()
    ln2b_dram = nc.dram_tensor("ln2b_dram", [1, D], BF16).ap()
    gin_S = nc.dram_tensor("gin_S", [128, DI], F32).ap()
    gout_S = nc.dram_tensor("gout_S", [8 * 128, DI], F32).ap()
    gin_as = nc.dram_tensor("gin_as", [1, 32], F32).ap()
    gout_as = nc.dram_tensor("gout_as", [8, 32], F32).ap()
    gin_x1 = nc.dram_tensor("gin_x1", [1, D], F32).ap()
    gout_x1 = nc.dram_tensor("gout_x1", [8, D], F32).ap()

    stack = ExitStack()
    with stack:
        AR = 53000
        arena_t = stack.enter_context(nc.sbuf_tensor("arena", [128, AR], F32))
        arena = arena_t.ap()
        off = [0]

        def af(cols, parts=128, align=16):
            off[0] = (off[0] + align - 1) // align * align
            o = off[0]
            off[0] += cols
            assert off[0] <= AR, f"arena overflow {off[0]}"
            return arena[0:parts, o:o + cols]

        def abf(cols_bf, parts=128, align=16):
            assert cols_bf % 2 == 0
            off[0] = (off[0] + align - 1) // align * align
            o = off[0]
            off[0] += cols_bf // 2
            assert off[0] <= AR, f"arena overflow {off[0]}"
            return arena[0:128, o:o + cols_bf // 2].bitcast(BF16)[0:parts, :]

        cb_fm = af(18)
        nwrep = af(D)
        onwrep = af(DI)
        murep = af(8)
        mukrep = af(8)
        dprep = af(32)
        dtb = af(1, 64)
        alogc = af(1, 64)
        nega = af(1, 64)
        ones1 = af(128, 1)
        onescol = af(1)
        u01b = abf(128)
        sel8b = af(1024, 8)
        mu1_st = af(NT + 1)
        r1_st = af(NT + 1)
        g_st = af(NT * 32)
        run_row = af(32, 1)
        rowb = af(32)
        rowdec = af(32, 1)
        wpp = af(32)
        u01f = af(128)
        i128 = af(128)
        ln2h15 = abf(16, 128)
        x0t = af(2 * D)
        selprev = af(1, 8)
        between = af(8, 8)
        maskj = af(1, 8)

        wbase = off[0]
        w_in_bf = abf(8 * DIP)
        wq_v = w_in_bf.rearrange("p (c d) -> p c d", c=8)
        cdiag_bf = abf(4 * 18 * 128)
        cdv = cdiag_bf.rearrange("p (k j t) -> p k j t", k=4, j=18)
        passA_wend = off[0]
        off[0] = wbase
        woutt_bf = abf(16 * D)
        wo_v = woutt_bf.rearrange("p (c d) -> p c d", c=16)
        wr_bf = abf(8 * D)
        wrv = wr_bf.rearrange("p (c d) -> p c d", c=8)
        wk_bf = abf(8 * D)
        wkv = wk_bf.rearrange("p (c d) -> p c d", c=8)
        wv_bf = abf(8 * D)
        wvv = wv_bf.rearrange("p (c d) -> p c d", c=8)
        off[0] = max(off[0], passA_wend)

        tbase = off[0]
        ppA = af(D)
        ppB = af(D)
        qblock = af(1536)
        q1 = qblock[:, 0:512].bitcast(BF16)
        q1T2 = [qblock[:, 512:1024].bitcast(BF16),
                qblock[:, 1024:1536].bitcast(BF16)]
        q1Tv2 = [t.rearrange("p (j t) -> p j t", j=8) for t in q1T2]
        szt = abf(DI)
        xdp = szt
        dt_tm2 = af(128)
        xbcg = abf(2 * 2304)
        xbv = xbcg.rearrange("p (b c) -> p b c", b=2)
        convin = abf(18 * 272)
        cvv = convin.rearrange("p (j t) -> p j t", j=18)
        xsil = abf(16 * 256)
        xsv = xsil.rearrange("p (j t) -> p j t", j=16)
        bcfm = abf(2 * 256)
        bcv = bcfm.rearrange("p (j t) -> p j t", j=2)
        x_tmP = [abf(DI), abf(DI)]
        xtvP = [t.rearrange("p (j t) -> p j t", j=16) for t in x_tmP]
        u_t = abf(DI)
        m_t = abf(2 * 512)
        m_v = m_t.rearrange("p (b d) -> p b d", b=2)
        pm_bfP = [abf(128), abf(128)]
        b_tmP = [abf(128), abf(128)]
        y_p = af(DI)
        acs8P = [af(512, 8), af(512, 8)]
        asum_row = af(32, 1)
        dtx_fm = af(128, 64)
        sp1_fm = af(128, 64)
        sp2_fm = af(128, 64)
        dta_fm = af(128, 64)
        a_fm = sp1_fm
        acs_fm = dtx_fm
        acs_tm = af(32)
        nacs_tmP = [af(32), af(32)]
        dta_tmP = [af(32), af(32)]
        gtmp = af(32)
        eacsP = [af(32), af(32)]
        ds_tmP = [af(32), af(32)]
        dec_repP = [af(32), af(32)]
        ones_a = af(128, 64)
        pstate = af(DI)
        pstate_bf = abf(DI)
        passA_tend = off[0]
        off[0] = tbase
        ypB = af(DI)
        stmp = ypB
        tcorr = af(DI)
        szB = abf(DI)
        cfmB = abf(128)
        qb2 = af(4096)
        q2 = qb2[:, 0:1024].bitcast(BF16)
        q2T = qb2[:, 1024:2048].bitcast(BF16)
        q2Tv = q2T.rearrange("p (j t) -> p j t", j=16)
        x1b2 = qb2[:, 2048:4096]
        x1bv = [x1b2[:, 0:1024], x1b2[:, 1024:2048]]
        x1b3 = af(D)
        t1b = af(D)
        sjbuf = qb2[:, 0:4096]
        sjv = sjbuf.rearrange("p (b d) -> p b d", b=2)
        sbgx = qb2[0:8, 0:1024]
        x1brow = qb2[0:1, 1024:2048]
        wflat = qb2[0:1, 3584:3840]
        ln2tm = abf(D)
        ln2fm = abf(8 * 144)
        lfv = ln2fm.rearrange("p (j t) -> p j t", j=8)
        dmix = ln2tm
        dmv = dmix.rearrange("p (j t) -> p j t", j=8)
        xr_fm = abf(8 * 128)
        xrv = xr_fm.rearrange("p (j t) -> p j t", j=8)
        xk_fm = abf(8 * 128)
        xkv = xk_fm.rearrange("p (j t) -> p j t", j=8)
        r_bf = abf(D)
        kact = abf(D)
        kab = af(512)
        sb8_32 = kab[0:8, 0:32]
        sbw8 = kab[0:8, 32:64]
        wrep = kab[:, 256:512]
        kactT = abf(8 * 128)
        kTv = kactT.rearrange("p (j t) -> p j t", j=8)
        outb = af(D)
        sin_bf = abf(DI)
        x1c15 = af(D)
        ln2brow = ppB[0:1, 0:D]
        ln2bb = ppB[0:1, 0:512].bitcast(BF16) if False else None
        krelu = af(D)
        lnsink = af(D)
        off[0] = max(off[0], passA_tend)
        stg0 = arena[:, tbase:tbase + DIP]
        stg1 = arena[:, tbase + DIP:tbase + 2 * DIP]
        stgs = [stg0, stg1]
        partial = arena[:, tbase + 2 * DIP:tbase + 2 * DIP + 32]
        cdiag_f = arena[:, tbase + 2 * DIP + 32:
                        tbase + 2 * DIP + 32 + 2 * 18 * 128]
        cfv = cdiag_f.rearrange("p (k j t) -> p k j t", k=2, j=18)
        ln2bb = tcorr[0:1, 0:512].bitcast(BF16)[0:1, 0:D]

        psA_t = stack.enter_context(nc.psum_tensor("psA", [128, 2048], F32))
        psB_t = stack.enter_context(nc.psum_tensor("psB", [128, 2048], F32))
        psA = psA_t.ap()
        psB = psB_t.ap()
        bkA = [psA[:, i * 512:(i + 1) * 512] for i in range(4)]
        bkB = [psB[:, i * 512:(i + 1) * 512] for i in range(4)]

        semh = {}
        cnt = {}
        for n in ["LDW", "LDX", "BNC", "ACS", "STO", "MM", "VE", "SE", "GP",
                  "CC"]:
            semh[n] = stack.enter_context(nc.semaphore(n))
            cnt[n] = 0
        SCR = {e: [] for e in ENGS}
        DMASEMS = {"LDW", "LDX", "BNC", "ACS", "STO"}
        dma_maxreq = {n: 0 for n in DMASEMS}
        dma_drained = {n: 0 for n in DMASEMS}
        M = {}

        def E(eng, fn, w=(), i=None, ik=1):
            wl = []
            for x in w:
                t = x if isinstance(x, tuple) else (x, cnt[x])
                if t[0] in DMASEMS:
                    t = (t[0], cnt[t[0]])
                    if t[1] > dma_maxreq[t[0]]:
                        dma_maxreq[t[0]] = t[1]
                wl.append(t)
            if i in DMASEMS and dma_maxreq[i] > dma_drained[i]:
                tgt = dma_maxreq[i]
                SCR[eng].append(([(i, tgt)], None, None))
                dma_drained[i] = tgt
            if i is not None:
                cnt[i] += ik
            SCR[eng].append((wl, fn, (i, ik) if i is not None else None))

        def snap(n):
            return (n, cnt[n])

        def mark(key, sem):
            M[key] = (sem, cnt[sem])

        def mk(key):
            return M.get(key, ("VE", 0))

        def DMA(sem, outap, inap, w=(), **kw):
            E("sync", lambda e, o=outap, ii=inap, k=kw:
              e.dma_start(out=o, in_=ii, **k), w=w, i=sem, ik=16)

        def VOP(meth, *args, w=(), **kw):
            E("vector", lambda e, a=args, k=kw: getattr(e, meth)(*a, **k),
              w=w, i="VE")

        def GOP(meth, *args, w=(), **kw):
            E("gpsimd", lambda e, a=args, k=kw: getattr(e, meth)(*a, **k),
              w=w, i="GP")

        def SOP(*args, w=(), **kw):
            E("scalar", lambda e, a=args, k=kw: e.activation(*a, **k),
              w=w, i="SE")

        def MMOP(out, lhsT, rhs, start=True, stop=True, w=()):
            E("tensor", lambda e, o=out, l=lhsT, r=rhs, s1=start, s2=stop:
              e.matmul(o, l, r, start=s1, stop=s2), w=w, i="MM")

        def TROP(out, inp, ident, w=()):
            E("tensor", lambda e, o=out, ii=inp, dd=ident: e.transpose(o, ii, dd),
              w=w, i="MM")

        i128b = u01f.bitcast(BF16)[:, 0:128]

        def PET(dst3, srcs, banks, w=(), evac_w=()):
            n = len(srcs)
            i = 0
            r = 0
            firstw = list(w)
            evs = []
            while i < n:
                m = min(8, n - i)
                bank = banks[r % len(banks)].bitcast(BF16)
                b3 = bank.rearrange("p (k t) -> p k t", k=8)
                roundw = list(firstw) + [snap("VE")]
                firstw = []
                for k in range(m):
                    E("tensor",
                      lambda e, ss=srcs[i + k], bb=bank, kk=k:
                      e.transpose(bb[:, kk * 128:(kk + 1) * 128], ss, i128b),
                      w=roundw if k == 0 else (), i="MM")
                mmv = snap("MM")
                VOP("tensor_copy", dst3[:, i:i + m, :], b3[:, 0:m, :],
                    w=[mmv] + list(evac_w))
                evs.append(snap("VE"))
                i += m
                r += 1

        def rsqrt_rows(dst, src, eps, w=()):
            VOP("tensor_scalar", src, src, eps, None, ALU.add, w=w)
            SOP(src, src, AF.Ln, w=[snap("VE")])
            SOP(dst, src, AF.Exp, scale=-0.5)

        def recip_rows(dst, src, w=()):
            SOP(src, src, AF.Ln, w=w)
            SOP(dst, src, AF.Exp, scale=-1.0)

        for dst, src in [(cb_fm, cbias_fm_d), (nwrep, nwrep_d),
                         (onwrep, onwrep_d), (murep, murep_d),
                         (mukrep, mukrep_d), (dprep, dprep_d), (dtb, dtb_d),
                         (alogc, alog_d), (ones1, ones1_d),
                         (onescol, onescol_d), (i128, i128_d), (u01f, u01_d),
                         (sel8b, sel8_d), (selprev, selprev_d),
                         (between, between_d), (maskj, maskj_d)]:
            DMA("LDW", dst, src)
        ld0 = snap("LDW")
        VOP("tensor_copy", u01b[:], u01f[:], w=[ld0])
        VOP("tensor_copy", i128b, i128, w=[ld0])
        VOP("memset", ones_a[:], 1.0)
        VOP("memset", run_row[:], 0.0)
        VOP("memset", pstate[:], 0.0)
        VOP("memset", pstate_bf[:], 0.0)
        SOP(nega[:], alogc[:], AF.Exp, w=[ld0])
        VOP("tensor_scalar_mul", nega[:], nega[:], -1.0, w=[snap("SE")])
        for half in range(2):
            DMA("LDW", cfv,
                cdiag_d.rearrange("k j a t -> a k j t")[:, 2 * half:2 * half + 2],
                w=[snap("VE")])
            VOP("tensor_copy", cdv[:, 2 * half:2 * half + 2], cfv,
                w=[snap("LDW")])
        wiv = wint_d.rearrange("(ch cl) d -> cl ch d", cl=128)
        for cc in range(8):
            DMA("LDW", stgs[cc % 2], wiv[:, cc, :], w=[snap("VE")])
            VOP("tensor_reduce", partial[:, cc:cc + 1], stgs[cc % 2][:, :],
                AX.X, ALU.add, apply_absolute_value=True, w=[snap("LDW")])
        VOP("tensor_reduce", partial[:, 8:9], partial[:, 0:8], AX.X, ALU.add,
            w=[snap("VE")])
        MMOP(psA[0:1, 0:1], onescol, partial[:, 8:9], w=[snap("VE")])
        VOP("tensor_scalar", rowb[0:1, 0:1], psA[0:1, 0:1], 1.0 / (DIP * D),
            1e-5, ALU.mult, ALU.max, w=[snap("MM")])
        recip_rows(rowb[0:1, 1:2], rowb[0:1, 0:1], w=[snap("VE")])
        MMOP(psA[:, 8:9], ones1, rowb[0:1, 1:2], w=[snap("SE")])
        VOP("tensor_copy", rowb[:, 2:3], psA[:, 8:9], w=[snap("MM")])
        for cc in range(8):
            DMA("LDW", stgs[cc % 2], wiv[:, cc, :], w=[snap("VE")])
            VOP("tensor_scalar", stgs[cc % 2][:, :], stgs[cc % 2][:, :],
                rowb[:, 2:3], MAGIC, ALU.mult, ALU.add, w=[snap("LDW")])
            VOP("tensor_scalar", stgs[cc % 2][:, :], stgs[cc % 2][:, :],
                MAGIC - 1.0, MAGIC, ALU.max, ALU.subtract)
            VOP("tensor_scalar", wq_v[:, cc, :], stgs[cc % 2][:, :],
                1.0, None, ALU.min)
        mark("wprep", "VE")

        ipsevs = {}

        def prepipe(tau):
            c = tau
            slot = (tau + 1) % 2
            par = (tau + 1) % 2
            rb = rowb[:, par * 16:par * 16 + 16]
            xt = x0t.rearrange("p (b d) -> p b d", b=2)[:, slot, :]
            DMA("LDX", xt, x0_d[tau + 1],
                w=[mk(f"ppve{c-2}"), mk(f"ppse{c-2}")])
            lx = snap("LDX")
            st = tau + 1
            VOP("tensor_reduce", rb[:, 0:1], xt, AX.X, ALU.add,
                w=[lx, mk(f"ppse{c-1}"), mk(f"ppgp{c-1}"),
                   mk(f"ipse{c-2}")])
            VOP("scalar_tensor_tensor", ppA[:, :], xt, 1.0, xt,
                ALU.mult, ALU.mult, accum_out=rb[:, 1:2])
            sv = snap("VE")
            VOP("tensor_scalar", mu1_st[:, st:st + 1], rb[:, 0:1], 1.0 / D,
                None, ALU.mult, w=[sv])
            VOP("tensor_tensor", rb[:, 2:3], mu1_st[:, st:st + 1],
                mu1_st[:, st:st + 1], ALU.mult)
            VOP("tensor_scalar", rb[:, 3:4], rb[:, 1:2], 1.0 / D, None,
                ALU.mult)
            VOP("tensor_tensor", rb[:, 3:4], rb[:, 3:4], rb[:, 2:3],
                ALU.subtract)
            sv = snap("VE")
            VOP("tensor_scalar", rb[:, 4:5], rb[:, 3:4], 1e-5, None, ALU.add)
            SOP(rb[:, 4:5], rb[:, 4:5], AF.Ln, w=[snap("VE")])
            SOP(r1_st[:, st:st + 1], rb[:, 4:5], AF.Exp, scale=-0.5)
            se = snap("SE")
            VOP("tensor_tensor", rb[:, 5:6], r1_st[:, st:st + 1],
                r1_st[:, st:st + 1], ALU.mult, w=[se])
            VOP("tensor_tensor", rb[:, 5:6], rb[:, 5:6], rb[:, 3:4], ALU.mult)
            VOP("tensor_scalar", rb[:, 5:6], rb[:, 5:6], 1e-6, None, ALU.add)
            SOP(rb[:, 5:6], rb[:, 5:6], AF.Ln, w=[snap("VE")])
            SOP(rb[:, 6:7], rb[:, 5:6], AF.Exp, scale=-0.5)
            VOP("tensor_tensor", rb[:, 7:8], r1_st[:, st:st + 1], rb[:, 6:7],
                ALU.mult, w=[snap("SE")])
            VOP("tensor_tensor", rb[:, 8:9], mu1_st[:, st:st + 1], rb[:, 7:8],
                ALU.mult)
            VOP("tensor_scalar_mul", rb[:, 8:9], rb[:, 8:9], -1.0)
            SOP(ppA[:, :], xt, AF.Identity, bias=rb[:, 8:9], scale=rb[:, 7:8],
                w=[snap("VE")])
            GOP("tensor_tensor", ppB[:, :], ppA[:, :], nwrep, ALU.mult,
                w=[snap("SE")])
            gv = snap("GP")
            mark(f"ppgp{c}", "GP")
            VOP("tensor_reduce", rb[:, 0:1], ppB[:, :], AX.X, ALU.add, w=[gv])
            VOP("scalar_tensor_tensor", ppA[:, :], ppB[:, :], 1.0, ppB[:, :],
                ALU.mult, ALU.mult, accum_out=rb[:, 1:2])
            sv = snap("VE")
            VOP("tensor_scalar", rb[:, 2:3], rb[:, 0:1], 1.0 / D, None,
                ALU.mult, w=[sv])
            VOP("tensor_tensor", rb[:, 3:4], rb[:, 2:3], rb[:, 2:3], ALU.mult)
            VOP("tensor_scalar", rb[:, 4:5], rb[:, 1:2], 1.0 / D, None,
                ALU.mult)
            VOP("tensor_tensor", rb[:, 4:5], rb[:, 4:5], rb[:, 3:4],
                ALU.subtract)
            VOP("tensor_scalar", rb[:, 4:5], rb[:, 4:5], 1e-5, None, ALU.add)
            SOP(rb[:, 4:5], rb[:, 4:5], AF.Ln, w=[snap("VE")])
            SOP(rb[:, 5:6], rb[:, 4:5], AF.Exp, scale=-0.5)
            VOP("tensor_tensor", rb[:, 9:10], rb[:, 2:3], rb[:, 5:6],
                ALU.mult, w=[snap("SE")])
            VOP("tensor_scalar_mul", rb[:, 9:10], rb[:, 9:10], -1.0)
            SOP(ppA[:, :], ppB[:, :], AF.Identity, bias=rb[:, 9:10],
                scale=rb[:, 5:6], w=[snap("VE")])
            VOP("tensor_reduce", rb[:, 10:11], ppA[:, :], AX.X, ALU.max,
                apply_absolute_value=True, w=[snap("SE")])
            sv = snap("VE")
            VOP("tensor_scalar", rb[:, 10:11], rb[:, 10:11], 1e-5, None,
                ALU.max, w=[sv])
            VOP("tensor_scalar", rb[:, 13:14], rb[:, 10:11], 1.0 / 127.0,
                None, ALU.mult)
            VOP("tensor_copy", rb[:, 11:12], rb[:, 10:11])
            SOP(rb[:, 11:12], rb[:, 11:12], AF.Ln, w=[snap("VE")])
            SOP(rb[:, 12:13], rb[:, 11:12], AF.Exp, scale=-1.0)
            VOP("tensor_scalar", rb[:, 12:13], rb[:, 12:13], 127.0, None,
                ALU.mult, w=[snap("SE")])
            VOP("tensor_scalar", ppB[:, :], ppA[:, :], rb[:, 12:13], MAGIC,
                ALU.mult, ALU.add, w=[mk(f"q1mm{c-1}")])
            VOP("tensor_scalar", ppB[:, :], ppB[:, :], MAGIC - 128.0, MAGIC,
                ALU.max, ALU.subtract)
            VOP("tensor_scalar", q1[:, :], ppB[:, :], 127.0, None, ALU.min)
            mark(f"ppve{c}", "VE")
            mark(f"ppse{c}", "SE")
            qv = snap("VE")
            PET(q1Tv2[(tau + 16) % 2],
                [q1[:, j * 128:(j + 1) * 128] for j in range(8)],
                [bkA[0]], w=[qv, mk(f"ipse{c-1}"), mk(f"q1mm{c-2}")])
            mark(f"q1mm{c}", "MM")
            mark(f"q1ve{c}", "VE")

        def in_proj_dg(tau, dg):
            c = tau
            par = (tau + 1) % 2
            rb = rowb[:, par * 16:par * 16 + 16]
            recip = rb[:, 13:14]
            tpq = mk(f"q1ve{c}")
            q1Tv = q1Tv2[(tau + 16) % 2]
            xv = xbv[:, tau % 2, :]
            g = tau // 2
            sevs = ipsevs[tau]
            d0 = dg * 512
            dw = min(512, DIP - d0)
            bank = bkA[dg % 2][:, 0:dw]
            if dg < 2:
                wmm = [tpq, mk(f"ipse{c-1}")]
            else:
                wmm = [tpq, sevs[dg - 2]]
            for ccx in range(8):
                MMOP(bank, q1Tv[:, ccx, :], wq_v[:, ccx, d0:d0 + dw],
                     start=(ccx == 0), stop=(ccx == 7),
                     w=wmm if ccx == 0 else ())
            mmv = snap("MM")
            if dg < 4:
                SOP(szt[:, d0:d0 + 512], bank, AF.Identity, bias=0.0,
                    scale=recip,
                    w=[mmv, mk(f"sz{c-1}"),
                       mk(f"xdpve{((c - 4) // 2) * 2 - 1}")]
                    if dg == 0 else [mmv])
                if dg == 3:
                    SOP(szt[:, :], szt[:, :], AF.Silu)
            elif dg < 8:
                SOP(xv[:, d0 - DI:d0 - DI + 512], bank, AF.Identity,
                    bias=0.0, scale=recip,
                    w=[mmv, mk(f"cvpet{g-1}")] if dg == 4 else [mmv])
            else:
                SOP(xv[:, 2048:2304], bank[:, 0:256], AF.Identity,
                    bias=0.0, scale=recip, w=[mmv])
                SOP(dt_tm2[:, (tau % 4) * 32:(tau % 4) * 32 + 32],
                    bank[:, 256:288], AF.Identity, bias=0.0, scale=recip,
                    w=[mk(f"dtmm{c-4}")])
            sevs.append(snap("SE"))
            if dg == 3:
                DMA("BNC", sz_dram[tau], szt[:, :], w=[snap("SE")])
                mark(f"sz{c}", "BNC")
            if dg == 8:
                mark(f"ipse{c}", "SE")

        def ip_fillers(tau):
            return [(lambda dg=dg: in_proj_dg(tau, dg)) for dg in range(4, 9)]

        def in_proj(tau, part=None):
            c = tau
            par = (tau + 1) % 2
            rb = rowb[:, par * 16:par * 16 + 16]
            recip = rb[:, 13:14]
            tpq = mk(f"q1ve{c}")
            q1Tv = q1Tv2[(tau + 16) % 2]
            if tau == -1:
                for j in range(18):
                    bank = bkA[j % 2]
                    wmm = [tpq] if j < 2 else [tpq, snap("SE")]
                    for ccx in range(8):
                        MMOP(bank[:, 0:128],
                             wq_v[:, ccx, DI + j * 128:DI + (j + 1) * 128],
                             q1Tv[:, ccx, :], start=(ccx == 0), stop=(ccx == 7),
                             w=wmm if ccx == 0 else ())
                    SOP(cvv[:, j, 13:16], bank[:, 125:128], AF.Identity,
                        bias=0.0, scale=recip, w=[snap("MM")])
                mark("ipse-1", "SE")
                mark("cvse-1", "SE")
                return
            if part in (0, None):
                ipsevs[tau] = []
            dgr = (range(9) if part is None else
                   (range(0, 4) if part == 0 else range(4, 9)))
            for dg in dgr:
                in_proj_dg(tau, dg)

        cvsevs = {}

        def conv_pet(g):
            PET(cvv[:, :, 16:16 + 128],
                [xbv[:, 0, j * 128:(j + 1) * 128] for j in range(18)],
                [bkA[2], bkA[3]],
                w=[mk(f"ipse{2*g}"), mk(f"halo{g-1}"), mk(f"cvse{g-1}"),
                   mk(f"fve{2*g-1}")])
            PET(cvv[:, :, 16 + 128:16 + 256],
                [xbv[:, 1, j * 128:(j + 1) * 128] for j in range(18)],
                [bkA[2], bkA[3]], w=[mk(f"ipse{2*g+1}")])
            mark(f"cvpet{g}", "MM")
            mark(f"cvpve{g}", "VE")

        def conv_j(g, j):
            tpx = mk(f"cvpve{g}")
            sevs = cvsevs[g]
            bank = bkA[2 + (j % 2)][:, 0:256]
            wmm = ([tpx, mk(f"cvse{g-1}"), mk(f"fve{2*g-1}")]
                   if j < 2 else [tpx, sevs[j - 2]])
            for k in range(4):
                MMOP(bank, cdv[:, k, j, :], cvv[:, j, 13 + k:13 + k + 256],
                     start=(k == 0), stop=(k == 3),
                     w=wmm if k == 0 else ())
            mmv = snap("MM")
            if j < 16:
                SOP(xsv[:, j, :], bank, AF.Identity, bias=cb_fm[:, j:j + 1],
                    scale=1.0,
                    w=[mmv, mk(f"fmm{2*g-1}"), mk(f"cbnc{2*g-1}"),
                       mk(f"rtmm{2*g-1}")]
                    if j == 0 else [mmv])
            else:
                SOP(bcv[:, j - 16, :], bank, AF.Identity,
                    bias=cb_fm[:, j:j + 1], scale=1.0, w=[mmv])
            sevs.append(snap("SE"))
            if j == 17:
                SOP(xsil[:, :], xsil[:, :], AF.Silu)
                SOP(bcfm[:, :], bcfm[:, :], AF.Silu)
                mark(f"cvse{g}", "SE")
                mark(f"cvmm{g}", "MM")
                GOP("tensor_copy", cvv[:, :, 13:16], cvv[:, :, 269:272],
                    w=[snap("SE"), mk(f"cvmm{g}")])
                mark(f"halo{g}", "GP")

        def cv_fillers(g):
            cvsevs[g] = []
            out = []
            for j0 in range(0, 18, 3):
                out.append(lambda j0=j0: [conv_j(g, j)
                                          for j in range(j0, j0 + 3)])
            return out

        def conv_mm(g, part=None):
            if part in (0, None):
                cvsevs[g] = []
            jr = (range(18) if part is None else
                  (range(0, 9) if part == 0 else range(9, 18)))
            for j in jr:
                conv_j(g, j)

        def ssd_front1(c):
            par = c % 2
            g = c // 2
            cse = mk(f"cvse{g}")
            if par == 0:
                pbanks = [bkB[2]]
                pw = [cse, mk(f"ydve{c-1}")]
            else:
                pbanks = [bkA[2], bkA[3]]
                pw = [cse, mk(f"cvpve{g+1}")]
            PET(xtvP[par],
                [xsv[:, j, par * 128:(par + 1) * 128] for j in range(16)],
                pbanks, w=pw, evac_w=[mk(f"xdpgp{c-2}")])
            PET(b_tmP[par].rearrange("p (a t) -> p a t", a=1),
                [bcv[:, 0, par * 128:(par + 1) * 128]],
                pbanks, w=[mk(f"smm{c-1}")], evac_w=[mk(f"smm{c-2}")])
            DMA("BNC", c_dram[c], bcv[:, 1, par * 128:(par + 1) * 128],
                w=[cse])
            mark(f"cbnc{c}", "BNC")
            if par == 0:
                TROP(psB[0:64, 0:128],
                     dt_tm2[:, (c % 4) * 32:(c % 4) * 32 + 64],
                     i128, w=[mk(f"ipse{c}"), mk(f"ipse{c+1}"),
                              mk(f"b0ve{c-1}")])
                mark(f"dtmm{c}", "MM")
                M[f"dtmm{c+1}"] = M[f"dtmm{c}"]
                VOP("tensor_copy", dtx_fm, psB[0:64, 0:128],
                    w=[snap("MM"), mk(f"acs{c-1}"), mk(f"fmm{c-1}")])
                SOP(dtx_fm, dtx_fm, AF.Identity, bias=dtb, scale=1.0,
                    w=[snap("VE")])
                se = snap("SE")
                VOP("tensor_scalar_mul", sp1_fm, dtx_fm, -1.0, w=[se])
                VOP("tensor_tensor", sp1_fm, dtx_fm, sp1_fm, ALU.max)
                SOP(sp2_fm, sp1_fm, AF.Exp, bias=0.0, scale=-1.0,
                    w=[snap("VE")])
                VOP("tensor_scalar", sp2_fm, sp2_fm, 1.0, None, ALU.add,
                    w=[snap("SE")])
                SOP(sp2_fm, sp2_fm, AF.Ln, bias=0.0, scale=1.0,
                    w=[snap("VE")])
                se = snap("SE")
                VOP("tensor_scalar", dta_fm, dtx_fm, 0.0, None, ALU.max,
                    w=[se, mk(f"fmm{c-1}")])
                VOP("tensor_tensor", dta_fm, dta_fm, sp2_fm, ALU.add)
                VOP("tensor_scalar", a_fm, dta_fm, nega, None, ALU.mult)
                VOP("tensor_tensor_scan", acs_fm, ones_a, a_fm, 0.0,
                    ALU.mult, ALU.add, w=[snap("VE")])
                mark(f"scan{c}", "VE")
                M[f"scan{c+1}"] = M[f"scan{c}"]
            sva = mk(f"scan{c}")
            DMA("ACS", acs_dram[par],
                acs_fm[par * 32:(par + 1) * 32, :],
                w=[sva, mk(f"acs{c-2}")])
            DMA("ACS", acs8P[par],
                acs_dram[par].rearrange("h t -> (h t)").rearrange(
                    "(g x) -> g x", g=8),
                w=[snap("ACS"), mk(f"rtmm{c-2}")])
            DMA("ACS", asum_row,
                acs_dram[par].rearrange("h t -> t h")[127:128, :],
                w=[mk(f"fmm{c-1}"), mk(f"fse{c-1}")],
                allow_slow_non_contiguous=True)
            mark(f"acs8{c}", "ACS")
            mark(f"acs{c}", "ACS")

        def ssd_front2(c):
            par = c % 2
            g = c // 2
            cse = mk(f"cvse{g}")
            sva = mk(f"scan{c}")
            acsf = mk(f"acs8{c}")
            idn = i128[par * 32:(par + 1) * 32, par * 32:(par + 1) * 32]
            TROP(psB[0:128, 128:160], acs_fm[par * 32:(par + 1) * 32, :],
                 idn, w=[sva])
            TROP(psB[0:128, 160:192], dta_fm[par * 32:(par + 1) * 32, :],
                 idn)
            mmv = snap("MM")
            VOP("tensor_copy", acs_tm, psB[:, 128:160],
                w=[mmv, mk(f"fse{c-1}")])
            VOP("tensor_scalar_mul", nacs_tmP[par], acs_tm, -1.0,
                w=[mk(f"rtse{c-2}")])
            VOP("tensor_copy", dta_tmP[par], psB[:, 160:192],
                w=[mk(f"umgp{c-2}")])
            svt = snap("VE")
            MMOP(psB[:, 192:224], ones1, run_row[0:1, 0:32], w=[svt])
            VOP("tensor_tensor", gtmp, acs_tm, psB[:, 192:224], ALU.add,
                w=[snap("MM"), mk(f"fse{c-1}")])
            SOP(g_st[:, 32 * c:32 * c + 32], gtmp, AF.Exp, w=[snap("VE")])
            SOP(eacsP[par], acs_tm, AF.Exp, w=[mk(f"yove{c-2}")])
            MMOP(psB[:, 224:256], ones1, asum_row[0:1, 0:32],
                 w=[acsf, snap("SE")])
            VOP("tensor_tensor", gtmp, psB[:, 224:256], acs_tm, ALU.subtract,
                w=[snap("MM"), snap("SE")])
            SOP(ds_tmP[par], gtmp, AF.Exp,
                w=[snap("VE"), mk(f"uvgp{c-2}")])
            SOP(rowdec, asum_row, AF.Exp, w=[acsf, mk(f"fmm{c-1}")])
            MMOP(psB[:, 256:288], ones1, rowdec, w=[snap("SE")])
            VOP("tensor_copy", dec_repP[par], psB[:, 256:288],
                w=[snap("MM"), mk(f"psgp{c-2}")])
            VOP("tensor_tensor", run_row, run_row, asum_row, ALU.add,
                w=[snap("MM"), acsf])
            MMOP(psB[:, 288:416], bcv[:, 0, par * 128:(par + 1) * 128],
                 bcv[:, 1, par * 128:(par + 1) * 128],
                 w=[cse, snap("VE")])
            VOP("tensor_tensor", pm_bfP[par], psB[:, 288:416], u01b, ALU.mult,
                w=[snap("MM")])
            mark(f"b0ve{c}", "VE")
            mark(f"fve{c}", "VE")
            mark(f"fmm{c}", "MM")
            mark(f"fse{c}", "SE")

        def ssd_back1(c):
            par = c % 2
            g = c // 2
            yoev = []
            for q in range(4):
                bank = bkB[1 if q % 2 == 0 else 3]
                if q == 0:
                    wmm = [mk(f"cvse{g}"), mk(f"rtse{c-1}"),
                           mk(f"pbfq{c-1}_0"), mk(f"pbfq{c-1}_2"),
                           mk("wprep")]
                elif q >= 2:
                    wmm = [yoev[q - 2], mk(f"pbfq{c-1}_{q}")]
                else:
                    wmm = [mk(f"pbfq{c-1}_1"), mk(f"pbfq{c-1}_3")]
                MMOP(bank, bcv[:, 1, par * 128:(par + 1) * 128],
                     pstate_bf[:, q * 512:(q + 1) * 512], w=wmm)
                VOP("tensor_tensor",
                    y_p[:, q * 512:(q + 1) * 512].rearrange(
                        "p (h q2) -> p h q2", h=8),
                    bank.rearrange("p (h q2) -> p h q2", h=8),
                    bcast2(eacsP[par][:, q * 8:(q + 1) * 8], 64), ALU.mult,
                    w=[snap("MM"), mk(f"ypbnc{c-1}"), mk(f"fse{c}")]
                    if q == 0 else [snap("MM")])
                yoev.append(snap("VE"))
            mark(f"yove{c}", "VE")
            GOP("tensor_tensor", u_t.rearrange("p (h q) -> p h q", h=32),
                x_tmP[par].rearrange("p (h q) -> p h q", h=32),
                bcast2(dta_tmP[par], 64), ALU.mult,
                w=[mk(f"fve{c}"), mk(f"smm{c-1}")])
            mark(f"umgp{c}", "GP")

        def ssd_back2(c, fillers=()):
            par = c % 2
            fillers = list(fillers)
            acsf = mk(f"acs8{c}")
            tau_last = min(2 * (c // 2) + 5, 15)
            GOP("tensor_tensor", xdp.rearrange("p (h q) -> p h q", h=32),
                x_tmP[par].rearrange("p (h q) -> p h q", h=32),
                bcast2(dprep, 64), ALU.mult,
                w=[mk(f"fve{c}"), mk(f"xdpve{c-1}"), mk(f"sz{tau_last}")])
            mark(f"xdpgp{c}", "GP")
            VOP("tensor_tensor", y_p, y_p, xdp, ALU.add,
                w=[snap("GP"), mk(f"yove{c}")])
            mark(f"xdpve{c}", "VE")
            grp_se = []
            grp_ye = []
            grp_sv = []
            grp_sttv = []

            def lam_rt(g8):
                bank = bkB[1 if g8 % 2 == 0 else 3]
                MMOP(bank, sel8b[:, g8 * 128:(g8 + 1) * 128], acs8P[par],
                     w=[acsf, mk(f"yove{c}"), mk(f"rtse{c-1}")] if g8 < 2 else
                     [acsf, grp_se[g8 - 2]])
                mmv = snap("MM")
                ew = [mmv, mk(f"stt{c-1}")] if g8 < 2 else [mmv, grp_sv[g8 - 2]]
                for hh in range(4):
                    SOP(m_v[:, g8 % 2, hh * 128:(hh + 1) * 128],
                        bank[:, hh * 128:(hh + 1) * 128], AF.Exp,
                        bias=nacs_tmP[par][:, g8 * 4 + hh:g8 * 4 + hh + 1],
                        scale=1.0, w=ew if hh == 0 else ())
                VOP("scalar_tensor_tensor",
                    m_v[:, g8 % 2, :].rearrange("p (h t) -> p h t", h=4),
                    m_v[:, g8 % 2, :].rearrange("p (h t) -> p h t", h=4),
                    1.0,
                    pm_bfP[par].unsqueeze(1).broadcast_to([128, 4, 128]),
                    ALU.min, ALU.mult, w=[snap("SE"), mk(f"b0ve{c}")])
                grp_se.append(snap("SE"))
                grp_sv.append(snap("VE"))
                grp_sttv.append(snap("VE"))

            def lam_yd(g8):
                ydbank = bkB[2][:, (g8 % 2) * 256:(g8 % 2) * 256 + 256]
                ydw = [grp_sttv[g8]]
                if g8 >= 2:
                    ydw.append(grp_ye[g8 - 2])
                else:
                    ydw.append(mk(f"fve{c}"))
                    ydw.append(mk(f"umgp{c}"))
                for hh in range(4):
                    h = g8 * 4 + hh
                    MMOP(ydbank[:, hh * 64:(hh + 1) * 64],
                         m_v[:, g8 % 2, hh * 128:(hh + 1) * 128],
                         u_t[:, h * 64:(h + 1) * 64],
                         w=ydw if hh == 0 else ())
                VOP("tensor_tensor", y_p[:, g8 * 256:(g8 + 1) * 256],
                    y_p[:, g8 * 256:(g8 + 1) * 256], ydbank, ALU.add,
                    w=[snap("MM")])
                grp_ye.append(snap("VE"))

            for g8 in range(8):
                lam_rt(g8)
                if fillers:
                    fillers.pop(0)()
                lam_yd(g8)
            while fillers:
                fillers.pop(0)()
            mark(f"rtse{c}", "SE")
            mark(f"stt{c}", "VE")
            mark(f"ydve{c}", "VE")
            mark(f"rtmm{c}", "MM")

        def ssd_back3(c):
            par = c % 2
            GOP("tensor_tensor", u_t.rearrange("p (h q) -> p h q", h=32),
                u_t.rearrange("p (h q) -> p h q", h=32),
                bcast2(ds_tmP[par], 64), ALU.mult, w=[mk(f"rtmm{c}")])
            mark(f"uvgp{c}", "GP")
            GOP("tensor_tensor", pstate.rearrange("p (h q) -> p h q", h=32),
                pstate.rearrange("p (h q) -> p h q", h=32),
                bcast2(dec_repP[par], 64), ALU.mult,
                w=[mk(f"pbfve{c-1}"), mk(f"fve{c}")])
            mark(f"psgp{c}", "GP")
            stev = []
            for q in range(4):
                bank = bkB[1 if q % 2 == 0 else 3]
                if q == 0:
                    wmm = [mk(f"uvgp{c}"), mk(f"rtse{c}")]
                elif q >= 2:
                    wmm = [stev[q - 2]]
                else:
                    wmm = ()
                MMOP(bank, b_tmP[par], u_t[:, q * 512:(q + 1) * 512], w=wmm)
                VOP("tensor_tensor", pstate[:, q * 512:(q + 1) * 512],
                    pstate[:, q * 512:(q + 1) * 512], bank, ALU.add,
                    w=[snap("MM"), snap("GP")] if q == 0 else [snap("MM")])
                stev.append(snap("VE"))
                VOP("tensor_copy", pstate_bf[:, q * 512:(q + 1) * 512],
                    pstate[:, q * 512:(q + 1) * 512])
                mark(f"pbfq{c}_{q}", "VE")
            mark(f"smm{c}", "MM")
            mark(f"sve{c}", "VE")
            mark(f"pbfve{c}", "VE")
            DMA("BNC", yp_dram[c], y_p, w=[snap("VE")])
            mark(f"ypbnc{c}", "BNC")

        wov = woutt_d.rearrange("(ch cl) d -> cl ch d", cl=128)
        wstB = [ppA, ppB]

        def w1_thunks():
            out = []
            first = [True]

            def cast2(wvd, wv_, cc):
                for q in range(2):
                    w = ([mk("q1ve15"), mk("ppve15"), mk("ppgp15"),
                          mk("ppse15"), mk("ipse15")]
                         if first[0] else [snap("VE")])
                    first[0] = False
                    DMA("LDX", wstB[(cc + q) % 2], wvd[:, cc + q, :], w=w)
                    VOP("tensor_copy", wv_[:, cc + q, :],
                        wstB[(cc + q) % 2], w=[snap("LDX")])
            for wd, wv_ in [(wrt_d, wrv), (wkt_d, wkv)]:
                wvd = wd.rearrange("(ch cl) d -> cl ch d", cl=128)
                for cc in range(0, 8, 2):
                    out.append(lambda wvd=wvd, wv_=wv_, cc=cc:
                               cast2(wvd, wv_, cc))
            return out

        def w1b_thunks():
            out = []
            first = [True]

            def cast2(cc):
                wvd = wvt_d.rearrange("(ch cl) d -> cl ch d", cl=128)
                for q in range(2):
                    DMA("LDX", wstB[(cc + q) % 2], wvd[:, cc + q, :],
                        w=[snap("VE")])
                    w = ([snap("LDX"), mk("cvmm7")] if first[0]
                         else [snap("LDX")])
                    first[0] = False
                    VOP("tensor_copy", wvv[:, cc + q, :],
                        wstB[(cc + q) % 2], w=w)
            return [lambda cc=cc: cast2(cc) for cc in range(0, 8, 2)]

        def w2_thunks():
            out = []

            def red2(cc):
                for q in range(2):
                    DMA("LDX", wstB[(cc + q) % 2], wov[:, cc + q, :],
                        w=[snap("VE")])
                    VOP("tensor_reduce", wpp[:, cc + q:cc + q + 1],
                        wstB[(cc + q) % 2], AX.X, ALU.add,
                        apply_absolute_value=True, w=[snap("LDX")])

            def tail():
                VOP("tensor_reduce", wpp[:, 16:17], wpp[:, 0:16], AX.X,
                    ALU.add, w=[snap("VE")])
                MMOP(psA[0:1, 0:1], onescol, wpp[:, 16:17],
                     w=[snap("VE"), mk("ipse15"), mk("q1mm15")])
                VOP("tensor_scalar", wpp[0:1, 17:18], psA[0:1, 0:1],
                    1.0 / (DI * D), 1e-5, ALU.mult, ALU.max, w=[snap("MM")])
                recip_rows(wpp[0:1, 18:19], wpp[0:1, 17:18], w=[snap("VE")])
                MMOP(psA[:, 8:9], ones1, wpp[0:1, 18:19], w=[snap("SE")])
                VOP("tensor_copy", wpp[:, 2:3], psA[:, 8:9], w=[snap("MM")])
            for cc in range(0, 16, 2):
                out.append(lambda cc=cc: red2(cc))
            out.append(tail)
            return out

        def w3_thunks():
            first = [True]

            def quant2(cc):
                for q in range(2):
                    DMA("LDX", wstB[(cc + q) % 2], wov[:, cc + q, :],
                        w=[snap("VE")])
                    w = ([snap("LDX"), mk("cvmm7"), mk("ipse15")]
                         if first[0] else [snap("LDX")])
                    first[0] = False
                    VOP("tensor_scalar", wstB[(cc + q) % 2],
                        wstB[(cc + q) % 2], wpp[:, 2:3], MAGIC,
                        ALU.mult, ALU.add, w=w)
                    VOP("tensor_scalar", wstB[(cc + q) % 2],
                        wstB[(cc + q) % 2], MAGIC - 1.0, MAGIC,
                        ALU.max, ALU.subtract)
                    VOP("tensor_scalar", wo_v[:, cc + q, :],
                        wstB[(cc + q) % 2], 1.0, None, ALU.min)
            return [lambda cc=cc: quant2(cc) for cc in range(0, 16, 2)]

        prepipe(-1)
        in_proj(-1)
        prepipe(0)
        in_proj(0)
        prepipe(1)
        in_proj(1)
        conv_pet(0)
        conv_mm(0)
        prepipe(2)
        in_proj(2)
        prepipe(3)
        in_proj(3)
        prepipe(4)
        prepipe(5)
        w3t = None
        for g in range(8):
            if g == 7:
                w3t = w3_thunks()
            ssd_front1(2 * g)
            if g < 6:
                in_proj(2 * g + 4, 0)
            ssd_front2(2 * g)
            ssd_back1(2 * g)
            if g < 7:
                conv_pet(g + 1)
            if g < 6:
                in_proj(2 * g + 4, 1)
            ssd_front1(2 * g + 1)
            if g < 6:
                in_proj(2 * g + 5, 0)
            ssd_back2(2 * g,
                      ip_fillers(2 * g + 5) if g < 6 else
                      (w1_thunks() if g == 6 else
                       w1b_thunks() + w3t[0:4]))
            ssd_front2(2 * g + 1)
            ssd_back3(2 * g)
            ssd_back1(2 * g + 1)
            ssd_back2(2 * g + 1,
                      cv_fillers(g + 1) if g < 7 else w3t[4:8])
            if g == 6:
                for th in w2_thunks():
                    th()
            ssd_back3(2 * g + 1)
            if g < 5:
                prepipe(2 * g + 6)
                prepipe(2 * g + 7)

        DMA("BNC", gin_S, pstate, w=[mk("pbfve15"), snap("VE")])
        DMA("BNC", gin_as, run_row[0:1, 0:32], w=[snap("VE")])
        bncv = snap("BNC")
        E("gpsimd", lambda e: e.collective_compute(
            "AllGather", ALU.bypass, [list(range(8))],
            [gin_S.opt()], [gout_S.opt()]), w=[bncv], i="CC")
        E("gpsimd", lambda e: e.collective_compute(
            "AllGather", ALU.bypass, [list(range(8))],
            [gin_as.opt()], [gout_as.opt()]), i="CC")
        ccv = snap("CC")

        DMA("LDX", sb8_32, gout_as, w=[ccv])
        MMOP(psA[0:8, 32:64], between, sb8_32, w=[snap("LDX"), snap("VE")])
        SOP(sbw8, psA[0:8, 32:64], AF.Exp, w=[snap("MM")])
        VOP("tensor_scalar", sbw8, sbw8, maskj, None, ALU.mult,
            w=[snap("SE")])
        DMA("BNC", w8_dram, sbw8, w=[snap("VE")])
        DMA("BNC", wflat, w8_dram.rearrange("j h -> (j h)").unsqueeze(0),
            w=[snap("BNC")])
        MMOP(psA[:, 256:512], ones1, wflat, w=[snap("BNC")])
        VOP("tensor_copy", wrep, psA[:, 256:512], w=[snap("MM")])
        VOP("memset", stmp[:, :], 0.0)
        gsv = gout_S.rearrange("(r p) d -> r p d", p=128)
        smul = []
        for j in range(8):
            DMA("LDX", sjv[:, j % 2, :], gsv[j],
                w=[ccv, snap("VE")] if j < 2 else [ccv, smul[j - 2]])
            VOP("tensor_tensor", tcorr.rearrange("p (h q) -> p h q", h=32),
                sjv[:, j % 2, :].rearrange("p (h q) -> p h q", h=32),
                bcast2(wrep[:, j * 32:(j + 1) * 32], 64), ALU.mult,
                w=[snap("LDX")])
            smul.append(snap("VE"))
            VOP("tensor_tensor", stmp, stmp, tcorr, ALU.add)
        VOP("tensor_copy", sin_bf, stmp, w=[snap("VE")])
        mark("sinve", "VE")

        def passb_attn_pre_a(c):
            par = c % 2
            DMA("LDX", ypB, yp_dram[c],
                w=[mk("sinve"), mk("attnveL"), mk("attnseL")])
            DMA("LDX", szB, sz_dram[c])
            DMA("LDX", cfmB, c_dram[c])
            xt = x0t.rearrange("p (b d) -> p b d", b=2)[:, 0, :]
            DMA("LDX", xt, x0_d[c + 1], w=[mk("attnseL")])
            lx = snap("LDX")
            ycev = []
            for q in range(4):
                bank = bkA[q % 2]
                if q == 0:
                    wmm = [lx, mk("sinve"), mk("oppveL"), mk("q2mmL")]
                elif q >= 2:
                    wmm = [ycev[q - 2]]
                else:
                    wmm = ()
                MMOP(bank, cfmB, sin_bf[:, q * 512:(q + 1) * 512], w=wmm)
                VOP("tensor_tensor",
                    tcorr[:, q * 512:(q + 1) * 512].rearrange(
                        "p (h q2) -> p h q2", h=8),
                    bank.rearrange("p (h q2) -> p h q2", h=8),
                    bcast2(g_st[:, 32 * c + q * 8:32 * c + (q + 1) * 8], 64),
                    ALU.mult,
                    w=[snap("MM"), ("BNC", cnt["BNC"]), mk("blnL"),
                       mk("qgpL")] if q == 0
                    else [snap("MM")])
                ycev.append(snap("VE"))
            VOP("tensor_tensor", ypB, ypB, tcorr, ALU.add, w=[lx])
            VOP("tensor_tensor", ypB, ypB, szB, ALU.mult)
            VOP("scalar_tensor_tensor", tcorr, ypB, 1.0, ypB, ALU.mult,
                ALU.mult, accum_out=rowb[:, 0:1])
            sv = snap("VE")
            VOP("tensor_scalar", rowb[:, 1:2], rowb[:, 0:1], 1.0 / DI, None,
                ALU.mult, w=[sv])
            VOP("tensor_scalar", rowb[:, 1:2], rowb[:, 1:2], 1e-6, None,
                ALU.add)
            SOP(rowb[:, 1:2], rowb[:, 1:2], AF.Ln, w=[snap("VE")])
            SOP(rowb[:, 2:3], rowb[:, 1:2], AF.Exp, scale=-0.5)
            VOP("scalar_tensor_tensor", tcorr, ypB, rowb[:, 2:3], onwrep,
                ALU.mult, ALU.mult, w=[snap("SE")])

        def passb_attn_pre_b(c):
            par = c % 2
            VOP("tensor_reduce", rowb[:, 3:4], tcorr, AX.X, ALU.add)
            VOP("scalar_tensor_tensor", ypB, tcorr, 1.0, tcorr, ALU.mult,
                ALU.mult, accum_out=rowb[:, 4:5])
            sv = snap("VE")
            VOP("tensor_scalar", rowb[:, 5:6], rowb[:, 3:4], 1.0 / DI, None,
                ALU.mult, w=[sv])
            VOP("tensor_tensor", rowb[:, 6:7], rowb[:, 5:6], rowb[:, 5:6],
                ALU.mult)
            VOP("tensor_scalar", rowb[:, 7:8], rowb[:, 4:5], 1.0 / DI, None,
                ALU.mult)
            VOP("tensor_tensor", rowb[:, 7:8], rowb[:, 7:8], rowb[:, 6:7],
                ALU.subtract)
            VOP("tensor_scalar", rowb[:, 7:8], rowb[:, 7:8], 1e-5, None,
                ALU.add)
            SOP(rowb[:, 7:8], rowb[:, 7:8], AF.Ln, w=[snap("VE")])
            SOP(rowb[:, 8:9], rowb[:, 7:8], AF.Exp, scale=-0.5)
            VOP("tensor_tensor", rowb[:, 9:10], rowb[:, 5:6], rowb[:, 8:9],
                ALU.mult, w=[snap("SE")])
            VOP("tensor_scalar_mul", rowb[:, 9:10], rowb[:, 9:10], -1.0)
            SOP(ypB, tcorr, AF.Identity, bias=rowb[:, 9:10],
                scale=rowb[:, 8:9], w=[snap("VE")])
            VOP("tensor_reduce", rowb[:, 10:11], ypB, AX.X, ALU.max,
                apply_absolute_value=True, w=[snap("SE")])
            sv = snap("VE")
            VOP("tensor_scalar", rowb[:, 10:11], rowb[:, 10:11], 1e-5, None,
                ALU.max, w=[sv])
            VOP("tensor_scalar", rowb[:, 13:14], rowb[:, 10:11], 1.0 / 127.0,
                None, ALU.mult)
            VOP("tensor_copy", rowb[:, 11:12], rowb[:, 10:11])
            SOP(rowb[:, 11:12], rowb[:, 11:12], AF.Ln, w=[snap("VE")])
            SOP(rowb[:, 12:13], rowb[:, 11:12], AF.Exp, scale=-1.0)
            VOP("tensor_scalar", rowb[:, 12:13], rowb[:, 12:13], 127.0, None,
                ALU.mult, w=[snap("SE")])
            VOP("tensor_scalar", tcorr, ypB, rowb[:, 12:13], MAGIC, ALU.mult,
                ALU.add, w=[mk("q2mmL")])
            VOP("tensor_scalar", tcorr, tcorr, MAGIC - 128.0, MAGIC, ALU.max,
                ALU.subtract)
            VOP("tensor_scalar", q2, tcorr, 127.0, None, ALU.min)
            mark("qgpL", "VE")
            qv = snap("VE")
            PET(q2Tv, [q2[:, j * 128:(j + 1) * 128] for j in range(16)],
                [bkA[0], bkA[1]], w=[qv, mk("oppveL")])
            mark("q2mmL", "MM")
            mark("apveL", "VE")
            mark(f"apve{c}", "VE")
            mark("q2tpqL", "VE")

        def passb_attn_pre(c):
            passb_attn_pre_a(c)
            passb_attn_pre_b(c)

        def passb_attn_mm(c):
            par = c % 2
            x1b = x1bv[par] if c != 15 else x1b3
            xt = x0t.rearrange("p (b d) -> p b d", b=2)[:, 0, :]
            lx = ("LDX", cnt["LDX"])
            tpq = mk("q2tpqL")
            for dg in range(2):
                bank = bkA[dg]
                for cc2 in range(16):
                    MMOP(bank, q2Tv[:, cc2, :],
                         wo_v[:, cc2, dg * 512:dg * 512 + 512],
                         start=(cc2 == 0), stop=(cc2 == 15),
                         w=[tpq] if (dg == 0 and cc2 == 0) else ())
            mmv = snap("MM")
            VOP("tensor_scalar", rowb[:, 14:15], r1_st[:, c + 1:c + 2], 1.0,
                None, ALU.add, w=[mmv])
            VOP("tensor_tensor", rowb[:, 15:16], mu1_st[:, c + 1:c + 2],
                r1_st[:, c + 1:c + 2], ALU.mult)
            VOP("tensor_scalar_mul", rowb[:, 15:16], rowb[:, 15:16], -1.0)
            SOP(t1b, xt, AF.Identity, bias=rowb[:, 15:16],
                scale=rowb[:, 14:15],
                w=[snap("VE"), lx, mk("blnL")])
            sev = snap("SE")
            for dg in range(2):
                VOP("scalar_tensor_tensor", x1b[:, dg * 512:dg * 512 + 512],
                    psA[:, dg * 512:dg * 512 + 512], rowb[:, 13:14],
                    t1b[:, dg * 512:dg * 512 + 512], ALU.mult, ALU.add,
                    w=[sev, mmv, mk(f"ffnve{c-2}")]
                    if dg == 0 else ())
            mark("attnveL", "VE")
            mark("attnseL", "SE")
            mark("oppveL", "VE")
            mark(f"attnve{c}", "VE")

        def passb_ln2(c, src):
            rb = rowb[:, 16:32]
            VOP("tensor_reduce", rb[:, 0:1], src, AX.X, ALU.add,
                w=[mk(f"attnve{c}")])
            VOP("scalar_tensor_tensor", lnsink, src, 1.0, src, ALU.mult,
                ALU.mult, accum_out=rb[:, 1:2], w=[mk(f"blnrow{c-1}")])
            sv = snap("VE")
            VOP("tensor_scalar", rb[:, 2:3], rb[:, 0:1], 1.0 / D, None,
                ALU.mult, w=[sv])
            VOP("tensor_tensor", rb[:, 3:4], rb[:, 2:3], rb[:, 2:3], ALU.mult)
            VOP("tensor_scalar", rb[:, 4:5], rb[:, 1:2], 1.0 / D, None,
                ALU.mult)
            VOP("tensor_tensor", rb[:, 4:5], rb[:, 4:5], rb[:, 3:4],
                ALU.subtract)
            VOP("tensor_scalar", rb[:, 4:5], rb[:, 4:5], 1e-5, None, ALU.add)
            SOP(rb[:, 4:5], rb[:, 4:5], AF.Ln, w=[snap("VE"), mk(f"bse{c-1}")])
            SOP(rb[:, 5:6], rb[:, 4:5], AF.Exp, scale=-0.5)
            VOP("tensor_tensor", rb[:, 6:7], rb[:, 2:3], rb[:, 5:6], ALU.mult,
                w=[snap("SE")])
            VOP("tensor_scalar_mul", rb[:, 6:7], rb[:, 6:7], -1.0)
            mark(f"blnrow{c}", "VE")
            SOP(ln2tm, src, AF.Identity, bias=rb[:, 6:7], scale=rb[:, 5:6],
                w=[snap("VE"), mk(f"ffnve{c-1}")])
            sev = snap("SE")
            if c == 0:
                DMA("LDX", lfv[:, :, 15:16],
                    ln2b_dram.rearrange("a (j p) -> p j a", p=128),
                    w=[mk("b2bnc"), mk(f"ffngp{c-1}")],
                    allow_slow_non_contiguous=True)
                hv = [snap("LDX")]
            else:
                VOP("tensor_copy", lfv[:, :, 15:16], lfv[:, :, 143:144],
                    w=[mk(f"ffngp{c-1}"), mk(f"ffnmm{c-1}")])
                hv = [snap("VE")]
            PET(lfv[:, :, 16:144],
                [ln2tm[:, j * 128:(j + 1) * 128] for j in range(8)],
                [bkB[0], bkB[1]],
                w=[sev, mk(f"ffnmm{c-1}"), mk(f"rve{c-1}")] + hv,
                evac_w=[mk(f"ffngp{c-1}")])
            if c == 14:
                VOP("tensor_copy", ln2h15[:, 0:8],
                    lfv[:, :, 143:144].squeeze(2), w=[snap("VE")])
            mark(f"bln{c}", "VE")
            mark("blnL", "VE")

        def passb_ffn_mix(c, x1src):
            VOP("tensor_tensor", dmv, lfv[:, :, 16:144], lfv[:, :, 15:143],
                ALU.subtract, w=[mk(f"bln{c}"), mk(f"ffngp{c-1}")])
            VOP("tensor_tensor", xrv, dmv,
                murep.unsqueeze(2).broadcast_to([128, 8, 128]), ALU.mult,
                w=[mk(f"ffnmm{c-1}")])
            VOP("tensor_tensor", xrv, xrv, lfv[:, :, 15:143], ALU.add)
            GOP("tensor_tensor", xkv, dmv,
                mukrep.unsqueeze(2).broadcast_to([128, 8, 128]), ALU.mult,
                w=[snap("VE"), mk(f"ffnmm{c-1}")])
            GOP("tensor_tensor", xkv, xkv, lfv[:, :, 15:143], ALU.add)
            mark(f"ffngp{c}", "GP")
            mark(f"ffnmixv{c}", "VE")

        def passb_ffn_rest(c, x1src):
            svx = mk(f"ffnmixv{c}")
            ggx = mk(f"ffngp{c}")
            for dg in range(2):
                bank = bkB[dg]
                for cc in range(8):
                    MMOP(bank, xrv[:, cc, :],
                         wrv[:, cc, dg * 512:dg * 512 + 512],
                         start=(cc == 0), stop=(cc == 7),
                         w=[svx, mk(f"blnve{c}") if False else mk(f"bln{c}")]
                         if (dg == 0 and cc == 0) else ())
            mmv = snap("MM")
            SOP(r_bf[:, 0:512], bkB[0], AF.Sigmoid,
                w=[mmv, mk(f"vmm{c-1}")])
            SOP(r_bf[:, 512:1024], bkB[1], AF.Sigmoid)
            mark(f"rse{c}", "SE")
            for dg in range(2):
                bank = bkB[2 + dg]
                for cc in range(8):
                    MMOP(bank, xkv[:, cc, :],
                         wkv[:, cc, dg * 512:dg * 512 + 512],
                         start=(cc == 0), stop=(cc == 7),
                         w=[ggx, mk(f"kse{c-1}"), mk(f"kve{c-1}")]
                         if (dg == 0 and cc == 0) else ())
            mmv = snap("MM")
            SOP(krelu[:, 0:512], bkB[2], AF.Relu, w=[mmv])
            SOP(krelu[:, 512:1024], bkB[3], AF.Relu)
            SOP(kact, krelu, AF.Square)
            mark(f"kse{c}", "SE")
            sev = snap("SE")
            PET(kTv, [kact[:, j * 128:(j + 1) * 128] for j in range(8)],
                [bkB[2], bkB[3]], w=[sev, mk(f"vmm{c-1}")])
            mark(f"kpet{c}", "MM")
            mark(f"kve{c}", "VE")
            tpk = snap("VE")
            for dg in range(2):
                bank = bkB[dg]
                for cc in range(8):
                    MMOP(bank, kTv[:, cc, :],
                         wvv[:, cc, dg * 512:dg * 512 + 512],
                         start=(cc == 0), stop=(cc == 7),
                         w=[tpk, mk(f"rse{c}")]
                         if (dg == 0 and cc == 0) else ())
            mark(f"vmm{c}", "MM")
            mmv = snap("MM")
            for dg in range(2):
                VOP("scalar_tensor_tensor", outb[:, dg * 512:dg * 512 + 512],
                    bkB[dg], 1.0, r_bf[:, dg * 512:dg * 512 + 512],
                    ALU.mult, ALU.mult,
                    w=[mmv, mk(f"rse{c}"), mk(f"osto{c-1}")]
                    if dg == 0 else ())
            VOP("tensor_tensor", outb, outb, x1src, ALU.add)
            mark(f"ffnve{c}", "VE")
            mark(f"rve{c}", "VE")
            DMA("STO", out_d[c * 128:(c + 1) * 128, :], outb, w=[snap("VE")])
            mark(f"osto{c}", "STO")
            mark(f"ffnmm{c}", "MM")
            mark(f"bse{c}", "SE")

        def passb_ffn(c, x1src):
            passb_ffn_mix(c, x1src)
            passb_ffn_rest(c, x1src)

        passb_attn_pre(15)
        passb_attn_mm(15)
        VOP("tensor_copy", x1c15, x1b3, w=[mk("attnveL")])
        DMA("BNC", gin_x1, x1b3[127:128, :], w=[snap("VE")])
        E("gpsimd", lambda e: e.collective_compute(
            "AllGather", ALU.bypass, [list(range(8))],
            [gin_x1.opt()], [gout_x1.opt()]),
          w=[snap("BNC"), snap("VE")], i="CC")
        DMA("LDX", sbgx, gout_x1, w=[snap("CC"), mk("q2mmL"),
                                     mk("oppveL")])
        lgx = snap("LDX")
        MMOP(psA[0:1, 1024:1536], selprev, sbgx[:, 0:512],
             w=[lgx, snap("VE"), snap("SE")])
        MMOP(psA[0:1, 1536:2048], selprev, sbgx[:, 512:1024])
        mark("q2mmL", "MM")
        VOP("tensor_copy", x1brow[0:1, 0:512], psA[0:1, 1024:1536],
            w=[snap("MM"), snap("STO")])
        VOP("tensor_copy", x1brow[0:1, 512:1024], psA[0:1, 1536:2048])
        VOP("tensor_reduce", rowb[0:1, 16:17], x1brow, AX.X, ALU.add,
            w=[snap("VE")])
        VOP("scalar_tensor_tensor", t1b[0:1, :], x1brow, 1.0, x1brow,
            ALU.mult, ALU.mult, accum_out=rowb[0:1, 17:18])
        sv = snap("VE")
        VOP("tensor_scalar", rowb[0:1, 18:19], rowb[0:1, 16:17], 1.0 / D,
            None, ALU.mult, w=[sv])
        VOP("tensor_tensor", rowb[0:1, 19:20], rowb[0:1, 18:19],
            rowb[0:1, 18:19], ALU.mult)
        VOP("tensor_scalar", rowb[0:1, 20:21], rowb[0:1, 17:18], 1.0 / D,
            None, ALU.mult)
        VOP("tensor_tensor", rowb[0:1, 20:21], rowb[0:1, 20:21],
            rowb[0:1, 19:20], ALU.subtract)
        VOP("tensor_scalar", rowb[0:1, 20:21], rowb[0:1, 20:21], 1e-5, None,
            ALU.add)
        SOP(rowb[0:1, 20:21], rowb[0:1, 20:21], AF.Ln, w=[snap("VE")])
        SOP(rowb[0:1, 21:22], rowb[0:1, 20:21], AF.Exp, scale=-0.5)
        VOP("tensor_tensor", rowb[0:1, 22:23], rowb[0:1, 18:19],
            rowb[0:1, 21:22], ALU.mult, w=[snap("SE")])
        VOP("tensor_scalar_mul", rowb[0:1, 22:23], rowb[0:1, 22:23], -1.0)
        VOP("tensor_scalar", ln2brow, x1brow, rowb[0:1, 21:22],
            rowb[0:1, 22:23], ALU.mult, ALU.add)
        VOP("tensor_copy", ln2bb, ln2brow)
        DMA("BNC", ln2b_dram, ln2bb, w=[snap("VE")])
        mark("b2bnc", "BNC")

        passb_attn_pre(0)
        for c in range(15):
            passb_attn_mm(c)
            passb_ln2(c, x1bv[c % 2])
            passb_ffn_mix(c, x1bv[c % 2])
            if c < 14:
                passb_attn_pre_a(c + 1)
            passb_ffn_rest(c, x1bv[c % 2])
            if c < 14:
                passb_attn_pre_b(c + 1)
        passb_ln2(15, x1c15)
        passb_ffn(15, x1c15)

        E("sync", lambda e: e.nop(), w=[snap("STO"), snap("BNC"),
                                        snap("VE"), snap("MM"), snap("SE"),
                                        snap("GP")])

        block = stack.enter_context(nc.Block())
        SELFW = {"vector": "VE", "scalar": "SE", "gpsimd": "GP"}

        def replay(name):
            own = SELFW.get(name)

            def go(eng):
                done = 0
                for wl, fn, pend in SCR[name]:
                    if fn is None:
                        for (n, v) in wl:
                            eng.wait_ge(semh[n], v)
                        continue
                    if own is not None and done > 0:
                        eng.wait_ge(semh[own], done)
                    for (n, v) in wl:
                        if v > 0:
                            eng.wait_ge(semh[n], v)
                    inst = fn(eng)
                    if pend is not None:
                        inst.then_inc(semh[pend[0]], pend[1])
                        if own is not None and pend[0] == own:
                            done += pend[1]
            return go

        block.sync(replay("sync"))
        block.tensor(replay("tensor"))
        block.vector(replay("vector"))
        block.scalar(replay("scalar"))
        block.gpsimd(replay("gpsimd"))

    return nc


_NC_CACHE = [None]


def _host_inputs(inputs):
    f = np.float32
    x = np.ascontiguousarray(
        inputs["L0_attn_hidden_states"].reshape(16384, D)).astype(f)
    conv_w = np.asarray(inputs["L0_attn_conv_w"], f)
    conv_b = np.asarray(inputs["L0_attn_conv_b"], f)
    cdiag = np.zeros((4, 18, 128, 128), f)
    idx = np.arange(128)
    for k in range(4):
        for j in range(18):
            cdiag[k, j, idx, idx] = conv_w[j * 128:(j + 1) * 128, k]
    sel8 = np.zeros((8, 1024), f)
    for g in range(8):
        sel8[g, g * 128:(g + 1) * 128] = 1.0
    base = {
        "wint": np.ascontiguousarray(np.asarray(inputs["L0_attn_in_proj_w"], f).T),
        "woutt": np.ascontiguousarray(np.asarray(inputs["L0_attn_out_proj_w"], f).T),
        "wrt": np.ascontiguousarray(np.asarray(inputs["L0_ffn_W_r"], f).T),
        "wkt": np.ascontiguousarray(np.asarray(inputs["L0_ffn_W_k"], f).T),
        "wvt": np.ascontiguousarray(np.asarray(inputs["L0_ffn_W_v"], f).T),
        "cdiag": cdiag,
        "cbias_fm": np.ascontiguousarray(conv_b.reshape(18, 128).T),
        "nwrep": np.ascontiguousarray(
            np.broadcast_to(np.asarray(inputs["L0_attn_norm_w"], f), (128, D))),
        "onwrep": np.ascontiguousarray(
            np.broadcast_to(np.asarray(inputs["L0_attn_out_norm_w"], f),
                            (128, DI))),
        "murep": np.ascontiguousarray(
            np.asarray(inputs["L0_ffn_mu_r"], f).reshape(8, 128).T),
        "mukrep": np.ascontiguousarray(
            np.asarray(inputs["L0_ffn_mu_k"], f).reshape(8, 128).T),
        "dtb": np.tile(np.asarray(inputs["L0_attn_dt_bias"],
                                  f).reshape(32, 1), (2, 1)),
        "alog": np.tile(np.asarray(inputs["L0_attn_A_log"],
                                   f).reshape(32, 1), (2, 1)),
        "dprep": np.ascontiguousarray(
            np.broadcast_to(np.asarray(inputs["L0_attn_Dp"], f), (128, 32))),
        "ones1": np.ones((1, 128), f),
        "onescol": np.ones((128, 1), f),
        "i128": np.eye(128, dtype=f),
        "u01": np.triu(np.ones((128, 128), f)),
        "sel8": sel8,
    }
    in_maps = []
    for c in range(NCORES):
        m = dict(base)
        xt = np.zeros((NT + 1, 128, D), f)
        s = c * T
        if c == 0:
            xt.reshape(-1, D)[128:] = x[0:T]
        else:
            xt.reshape(-1, D)[:] = x[s - 128:s + T]
        m["x0"] = xt
        sel = np.zeros((8, 1), f)
        if c > 0:
            sel[c - 1, 0] = 1.0
        m["selprev"] = sel
        bt = np.zeros((8, 8), f)
        for k in range(8):
            for j in range(8):
                if j < k < c:
                    bt[k, j] = 1.0
        m["between"] = bt
        mj = np.zeros((8, 1), f)
        mj[:c, 0] = 1.0
        m["maskj"] = mj
        in_maps.append(m)
    return in_maps


def kernel(**inputs):
    if _NC_CACHE[0] is None:
        _NC_CACHE[0] = build_program()
    nc = _NC_CACHE[0]
    in_maps = _host_inputs(inputs)
    res = run_bass_kernel_spmd(nc, in_maps, core_ids=list(range(NCORES)))
    out = np.concatenate([np.asarray(r["out"]) for r in res.results], axis=0)
    return out.reshape(1, 16384, D).astype(np.float32)


# version 30
# speedup vs baseline: 1.0040x; 1.0040x over previous
import os
from contextlib import ExitStack

import numpy as np
import ml_dtypes

import concourse.bass as bass
import concourse.mybir as mybir
from concourse.bass_utils import run_bass_kernel_spmd

F32 = mybir.dt.float32
BF16 = mybir.dt.bfloat16
AF = mybir.ActivationFunctionType
ALU = mybir.AluOpType
AX = mybir.AxisListType
BFNP = ml_dtypes.bfloat16

NCORES = 8
T = 2048
NT = 16
D = 1024
DI = 2048
DIP = 4384
NH = 32
MAGIC = float(np.float32(1.5 * 2 ** 23))

RACE = os.environ.get("KMK_RACE", "0") == "1"
ENGS = ("sync", "tensor", "vector", "scalar", "gpsimd")


def bcast2(ap2d, reps):
    return ap2d.unsqueeze(2).broadcast_to([ap2d.shape[0], ap2d.shape[1], reps])


def build_program():
    nc = bass.Bass("TRN2", target_bir_lowering=False,
                   detect_race_conditions=RACE)

    def din(name, shape, dt=F32):
        return nc.dram_tensor(name, list(shape), dt, kind="ExternalInput").ap()

    x0_d = din("x0", (NT + 1, 128, D))
    wint_d = din("wint", (D, DIP))
    woutt_d = din("woutt", (DI, D))
    wrt_d = din("wrt", (D, D))
    wkt_d = din("wkt", (D, D))
    wvt_d = din("wvt", (D, D))
    cdiag_d = din("cdiag", (4, 18, 128, 128))
    cbias_fm_d = din("cbias_fm", (128, 18))
    nwrep_d = din("nwrep", (128, D))
    onwrep_d = din("onwrep", (128, DI))
    murep_d = din("murep", (128, 8))
    mukrep_d = din("mukrep", (128, 8))
    dtb_d = din("dtb", (64, 1))
    alog_d = din("alog", (64, 1))
    dprep_d = din("dprep", (128, 32))
    ones1_d = din("ones1", (1, 128))
    onescol_d = din("onescol", (128, 1))
    i128_d = din("i128", (128, 128))
    u01_d = din("u01", (128, 128))
    sel8_d = din("sel8", (8, 1024))
    selprev_d = din("selprev", (8, 1))
    between_d = din("between", (8, 8))
    maskj_d = din("maskj", (8, 1))

    out_d = nc.dram_tensor("out", [T, D], F32, kind="ExternalOutput").ap()

    yp_dram = nc.dram_tensor("yp_dram", [NT, 128, DI], F32).ap()
    sz_dram = nc.dram_tensor("sz_dram", [NT, 128, DI], BF16).ap()
    c_dram = nc.dram_tensor("c_dram", [NT, 128, 128], BF16).ap()
    acs_dram = nc.dram_tensor("acs_dram", [2, 32, 128], F32).ap()
    w8_dram = nc.dram_tensor("w8_dram", [8, 32], F32).ap()
    ln2b_dram = nc.dram_tensor("ln2b_dram", [1, D], BF16).ap()
    gin_S = nc.dram_tensor("gin_S", [128, DI], F32).ap()
    gout_S = nc.dram_tensor("gout_S", [8 * 128, DI], F32).ap()
    gin_as = nc.dram_tensor("gin_as", [1, 32], F32).ap()
    gout_as = nc.dram_tensor("gout_as", [8, 32], F32).ap()
    gin_x1 = nc.dram_tensor("gin_x1", [1, D], F32).ap()
    gout_x1 = nc.dram_tensor("gout_x1", [8, D], F32).ap()

    stack = ExitStack()
    with stack:
        AR = 53000
        arena_t = stack.enter_context(nc.sbuf_tensor("arena", [128, AR], F32))
        arena = arena_t.ap()
        off = [0]

        def af(cols, parts=128, align=16):
            off[0] = (off[0] + align - 1) // align * align
            o = off[0]
            off[0] += cols
            assert off[0] <= AR, f"arena overflow {off[0]}"
            return arena[0:parts, o:o + cols]

        def abf(cols_bf, parts=128, align=16):
            assert cols_bf % 2 == 0
            off[0] = (off[0] + align - 1) // align * align
            o = off[0]
            off[0] += cols_bf // 2
            assert off[0] <= AR, f"arena overflow {off[0]}"
            return arena[0:128, o:o + cols_bf // 2].bitcast(BF16)[0:parts, :]

        cb_fm = af(18)
        nwrep = af(D)
        onwrep = af(DI)
        murep = af(8)
        mukrep = af(8)
        dprep = af(32)
        dtb = af(1, 64)
        alogc = af(1, 64)
        nega = af(1, 64)
        ones1 = af(128, 1)
        onescol = af(1)
        u01b = abf(128)
        sel8b = af(1024, 8)
        mu1_st = af(NT + 1)
        r1_st = af(NT + 1)
        g_st = af(NT * 32)
        run_row = af(32, 1)
        rowb = af(32)
        rowdec = af(32, 1)
        wpp = af(32)
        u01f = af(128)
        i128 = af(128)
        ln2h15 = abf(16, 128)
        x0t = af(2 * D)
        selprev = af(1, 8)
        between = af(8, 8)
        maskj = af(1, 8)

        wbase = off[0]
        w_in_bf = abf(8 * DIP)
        wq_v = w_in_bf.rearrange("p (c d) -> p c d", c=8)
        cdiag_bf = abf(4 * 18 * 128)
        cdv = cdiag_bf.rearrange("p (k j t) -> p k j t", k=4, j=18)
        passA_wend = off[0]
        off[0] = wbase
        woutt_bf = abf(16 * D)
        wo_v = woutt_bf.rearrange("p (c d) -> p c d", c=16)
        wr_bf = abf(8 * D)
        wrv = wr_bf.rearrange("p (c d) -> p c d", c=8)
        wk_bf = abf(8 * D)
        wkv = wk_bf.rearrange("p (c d) -> p c d", c=8)
        wv_bf = abf(8 * D)
        wvv = wv_bf.rearrange("p (c d) -> p c d", c=8)
        off[0] = max(off[0], passA_wend)

        tbase = off[0]
        ppA = af(D)
        ppB = af(D)
        qblock = af(1536)
        q1 = qblock[:, 0:512].bitcast(BF16)
        q1T2 = [qblock[:, 512:1024].bitcast(BF16),
                qblock[:, 1024:1536].bitcast(BF16)]
        q1Tv2 = [t.rearrange("p (j t) -> p j t", j=8) for t in q1T2]
        szt = abf(DI)
        xdp = szt
        dt_tm2 = af(128)
        xbcg = abf(2 * 2304)
        xbv = xbcg.rearrange("p (b c) -> p b c", b=2)
        convin = abf(18 * 272)
        cvv = convin.rearrange("p (j t) -> p j t", j=18)
        xsil = abf(16 * 256)
        xsv = xsil.rearrange("p (j t) -> p j t", j=16)
        bcfm = abf(2 * 256)
        bcv = bcfm.rearrange("p (j t) -> p j t", j=2)
        x_tmP = [abf(DI), abf(DI)]
        xtvP = [t.rearrange("p (j t) -> p j t", j=16) for t in x_tmP]
        u_t = abf(DI)
        m_t = abf(2 * 512)
        m_v = m_t.rearrange("p (b d) -> p b d", b=2)
        pm_bfP = [abf(128), abf(128)]
        b_tmP = [abf(128), abf(128)]
        y_p = af(DI)
        acs8P = [af(512, 8), af(512, 8)]
        asum_row = af(32, 1)
        dtx_fm = af(128, 64)
        sp1_fm = af(128, 64)
        sp2_fm = af(128, 64)
        dta_fm = af(128, 64)
        a_fm = sp1_fm
        acs_fm = dtx_fm
        acs_tm = af(32)
        nacs_tmP = [af(32), af(32)]
        dta_tmP = [af(32), af(32)]
        gtmp = af(32)
        eacsP = [af(32), af(32)]
        ds_tmP = [af(32), af(32)]
        dec_repP = [af(32), af(32)]
        ones_a = af(128, 64)
        pstate = af(DI)
        pstate_bf = abf(DI)
        passA_tend = off[0]
        off[0] = tbase
        ypBP = [af(DI), af(DI)]
        stmp = ypBP[0]
        tcorr = af(DI)
        szB = abf(DI)
        cfmB = abf(128)
        qb2 = af(4096)
        q2 = qb2[:, 0:1024].bitcast(BF16)
        q2T = qb2[:, 1024:2048].bitcast(BF16)
        q2Tv = q2T.rearrange("p (j t) -> p j t", j=16)
        x1b2 = qb2[:, 2048:4096]
        x1bv = [x1b2[:, 0:1024], x1b2[:, 1024:2048]]
        x1b3 = af(D)
        t1b = af(D)
        sjbuf = qb2[:, 0:4096]
        sjv = sjbuf.rearrange("p (b d) -> p b d", b=2)
        sbgx = qb2[0:8, 0:1024]
        x1brow = qb2[0:1, 1024:2048]
        wflat = qb2[0:1, 3584:3840]
        ln2tm = abf(D)
        ln2fm = abf(8 * 144)
        lfv = ln2fm.rearrange("p (j t) -> p j t", j=8)
        dmix = ln2tm
        dmv = dmix.rearrange("p (j t) -> p j t", j=8)
        xr_fm = abf(8 * 128)
        xrv = xr_fm.rearrange("p (j t) -> p j t", j=8)
        xk_fm = abf(8 * 128)
        xkv = xk_fm.rearrange("p (j t) -> p j t", j=8)
        r_bf = abf(D)
        kact = abf(D)
        kab = af(512)
        sb8_32 = kab[0:8, 0:32]
        sbw8 = kab[0:8, 32:64]
        wrep = kab[:, 256:512]
        kactT = abf(8 * 128)
        kTv = kactT.rearrange("p (j t) -> p j t", j=8)
        outb = af(D)
        sin_bf = abf(DI)
        x1c15 = af(D)
        ln2brow = ppB[0:1, 0:D]
        ln2bb = ppB[0:1, 0:512].bitcast(BF16) if False else None
        krelu = af(D)
        lnsink = af(D)
        off[0] = max(off[0], passA_tend)
        stg0 = arena[:, tbase:tbase + DIP]
        stg1 = arena[:, tbase + DIP:tbase + 2 * DIP]
        stgs = [stg0, stg1]
        partial = arena[:, tbase + 2 * DIP:tbase + 2 * DIP + 32]
        cdiag_f = arena[:, tbase + 2 * DIP + 32:
                        tbase + 2 * DIP + 32 + 2 * 18 * 128]
        cfv = cdiag_f.rearrange("p (k j t) -> p k j t", k=2, j=18)
        ln2bb = tcorr[0:1, 0:512].bitcast(BF16)[0:1, 0:D]

        psA_t = stack.enter_context(nc.psum_tensor("psA", [128, 2048], F32))
        psB_t = stack.enter_context(nc.psum_tensor("psB", [128, 2048], F32))
        psA = psA_t.ap()
        psB = psB_t.ap()
        bkA = [psA[:, i * 512:(i + 1) * 512] for i in range(4)]
        bkB = [psB[:, i * 512:(i + 1) * 512] for i in range(4)]

        semh = {}
        cnt = {}
        for n in ["LDW", "LDX", "BNC", "ACS", "STO", "MM", "VE", "SE", "GP",
                  "CC"]:
            semh[n] = stack.enter_context(nc.semaphore(n))
            cnt[n] = 0
        SCR = {e: [] for e in ENGS}
        DMASEMS = {"LDW", "LDX", "BNC", "ACS", "STO"}
        dma_maxreq = {n: 0 for n in DMASEMS}
        dma_drained = {n: 0 for n in DMASEMS}
        M = {}

        def E(eng, fn, w=(), i=None, ik=1):
            wl = []
            for x in w:
                t = x if isinstance(x, tuple) else (x, cnt[x])
                if t[0] in DMASEMS:
                    t = (t[0], cnt[t[0]])
                    if t[1] > dma_maxreq[t[0]]:
                        dma_maxreq[t[0]] = t[1]
                wl.append(t)
            if i in DMASEMS and dma_maxreq[i] > dma_drained[i]:
                tgt = dma_maxreq[i]
                SCR[eng].append(([(i, tgt)], None, None))
                dma_drained[i] = tgt
            if i is not None:
                cnt[i] += ik
            SCR[eng].append((wl, fn, (i, ik) if i is not None else None))

        def snap(n):
            return (n, cnt[n])

        def mark(key, sem):
            M[key] = (sem, cnt[sem])

        def mk(key):
            return M.get(key, ("VE", 0))

        def DMA(sem, outap, inap, w=(), **kw):
            E("sync", lambda e, o=outap, ii=inap, k=kw:
              e.dma_start(out=o, in_=ii, **k), w=w, i=sem, ik=16)

        def VOP(meth, *args, w=(), **kw):
            E("vector", lambda e, a=args, k=kw: getattr(e, meth)(*a, **k),
              w=w, i="VE")

        def GOP(meth, *args, w=(), **kw):
            E("gpsimd", lambda e, a=args, k=kw: getattr(e, meth)(*a, **k),
              w=w, i="GP")

        def SOP(*args, w=(), **kw):
            E("scalar", lambda e, a=args, k=kw: e.activation(*a, **k),
              w=w, i="SE")

        def MMOP(out, lhsT, rhs, start=True, stop=True, w=()):
            E("tensor", lambda e, o=out, l=lhsT, r=rhs, s1=start, s2=stop:
              e.matmul(o, l, r, start=s1, stop=s2), w=w, i="MM")

        def TROP(out, inp, ident, w=()):
            E("tensor", lambda e, o=out, ii=inp, dd=ident: e.transpose(o, ii, dd),
              w=w, i="MM")

        i128b = u01f.bitcast(BF16)[:, 0:128]

        def PET(dst3, srcs, banks, w=(), evac_w=()):
            n = len(srcs)
            i = 0
            r = 0
            firstw = list(w)
            evs = []
            while i < n:
                m = min(8, n - i)
                bank = banks[r % len(banks)].bitcast(BF16)
                b3 = bank.rearrange("p (k t) -> p k t", k=8)
                roundw = list(firstw) + [snap("VE")]
                firstw = []
                for k in range(m):
                    E("tensor",
                      lambda e, ss=srcs[i + k], bb=bank, kk=k:
                      e.transpose(bb[:, kk * 128:(kk + 1) * 128], ss, i128b),
                      w=roundw if k == 0 else (), i="MM")
                mmv = snap("MM")
                VOP("tensor_copy", dst3[:, i:i + m, :], b3[:, 0:m, :],
                    w=[mmv] + list(evac_w))
                evs.append(snap("VE"))
                i += m
                r += 1

        def rsqrt_rows(dst, src, eps, w=()):
            VOP("tensor_scalar", src, src, eps, None, ALU.add, w=w)
            SOP(src, src, AF.Ln, w=[snap("VE")])
            SOP(dst, src, AF.Exp, scale=-0.5)

        def recip_rows(dst, src, w=()):
            SOP(src, src, AF.Ln, w=w)
            SOP(dst, src, AF.Exp, scale=-1.0)

        for dst, src in [(cb_fm, cbias_fm_d), (nwrep, nwrep_d),
                         (onwrep, onwrep_d), (murep, murep_d),
                         (mukrep, mukrep_d), (dprep, dprep_d), (dtb, dtb_d),
                         (alogc, alog_d), (ones1, ones1_d),
                         (onescol, onescol_d), (i128, i128_d), (u01f, u01_d),
                         (sel8b, sel8_d), (selprev, selprev_d),
                         (between, between_d), (maskj, maskj_d)]:
            DMA("LDW", dst, src)
        ld0 = snap("LDW")
        VOP("tensor_copy", u01b[:], u01f[:], w=[ld0])
        VOP("tensor_copy", i128b, i128, w=[ld0])
        VOP("memset", ones_a[:], 1.0)
        VOP("memset", run_row[:], 0.0)
        VOP("memset", pstate[:], 0.0)
        VOP("memset", pstate_bf[:], 0.0)
        SOP(nega[:], alogc[:], AF.Exp, w=[ld0])
        VOP("tensor_scalar_mul", nega[:], nega[:], -1.0, w=[snap("SE")])
        for half in range(2):
            DMA("LDW", cfv,
                cdiag_d.rearrange("k j a t -> a k j t")[:, 2 * half:2 * half + 2],
                w=[snap("VE")])
            VOP("tensor_copy", cdv[:, 2 * half:2 * half + 2], cfv,
                w=[snap("LDW")])
        wiv = wint_d.rearrange("(ch cl) d -> cl ch d", cl=128)
        for cc in range(8):
            DMA("LDW", stgs[cc % 2], wiv[:, cc, :], w=[snap("VE")])
            VOP("tensor_reduce", partial[:, cc:cc + 1], stgs[cc % 2][:, :],
                AX.X, ALU.add, apply_absolute_value=True, w=[snap("LDW")])
        VOP("tensor_reduce", partial[:, 8:9], partial[:, 0:8], AX.X, ALU.add,
            w=[snap("VE")])
        MMOP(psA[0:1, 0:1], onescol, partial[:, 8:9], w=[snap("VE")])
        VOP("tensor_scalar", rowb[0:1, 0:1], psA[0:1, 0:1], 1.0 / (DIP * D),
            1e-5, ALU.mult, ALU.max, w=[snap("MM")])
        recip_rows(rowb[0:1, 1:2], rowb[0:1, 0:1], w=[snap("VE")])
        MMOP(psA[:, 8:9], ones1, rowb[0:1, 1:2], w=[snap("SE")])
        VOP("tensor_copy", rowb[:, 2:3], psA[:, 8:9], w=[snap("MM")])
        for cc in range(8):
            DMA("LDW", stgs[cc % 2], wiv[:, cc, :], w=[snap("VE")])
            VOP("tensor_scalar", stgs[cc % 2][:, :], stgs[cc % 2][:, :],
                rowb[:, 2:3], MAGIC, ALU.mult, ALU.add, w=[snap("LDW")])
            VOP("tensor_scalar", stgs[cc % 2][:, :], stgs[cc % 2][:, :],
                MAGIC - 1.0, MAGIC, ALU.max, ALU.subtract)
            VOP("tensor_scalar", wq_v[:, cc, :], stgs[cc % 2][:, :],
                1.0, None, ALU.min)
        mark("wprep", "VE")

        ipsevs = {}

        def prepipe(tau):
            c = tau
            slot = (tau + 1) % 2
            par = (tau + 1) % 2
            rb = rowb[:, par * 16:par * 16 + 16]
            xt = x0t.rearrange("p (b d) -> p b d", b=2)[:, slot, :]
            DMA("LDX", xt, x0_d[tau + 1],
                w=[mk(f"ppve{c-2}"), mk(f"ppse{c-2}")])
            lx = snap("LDX")
            st = tau + 1
            VOP("tensor_reduce", rb[:, 0:1], xt, AX.X, ALU.add,
                w=[lx, mk(f"ppse{c-1}"), mk(f"ppgp{c-1}"),
                   mk(f"ipse{c-2}")])
            VOP("scalar_tensor_tensor", ppA[:, :], xt, 1.0, xt,
                ALU.mult, ALU.mult, accum_out=rb[:, 1:2])
            sv = snap("VE")
            VOP("tensor_scalar", mu1_st[:, st:st + 1], rb[:, 0:1], 1.0 / D,
                None, ALU.mult, w=[sv])
            VOP("tensor_tensor", rb[:, 2:3], mu1_st[:, st:st + 1],
                mu1_st[:, st:st + 1], ALU.mult)
            VOP("tensor_scalar", rb[:, 3:4], rb[:, 1:2], 1.0 / D, None,
                ALU.mult)
            VOP("tensor_tensor", rb[:, 3:4], rb[:, 3:4], rb[:, 2:3],
                ALU.subtract)
            sv = snap("VE")
            VOP("tensor_scalar", rb[:, 4:5], rb[:, 3:4], 1e-5, None, ALU.add)
            SOP(rb[:, 4:5], rb[:, 4:5], AF.Ln, w=[snap("VE")])
            SOP(r1_st[:, st:st + 1], rb[:, 4:5], AF.Exp, scale=-0.5)
            se = snap("SE")
            VOP("tensor_tensor", rb[:, 5:6], r1_st[:, st:st + 1],
                r1_st[:, st:st + 1], ALU.mult, w=[se])
            VOP("tensor_tensor", rb[:, 5:6], rb[:, 5:6], rb[:, 3:4], ALU.mult)
            VOP("tensor_scalar", rb[:, 5:6], rb[:, 5:6], 1e-6, None, ALU.add)
            SOP(rb[:, 5:6], rb[:, 5:6], AF.Ln, w=[snap("VE")])
            SOP(rb[:, 6:7], rb[:, 5:6], AF.Exp, scale=-0.5)
            VOP("tensor_tensor", rb[:, 7:8], r1_st[:, st:st + 1], rb[:, 6:7],
                ALU.mult, w=[snap("SE")])
            VOP("tensor_tensor", rb[:, 8:9], mu1_st[:, st:st + 1], rb[:, 7:8],
                ALU.mult)
            VOP("tensor_scalar_mul", rb[:, 8:9], rb[:, 8:9], -1.0)
            SOP(ppA[:, :], xt, AF.Identity, bias=rb[:, 8:9], scale=rb[:, 7:8],
                w=[snap("VE")])
            GOP("tensor_tensor", ppB[:, :], ppA[:, :], nwrep, ALU.mult,
                w=[snap("SE")])
            gv = snap("GP")
            mark(f"ppgp{c}", "GP")
            VOP("tensor_reduce", rb[:, 0:1], ppB[:, :], AX.X, ALU.add, w=[gv])
            VOP("scalar_tensor_tensor", ppA[:, :], ppB[:, :], 1.0, ppB[:, :],
                ALU.mult, ALU.mult, accum_out=rb[:, 1:2])
            sv = snap("VE")
            VOP("tensor_scalar", rb[:, 2:3], rb[:, 0:1], 1.0 / D, None,
                ALU.mult, w=[sv])
            VOP("tensor_tensor", rb[:, 3:4], rb[:, 2:3], rb[:, 2:3], ALU.mult)
            VOP("tensor_scalar", rb[:, 4:5], rb[:, 1:2], 1.0 / D, None,
                ALU.mult)
            VOP("tensor_tensor", rb[:, 4:5], rb[:, 4:5], rb[:, 3:4],
                ALU.subtract)
            VOP("tensor_scalar", rb[:, 4:5], rb[:, 4:5], 1e-5, None, ALU.add)
            SOP(rb[:, 4:5], rb[:, 4:5], AF.Ln, w=[snap("VE")])
            SOP(rb[:, 5:6], rb[:, 4:5], AF.Exp, scale=-0.5)
            VOP("tensor_tensor", rb[:, 9:10], rb[:, 2:3], rb[:, 5:6],
                ALU.mult, w=[snap("SE")])
            VOP("tensor_scalar_mul", rb[:, 9:10], rb[:, 9:10], -1.0)
            SOP(ppA[:, :], ppB[:, :], AF.Identity, bias=rb[:, 9:10],
                scale=rb[:, 5:6], w=[snap("VE")])
            VOP("tensor_reduce", rb[:, 10:11], ppA[:, :], AX.X, ALU.max,
                apply_absolute_value=True, w=[snap("SE")])
            sv = snap("VE")
            VOP("tensor_scalar", rb[:, 10:11], rb[:, 10:11], 1e-5, None,
                ALU.max, w=[sv])
            VOP("tensor_scalar", rb[:, 13:14], rb[:, 10:11], 1.0 / 127.0,
                None, ALU.mult)
            VOP("tensor_copy", rb[:, 11:12], rb[:, 10:11])
            SOP(rb[:, 11:12], rb[:, 11:12], AF.Ln, w=[snap("VE")])
            SOP(rb[:, 12:13], rb[:, 11:12], AF.Exp, scale=-1.0)
            VOP("tensor_scalar", rb[:, 12:13], rb[:, 12:13], 127.0, None,
                ALU.mult, w=[snap("SE")])
            VOP("tensor_scalar", ppB[:, :], ppA[:, :], rb[:, 12:13], MAGIC,
                ALU.mult, ALU.add, w=[mk(f"q1mm{c-1}")])
            VOP("tensor_scalar", ppB[:, :], ppB[:, :], MAGIC - 128.0, MAGIC,
                ALU.max, ALU.subtract)
            VOP("tensor_scalar", q1[:, :], ppB[:, :], 127.0, None, ALU.min)
            mark(f"ppve{c}", "VE")
            mark(f"ppse{c}", "SE")
            qv = snap("VE")
            PET(q1Tv2[(tau + 16) % 2],
                [q1[:, j * 128:(j + 1) * 128] for j in range(8)],
                [bkA[0]], w=[qv, mk(f"ipse{c-1}"), mk(f"q1mm{c-2}")])
            mark(f"q1mm{c}", "MM")
            mark(f"q1ve{c}", "VE")

        def in_proj_dg(tau, dg):
            c = tau
            par = (tau + 1) % 2
            rb = rowb[:, par * 16:par * 16 + 16]
            recip = rb[:, 13:14]
            tpq = mk(f"q1ve{c}")
            q1Tv = q1Tv2[(tau + 16) % 2]
            xv = xbv[:, tau % 2, :]
            g = tau // 2
            sevs = ipsevs[tau]
            d0 = dg * 512
            dw = min(512, DIP - d0)
            bank = bkA[dg % 2][:, 0:dw]
            if dg < 2:
                wmm = [tpq, mk(f"ipse{c-1}")]
            else:
                wmm = [tpq, sevs[dg - 2]]
            for ccx in range(8):
                MMOP(bank, q1Tv[:, ccx, :], wq_v[:, ccx, d0:d0 + dw],
                     start=(ccx == 0), stop=(ccx == 7),
                     w=wmm if ccx == 0 else ())
            mmv = snap("MM")
            if dg < 4:
                SOP(szt[:, d0:d0 + 512], bank, AF.Identity, bias=0.0,
                    scale=recip,
                    w=[mmv, mk(f"sz{c-1}"),
                       mk(f"xdpve{((c - 4) // 2) * 2 - 1}")]
                    if dg == 0 else [mmv])
                if dg == 3:
                    SOP(szt[:, :], szt[:, :], AF.Silu)
            elif dg < 8:
                SOP(xv[:, d0 - DI:d0 - DI + 512], bank, AF.Identity,
                    bias=0.0, scale=recip,
                    w=[mmv, mk(f"cvpet{g-1}")] if dg == 4 else [mmv])
            else:
                SOP(xv[:, 2048:2304], bank[:, 0:256], AF.Identity,
                    bias=0.0, scale=recip, w=[mmv])
                SOP(dt_tm2[:, (tau % 4) * 32:(tau % 4) * 32 + 32],
                    bank[:, 256:288], AF.Identity, bias=0.0, scale=recip,
                    w=[mk(f"dtmm{c-4}")])
            sevs.append(snap("SE"))
            if dg == 3:
                DMA("BNC", sz_dram[tau], szt[:, :], w=[snap("SE")])
                mark(f"sz{c}", "BNC")
            if dg == 8:
                mark(f"ipse{c}", "SE")

        def ip_fillers(tau):
            return [(lambda dg=dg: in_proj_dg(tau, dg)) for dg in range(4, 9)]

        def in_proj(tau, part=None):
            c = tau
            par = (tau + 1) % 2
            rb = rowb[:, par * 16:par * 16 + 16]
            recip = rb[:, 13:14]
            tpq = mk(f"q1ve{c}")
            q1Tv = q1Tv2[(tau + 16) % 2]
            if tau == -1:
                for j in range(18):
                    bank = bkA[j % 2]
                    wmm = [tpq] if j < 2 else [tpq, snap("SE")]
                    for ccx in range(8):
                        MMOP(bank[:, 0:128],
                             wq_v[:, ccx, DI + j * 128:DI + (j + 1) * 128],
                             q1Tv[:, ccx, :], start=(ccx == 0), stop=(ccx == 7),
                             w=wmm if ccx == 0 else ())
                    SOP(cvv[:, j, 13:16], bank[:, 125:128], AF.Identity,
                        bias=0.0, scale=recip, w=[snap("MM")])
                mark("ipse-1", "SE")
                mark("cvse-1", "SE")
                return
            if part in (0, None):
                ipsevs[tau] = []
            dgr = (range(9) if part is None else
                   (range(0, 4) if part == 0 else range(4, 9)))
            for dg in dgr:
                in_proj_dg(tau, dg)

        cvsevs = {}

        def conv_pet(g):
            PET(cvv[:, :, 16:16 + 128],
                [xbv[:, 0, j * 128:(j + 1) * 128] for j in range(18)],
                [bkA[2], bkA[3]],
                w=[mk(f"ipse{2*g}"), mk(f"halo{g-1}"), mk(f"cvse{g-1}"),
                   mk(f"fve{2*g-1}")])
            PET(cvv[:, :, 16 + 128:16 + 256],
                [xbv[:, 1, j * 128:(j + 1) * 128] for j in range(18)],
                [bkA[2], bkA[3]], w=[mk(f"ipse{2*g+1}")])
            mark(f"cvpet{g}", "MM")
            mark(f"cvpve{g}", "VE")

        def conv_j(g, j):
            tpx = mk(f"cvpve{g}")
            sevs = cvsevs[g]
            bank = bkA[2 + (j % 2)][:, 0:256]
            wmm = ([tpx, mk(f"cvse{g-1}"), mk(f"fve{2*g-1}")]
                   if j < 2 else [tpx, sevs[j - 2]])
            for k in range(4):
                MMOP(bank, cdv[:, k, j, :], cvv[:, j, 13 + k:13 + k + 256],
                     start=(k == 0), stop=(k == 3),
                     w=wmm if k == 0 else ())
            mmv = snap("MM")
            if j < 16:
                SOP(xsv[:, j, :], bank, AF.Identity, bias=cb_fm[:, j:j + 1],
                    scale=1.0,
                    w=[mmv, mk(f"fmm{2*g-1}"), mk(f"cbnc{2*g-1}"),
                       mk(f"rtmm{2*g-1}")]
                    if j == 0 else [mmv])
            else:
                SOP(bcv[:, j - 16, :], bank, AF.Identity,
                    bias=cb_fm[:, j:j + 1], scale=1.0, w=[mmv])
            sevs.append(snap("SE"))
            if j == 17:
                SOP(xsil[:, :], xsil[:, :], AF.Silu)
                SOP(bcfm[:, :], bcfm[:, :], AF.Silu)
                mark(f"cvse{g}", "SE")
                mark(f"cvmm{g}", "MM")
                GOP("tensor_copy", cvv[:, :, 13:16], cvv[:, :, 269:272],
                    w=[snap("SE"), mk(f"cvmm{g}")])
                mark(f"halo{g}", "GP")

        def cv_fillers(g):
            cvsevs[g] = []
            out = []
            for j0 in range(0, 18, 3):
                out.append(lambda j0=j0: [conv_j(g, j)
                                          for j in range(j0, j0 + 3)])
            return out

        def conv_mm(g, part=None):
            if part in (0, None):
                cvsevs[g] = []
            jr = (range(18) if part is None else
                  (range(0, 9) if part == 0 else range(9, 18)))
            for j in jr:
                conv_j(g, j)

        def ssd_front1(c):
            par = c % 2
            g = c // 2
            cse = mk(f"cvse{g}")
            if par == 0:
                pbanks = [bkB[2]]
                pw = [cse, mk(f"ydve{c-1}")]
            else:
                pbanks = [bkA[2], bkA[3]]
                pw = [cse, mk(f"cvpve{g+1}")]
            PET(xtvP[par],
                [xsv[:, j, par * 128:(par + 1) * 128] for j in range(16)],
                pbanks, w=pw, evac_w=[mk(f"xdpgp{c-2}")])
            PET(b_tmP[par].rearrange("p (a t) -> p a t", a=1),
                [bcv[:, 0, par * 128:(par + 1) * 128]],
                pbanks, w=[mk(f"smm{c-1}")], evac_w=[mk(f"smm{c-2}")])
            DMA("BNC", c_dram[c], bcv[:, 1, par * 128:(par + 1) * 128],
                w=[cse])
            mark(f"cbnc{c}", "BNC")
            if par == 0:
                TROP(psB[0:64, 0:128],
                     dt_tm2[:, (c % 4) * 32:(c % 4) * 32 + 64],
                     i128, w=[mk(f"ipse{c}"), mk(f"ipse{c+1}"),
                              mk(f"b0ve{c-1}")])
                mark(f"dtmm{c}", "MM")
                M[f"dtmm{c+1}"] = M[f"dtmm{c}"]
                VOP("tensor_copy", dtx_fm, psB[0:64, 0:128],
                    w=[snap("MM"), mk(f"acs{c-1}"), mk(f"fmm{c-1}")])
                SOP(dtx_fm, dtx_fm, AF.Identity, bias=dtb, scale=1.0,
                    w=[snap("VE")])
                se = snap("SE")
                VOP("tensor_scalar_mul", sp1_fm, dtx_fm, -1.0, w=[se])
                VOP("tensor_tensor", sp1_fm, dtx_fm, sp1_fm, ALU.max)
                SOP(sp2_fm, sp1_fm, AF.Exp, bias=0.0, scale=-1.0,
                    w=[snap("VE")])
                VOP("tensor_scalar", sp2_fm, sp2_fm, 1.0, None, ALU.add,
                    w=[snap("SE")])
                SOP(sp2_fm, sp2_fm, AF.Ln, bias=0.0, scale=1.0,
                    w=[snap("VE")])
                se = snap("SE")
                VOP("tensor_scalar", dta_fm, dtx_fm, 0.0, None, ALU.max,
                    w=[se, mk(f"fmm{c-1}")])
                VOP("tensor_tensor", dta_fm, dta_fm, sp2_fm, ALU.add)
                VOP("tensor_scalar", a_fm, dta_fm, nega, None, ALU.mult)
                VOP("tensor_tensor_scan", acs_fm, ones_a, a_fm, 0.0,
                    ALU.mult, ALU.add, w=[snap("VE")])
                mark(f"scan{c}", "VE")
                M[f"scan{c+1}"] = M[f"scan{c}"]
            sva = mk(f"scan{c}")
            DMA("ACS", acs_dram[par],
                acs_fm[par * 32:(par + 1) * 32, :],
                w=[sva, mk(f"acs{c-2}")])
            DMA("ACS", acs8P[par],
                acs_dram[par].rearrange("h t -> (h t)").rearrange(
                    "(g x) -> g x", g=8),
                w=[snap("ACS"), mk(f"rtmm{c-2}")])
            DMA("ACS", asum_row,
                acs_dram[par].rearrange("h t -> t h")[127:128, :],
                w=[mk(f"fmm{c-1}"), mk(f"fse{c-1}")],
                allow_slow_non_contiguous=True)
            mark(f"acs8{c}", "ACS")
            mark(f"acs{c}", "ACS")

        def ssd_front2(c):
            par = c % 2
            g = c // 2
            cse = mk(f"cvse{g}")
            sva = mk(f"scan{c}")
            acsf = mk(f"acs8{c}")
            idn = i128[par * 32:(par + 1) * 32, par * 32:(par + 1) * 32]
            TROP(psB[0:128, 128:160], acs_fm[par * 32:(par + 1) * 32, :],
                 idn, w=[sva])
            TROP(psB[0:128, 160:192], dta_fm[par * 32:(par + 1) * 32, :],
                 idn)
            mmv = snap("MM")
            VOP("tensor_copy", acs_tm, psB[:, 128:160],
                w=[mmv, mk(f"fse{c-1}")])
            VOP("tensor_scalar_mul", nacs_tmP[par], acs_tm, -1.0,
                w=[mk(f"rtse{c-2}")])
            VOP("tensor_copy", dta_tmP[par], psB[:, 160:192],
                w=[mk(f"umgp{c-2}")])
            svt = snap("VE")
            MMOP(psB[:, 192:224], ones1, run_row[0:1, 0:32], w=[svt])
            VOP("tensor_tensor", gtmp, acs_tm, psB[:, 192:224], ALU.add,
                w=[snap("MM"), mk(f"fse{c-1}")])
            SOP(g_st[:, 32 * c:32 * c + 32], gtmp, AF.Exp, w=[snap("VE")])
            SOP(eacsP[par], acs_tm, AF.Exp, w=[mk(f"yove{c-2}")])
            MMOP(psB[:, 224:256], ones1, asum_row[0:1, 0:32],
                 w=[acsf, snap("SE")])
            VOP("tensor_tensor", gtmp, psB[:, 224:256], acs_tm, ALU.subtract,
                w=[snap("MM"), snap("SE")])
            SOP(ds_tmP[par], gtmp, AF.Exp,
                w=[snap("VE"), mk(f"uvgp{c-2}")])
            SOP(rowdec, asum_row, AF.Exp, w=[acsf, mk(f"fmm{c-1}")])
            MMOP(psB[:, 256:288], ones1, rowdec, w=[snap("SE")])
            VOP("tensor_copy", dec_repP[par], psB[:, 256:288],
                w=[snap("MM"), mk(f"psgp{c-2}")])
            VOP("tensor_tensor", run_row, run_row, asum_row, ALU.add,
                w=[snap("MM"), acsf])
            MMOP(psB[:, 288:416], bcv[:, 0, par * 128:(par + 1) * 128],
                 bcv[:, 1, par * 128:(par + 1) * 128],
                 w=[cse, snap("VE")])
            VOP("tensor_tensor", pm_bfP[par], psB[:, 288:416], u01b, ALU.mult,
                w=[snap("MM")])
            mark(f"b0ve{c}", "VE")
            mark(f"fve{c}", "VE")
            mark(f"fmm{c}", "MM")
            mark(f"fse{c}", "SE")

        def ssd_back1(c):
            par = c % 2
            g = c // 2
            yoev = []
            for q in range(4):
                bank = bkB[1 if q % 2 == 0 else 3]
                if q == 0:
                    wmm = [mk(f"cvse{g}"), mk(f"sve{c-1}"), mk(f"rtse{c-1}"),
                           mk(f"pbfve{c-1}"), mk("wprep")]
                elif q >= 2:
                    wmm = [yoev[q - 2]]
                else:
                    wmm = ()
                MMOP(bank, bcv[:, 1, par * 128:(par + 1) * 128],
                     pstate_bf[:, q * 512:(q + 1) * 512], w=wmm)
                VOP("tensor_tensor",
                    y_p[:, q * 512:(q + 1) * 512].rearrange(
                        "p (h q2) -> p h q2", h=8),
                    bank.rearrange("p (h q2) -> p h q2", h=8),
                    bcast2(eacsP[par][:, q * 8:(q + 1) * 8], 64), ALU.mult,
                    w=[snap("MM"), mk(f"ypbnc{c-1}"), mk(f"fse{c}")]
                    if q == 0 else [snap("MM")])
                yoev.append(snap("VE"))
            mark(f"yove{c}", "VE")
            GOP("tensor_tensor", u_t.rearrange("p (h q) -> p h q", h=32),
                x_tmP[par].rearrange("p (h q) -> p h q", h=32),
                bcast2(dta_tmP[par], 64), ALU.mult,
                w=[mk(f"fve{c}"), mk(f"smm{c-1}")])
            mark(f"umgp{c}", "GP")

        def ssd_back2(c, fillers=()):
            par = c % 2
            fillers = list(fillers)
            acsf = mk(f"acs8{c}")
            tau_last = min(2 * (c // 2) + 5, 15)
            GOP("tensor_tensor", xdp.rearrange("p (h q) -> p h q", h=32),
                x_tmP[par].rearrange("p (h q) -> p h q", h=32),
                bcast2(dprep, 64), ALU.mult,
                w=[mk(f"fve{c}"), mk(f"xdpve{c-1}"), mk(f"sz{tau_last}")])
            mark(f"xdpgp{c}", "GP")
            VOP("tensor_tensor", y_p, y_p, xdp, ALU.add,
                w=[snap("GP"), mk(f"yove{c}")])
            mark(f"xdpve{c}", "VE")
            grp_se = []
            grp_ye = []
            grp_sv = []
            grp_sttv = []

            def lam_rt(g8):
                bank = bkB[1 if g8 % 2 == 0 else 3]
                MMOP(bank, sel8b[:, g8 * 128:(g8 + 1) * 128], acs8P[par],
                     w=[acsf, mk(f"yove{c}"), mk(f"rtse{c-1}")] if g8 < 2 else
                     [acsf, grp_se[g8 - 2]])
                mmv = snap("MM")
                ew = [mmv, mk(f"stt{c-1}")] if g8 < 2 else [mmv, grp_sv[g8 - 2]]
                for hh in range(4):
                    SOP(m_v[:, g8 % 2, hh * 128:(hh + 1) * 128],
                        bank[:, hh * 128:(hh + 1) * 128], AF.Exp,
                        bias=nacs_tmP[par][:, g8 * 4 + hh:g8 * 4 + hh + 1],
                        scale=1.0, w=ew if hh == 0 else ())
                VOP("scalar_tensor_tensor",
                    m_v[:, g8 % 2, :].rearrange("p (h t) -> p h t", h=4),
                    m_v[:, g8 % 2, :].rearrange("p (h t) -> p h t", h=4),
                    1.0,
                    pm_bfP[par].unsqueeze(1).broadcast_to([128, 4, 128]),
                    ALU.min, ALU.mult, w=[snap("SE"), mk(f"b0ve{c}")])
                grp_se.append(snap("SE"))
                grp_sv.append(snap("VE"))
                grp_sttv.append(snap("VE"))

            def lam_yd(g8):
                ydbank = bkB[2][:, (g8 % 2) * 256:(g8 % 2) * 256 + 256]
                ydw = [grp_sttv[g8]]
                if g8 >= 2:
                    ydw.append(grp_ye[g8 - 2])
                else:
                    ydw.append(mk(f"fve{c}"))
                    ydw.append(mk(f"umgp{c}"))
                for hh in range(4):
                    h = g8 * 4 + hh
                    MMOP(ydbank[:, hh * 64:(hh + 1) * 64],
                         m_v[:, g8 % 2, hh * 128:(hh + 1) * 128],
                         u_t[:, h * 64:(h + 1) * 64],
                         w=ydw if hh == 0 else ())
                VOP("tensor_tensor", y_p[:, g8 * 256:(g8 + 1) * 256],
                    y_p[:, g8 * 256:(g8 + 1) * 256], ydbank, ALU.add,
                    w=[snap("MM")])
                grp_ye.append(snap("VE"))

            for g8 in range(8):
                lam_rt(g8)
                if fillers:
                    fillers.pop(0)()
                lam_yd(g8)
            while fillers:
                fillers.pop(0)()
            mark(f"rtse{c}", "SE")
            mark(f"stt{c}", "VE")
            mark(f"ydve{c}", "VE")
            mark(f"rtmm{c}", "MM")

        def ssd_back3(c):
            par = c % 2
            GOP("tensor_tensor", u_t.rearrange("p (h q) -> p h q", h=32),
                u_t.rearrange("p (h q) -> p h q", h=32),
                bcast2(ds_tmP[par], 64), ALU.mult, w=[mk(f"rtmm{c}")])
            mark(f"uvgp{c}", "GP")
            GOP("tensor_tensor", pstate.rearrange("p (h q) -> p h q", h=32),
                pstate.rearrange("p (h q) -> p h q", h=32),
                bcast2(dec_repP[par], 64), ALU.mult,
                w=[mk(f"pbfve{c-1}"), mk(f"fve{c}")])
            mark(f"psgp{c}", "GP")
            stev = []
            for q in range(4):
                bank = bkB[1 if q % 2 == 0 else 3]
                if q == 0:
                    wmm = [mk(f"uvgp{c}"), mk(f"rtse{c}")]
                elif q >= 2:
                    wmm = [stev[q - 2]]
                else:
                    wmm = ()
                MMOP(bank, b_tmP[par], u_t[:, q * 512:(q + 1) * 512], w=wmm)
                VOP("tensor_tensor", pstate[:, q * 512:(q + 1) * 512],
                    pstate[:, q * 512:(q + 1) * 512], bank, ALU.add,
                    w=[snap("MM"), snap("GP")] if q == 0 else [snap("MM")])
                stev.append(snap("VE"))
            mark(f"smm{c}", "MM")
            mark(f"sve{c}", "VE")
            VOP("tensor_copy", pstate_bf, pstate, w=[snap("VE")])
            mark(f"pbfve{c}", "VE")
            DMA("BNC", yp_dram[c], y_p, w=[snap("VE")])
            mark(f"ypbnc{c}", "BNC")

        wov = woutt_d.rearrange("(ch cl) d -> cl ch d", cl=128)
        wstB = [ppA, ppB]

        def w1_thunks():
            out = []
            first = [True]

            def cast2(wvd, wv_, cc):
                for q in range(2):
                    w = ([mk("q1ve15"), mk("ppve15"), mk("ppgp15"),
                          mk("ppse15"), mk("ipse15")]
                         if first[0] else [snap("VE")])
                    first[0] = False
                    DMA("LDX", wstB[(cc + q) % 2], wvd[:, cc + q, :], w=w)
                    VOP("tensor_copy", wv_[:, cc + q, :],
                        wstB[(cc + q) % 2], w=[snap("LDX")])
            for wd, wv_ in [(wrt_d, wrv), (wkt_d, wkv)]:
                wvd = wd.rearrange("(ch cl) d -> cl ch d", cl=128)
                for cc in range(0, 8, 2):
                    out.append(lambda wvd=wvd, wv_=wv_, cc=cc:
                               cast2(wvd, wv_, cc))
            return out

        def w1b_thunks():
            out = []
            first = [True]

            def cast2(cc):
                wvd = wvt_d.rearrange("(ch cl) d -> cl ch d", cl=128)
                for q in range(2):
                    DMA("LDX", wstB[(cc + q) % 2], wvd[:, cc + q, :],
                        w=[snap("VE")])
                    w = ([snap("LDX"), mk("cvmm7")] if first[0]
                         else [snap("LDX")])
                    first[0] = False
                    VOP("tensor_copy", wvv[:, cc + q, :],
                        wstB[(cc + q) % 2], w=w)
            return [lambda cc=cc: cast2(cc) for cc in range(0, 8, 2)]

        def w2_thunks():
            out = []

            def red2(cc):
                for q in range(2):
                    DMA("LDX", wstB[(cc + q) % 2], wov[:, cc + q, :],
                        w=[snap("VE")])
                    VOP("tensor_reduce", wpp[:, cc + q:cc + q + 1],
                        wstB[(cc + q) % 2], AX.X, ALU.add,
                        apply_absolute_value=True, w=[snap("LDX")])

            def tail():
                VOP("tensor_reduce", wpp[:, 16:17], wpp[:, 0:16], AX.X,
                    ALU.add, w=[snap("VE")])
                MMOP(psA[0:1, 0:1], onescol, wpp[:, 16:17],
                     w=[snap("VE"), mk("ipse15"), mk("q1mm15")])
                VOP("tensor_scalar", wpp[0:1, 17:18], psA[0:1, 0:1],
                    1.0 / (DI * D), 1e-5, ALU.mult, ALU.max, w=[snap("MM")])
                recip_rows(wpp[0:1, 18:19], wpp[0:1, 17:18], w=[snap("VE")])
                MMOP(psA[:, 8:9], ones1, wpp[0:1, 18:19], w=[snap("SE")])
                VOP("tensor_copy", wpp[:, 2:3], psA[:, 8:9], w=[snap("MM")])
            for cc in range(0, 16, 2):
                out.append(lambda cc=cc: red2(cc))
            out.append(tail)
            return out

        def w3_thunks():
            first = [True]

            def quant2(cc):
                for q in range(2):
                    DMA("LDX", wstB[(cc + q) % 2], wov[:, cc + q, :],
                        w=[snap("VE")])
                    w = ([snap("LDX"), mk("cvmm7"), mk("ipse15")]
                         if first[0] else [snap("LDX")])
                    first[0] = False
                    VOP("tensor_scalar", wstB[(cc + q) % 2],
                        wstB[(cc + q) % 2], wpp[:, 2:3], MAGIC,
                        ALU.mult, ALU.add, w=w)
                    VOP("tensor_scalar", wstB[(cc + q) % 2],
                        wstB[(cc + q) % 2], MAGIC - 1.0, MAGIC,
                        ALU.max, ALU.subtract)
                    VOP("tensor_scalar", wo_v[:, cc + q, :],
                        wstB[(cc + q) % 2], 1.0, None, ALU.min)
            return [lambda cc=cc: quant2(cc) for cc in range(0, 16, 2)]

        prepipe(-1)
        in_proj(-1)
        prepipe(0)
        in_proj(0)
        prepipe(1)
        in_proj(1)
        conv_pet(0)
        conv_mm(0)
        prepipe(2)
        in_proj(2)
        prepipe(3)
        in_proj(3)
        prepipe(4)
        prepipe(5)
        w3t = None
        for g in range(8):
            if g == 7:
                w3t = w3_thunks()
            ssd_front1(2 * g)
            if g < 6:
                in_proj(2 * g + 4, 0)
            ssd_front2(2 * g)
            ssd_back1(2 * g)
            if g < 7:
                conv_pet(g + 1)
            if g < 6:
                in_proj(2 * g + 4, 1)
            ssd_front1(2 * g + 1)
            if g < 6:
                in_proj(2 * g + 5, 0)
            ssd_back2(2 * g,
                      ip_fillers(2 * g + 5) if g < 6 else
                      (w1_thunks() if g == 6 else
                       w1b_thunks() + w3t[0:4]))
            ssd_front2(2 * g + 1)
            ssd_back3(2 * g)
            ssd_back1(2 * g + 1)
            ssd_back2(2 * g + 1,
                      cv_fillers(g + 1) if g < 7 else w3t[4:8])
            if g == 6:
                for th in w2_thunks():
                    th()
            ssd_back3(2 * g + 1)
            if g < 5:
                prepipe(2 * g + 6)
                prepipe(2 * g + 7)

        DMA("BNC", gin_S, pstate, w=[mk("pbfve15"), snap("VE")])
        DMA("BNC", gin_as, run_row[0:1, 0:32], w=[snap("VE")])
        bncv = snap("BNC")
        E("gpsimd", lambda e: e.collective_compute(
            "AllGather", ALU.bypass, [list(range(8))],
            [gin_S.opt()], [gout_S.opt()]), w=[bncv], i="CC")
        E("gpsimd", lambda e: e.collective_compute(
            "AllGather", ALU.bypass, [list(range(8))],
            [gin_as.opt()], [gout_as.opt()]), i="CC")
        ccv = snap("CC")

        DMA("LDX", sb8_32, gout_as, w=[ccv])
        MMOP(psA[0:8, 32:64], between, sb8_32, w=[snap("LDX"), snap("VE")])
        SOP(sbw8, psA[0:8, 32:64], AF.Exp, w=[snap("MM")])
        VOP("tensor_scalar", sbw8, sbw8, maskj, None, ALU.mult,
            w=[snap("SE")])
        DMA("BNC", w8_dram, sbw8, w=[snap("VE")])
        DMA("BNC", wflat, w8_dram.rearrange("j h -> (j h)").unsqueeze(0),
            w=[snap("BNC")])
        MMOP(psA[:, 256:512], ones1, wflat, w=[snap("BNC")])
        VOP("tensor_copy", wrep, psA[:, 256:512], w=[snap("MM")])
        VOP("memset", stmp[:, :], 0.0)
        gsv = gout_S.rearrange("(r p) d -> r p d", p=128)
        smul = []
        for j in range(8):
            DMA("LDX", sjv[:, j % 2, :], gsv[j],
                w=[ccv, snap("VE")] if j < 2 else [ccv, smul[j - 2]])
            VOP("tensor_tensor", tcorr.rearrange("p (h q) -> p h q", h=32),
                sjv[:, j % 2, :].rearrange("p (h q) -> p h q", h=32),
                bcast2(wrep[:, j * 32:(j + 1) * 32], 64), ALU.mult,
                w=[snap("LDX")])
            smul.append(snap("VE"))
            VOP("tensor_tensor", stmp, stmp, tcorr, ALU.add)
        VOP("tensor_copy", sin_bf, stmp, w=[snap("VE")])
        mark("sinve", "VE")

        def passb_attn_pre_a(c):
            par = c % 2
            ypB = ypBP[par]
            DMA("LDX", ypB, yp_dram[c],
                w=[mk("sinve"), mk(f"q2ve{(c - 2) % 16}"), mk("attnseL")])
            DMA("LDX", szB, sz_dram[c], w=[mk(f"gatev{(c - 1) % 16}")])
            DMA("LDX", cfmB, c_dram[c], w=[mk(f"ycmm{(c - 1) % 16}")])
            xt = x0t.rearrange("p (b d) -> p b d", b=2)[:, 0, :]
            DMA("LDX", xt, x0_d[c + 1], w=[mk("attnseL")])
            lx = snap("LDX")
            ycev = []
            for q in range(4):
                bank = bkA[q % 2]
                if q == 0:
                    wmm = [lx, mk("sinve"), mk("oppveL"), mk("q2mmL")]
                elif q >= 2:
                    wmm = [ycev[q - 2]]
                else:
                    wmm = ()
                MMOP(bank, cfmB, sin_bf[:, q * 512:(q + 1) * 512], w=wmm)
                VOP("tensor_tensor",
                    tcorr[:, q * 512:(q + 1) * 512].rearrange(
                        "p (h q2) -> p h q2", h=8),
                    bank.rearrange("p (h q2) -> p h q2", h=8),
                    bcast2(g_st[:, 32 * c + q * 8:32 * c + (q + 1) * 8], 64),
                    ALU.mult,
                    w=[snap("MM"), ("BNC", cnt["BNC"]), mk("blnL"),
                       mk("qgpL")] if q == 0
                    else [snap("MM")])
                ycev.append(snap("VE"))
            mark(f"ycmm{c}", "MM")
            VOP("tensor_tensor", ypB, ypB, tcorr, ALU.add, w=[lx])
            VOP("tensor_tensor", ypB, ypB, szB, ALU.mult)
            mark(f"gatev{c}", "VE")
            VOP("scalar_tensor_tensor", tcorr, ypB, 1.0, ypB, ALU.mult,
                ALU.mult, accum_out=rowb[:, 0:1])
            sv = snap("VE")
            VOP("tensor_scalar", rowb[:, 1:2], rowb[:, 0:1], 1.0 / DI, None,
                ALU.mult, w=[sv])
            VOP("tensor_scalar", rowb[:, 1:2], rowb[:, 1:2], 1e-6, None,
                ALU.add)
            SOP(rowb[:, 1:2], rowb[:, 1:2], AF.Ln, w=[snap("VE")])
            SOP(rowb[:, 2:3], rowb[:, 1:2], AF.Exp, scale=-0.5)
            VOP("scalar_tensor_tensor", tcorr, ypB, rowb[:, 2:3], onwrep,
                ALU.mult, ALU.mult, w=[snap("SE")])

        def passb_attn_pre_b(c):
            par = c % 2
            ypB = ypBP[par]
            VOP("tensor_reduce", rowb[:, 3:4], tcorr, AX.X, ALU.add)
            VOP("scalar_tensor_tensor", ypB, tcorr, 1.0, tcorr, ALU.mult,
                ALU.mult, accum_out=rowb[:, 4:5])
            sv = snap("VE")
            VOP("tensor_scalar", rowb[:, 5:6], rowb[:, 3:4], 1.0 / DI, None,
                ALU.mult, w=[sv])
            VOP("tensor_tensor", rowb[:, 6:7], rowb[:, 5:6], rowb[:, 5:6],
                ALU.mult)
            VOP("tensor_scalar", rowb[:, 7:8], rowb[:, 4:5], 1.0 / DI, None,
                ALU.mult)
            VOP("tensor_tensor", rowb[:, 7:8], rowb[:, 7:8], rowb[:, 6:7],
                ALU.subtract)
            VOP("tensor_scalar", rowb[:, 7:8], rowb[:, 7:8], 1e-5, None,
                ALU.add)
            SOP(rowb[:, 7:8], rowb[:, 7:8], AF.Ln, w=[snap("VE")])
            SOP(rowb[:, 8:9], rowb[:, 7:8], AF.Exp, scale=-0.5)
            VOP("tensor_tensor", rowb[:, 9:10], rowb[:, 5:6], rowb[:, 8:9],
                ALU.mult, w=[snap("SE")])
            VOP("tensor_scalar_mul", rowb[:, 9:10], rowb[:, 9:10], -1.0)
            SOP(ypB, tcorr, AF.Identity, bias=rowb[:, 9:10],
                scale=rowb[:, 8:9], w=[snap("VE")])
            VOP("tensor_reduce", rowb[:, 10:11], ypB, AX.X, ALU.max,
                apply_absolute_value=True, w=[snap("SE")])
            sv = snap("VE")
            VOP("tensor_scalar", rowb[:, 10:11], rowb[:, 10:11], 1e-5, None,
                ALU.max, w=[sv])
            VOP("tensor_scalar", rowb[:, 13:14], rowb[:, 10:11], 1.0 / 127.0,
                None, ALU.mult)
            VOP("tensor_copy", rowb[:, 11:12], rowb[:, 10:11])
            SOP(rowb[:, 11:12], rowb[:, 11:12], AF.Ln, w=[snap("VE")])
            SOP(rowb[:, 12:13], rowb[:, 11:12], AF.Exp, scale=-1.0)
            VOP("tensor_scalar", rowb[:, 12:13], rowb[:, 12:13], 127.0, None,
                ALU.mult, w=[snap("SE")])
            VOP("tensor_scalar", tcorr, ypB, rowb[:, 12:13], MAGIC, ALU.mult,
                ALU.add, w=[mk("q2mmL")])
            VOP("tensor_scalar", tcorr, tcorr, MAGIC - 128.0, MAGIC, ALU.max,
                ALU.subtract)
            VOP("tensor_scalar", q2, tcorr, 127.0, None, ALU.min)
            mark("qgpL", "VE")
            mark(f"q2ve{c}", "VE")
            qv = snap("VE")
            PET(q2Tv, [q2[:, j * 128:(j + 1) * 128] for j in range(16)],
                [bkA[0], bkA[1]], w=[qv, mk("oppveL")])
            mark("q2mmL", "MM")
            mark("apveL", "VE")
            mark(f"apve{c}", "VE")
            mark("q2tpqL", "VE")

        def passb_attn_pre(c):
            passb_attn_pre_a(c)
            passb_attn_pre_b(c)

        def passb_attn_mm(c):
            par = c % 2
            x1b = x1bv[par] if c != 15 else x1b3
            xt = x0t.rearrange("p (b d) -> p b d", b=2)[:, 0, :]
            lx = ("LDX", cnt["LDX"])
            tpq = mk("q2tpqL")
            for dg in range(2):
                bank = bkA[dg]
                for cc2 in range(16):
                    MMOP(bank, q2Tv[:, cc2, :],
                         wo_v[:, cc2, dg * 512:dg * 512 + 512],
                         start=(cc2 == 0), stop=(cc2 == 15),
                         w=[tpq] if (dg == 0 and cc2 == 0) else ())
            mmv = snap("MM")
            VOP("tensor_scalar", rowb[:, 14:15], r1_st[:, c + 1:c + 2], 1.0,
                None, ALU.add, w=[mmv])
            VOP("tensor_tensor", rowb[:, 15:16], mu1_st[:, c + 1:c + 2],
                r1_st[:, c + 1:c + 2], ALU.mult)
            VOP("tensor_scalar_mul", rowb[:, 15:16], rowb[:, 15:16], -1.0)
            SOP(t1b, xt, AF.Identity, bias=rowb[:, 15:16],
                scale=rowb[:, 14:15],
                w=[snap("VE"), lx, mk("blnL")])
            sev = snap("SE")
            for dg in range(2):
                VOP("scalar_tensor_tensor", x1b[:, dg * 512:dg * 512 + 512],
                    psA[:, dg * 512:dg * 512 + 512], rowb[:, 13:14],
                    t1b[:, dg * 512:dg * 512 + 512], ALU.mult, ALU.add,
                    w=[sev, mmv, mk(f"ffnve{c-2}")]
                    if dg == 0 else ())
            mark("attnveL", "VE")
            mark("attnseL", "SE")
            mark("oppveL", "VE")
            mark(f"attnve{c}", "VE")

        def passb_ln2(c, src):
            rb = rowb[:, 16:32]
            VOP("tensor_reduce", rb[:, 0:1], src, AX.X, ALU.add,
                w=[mk(f"attnve{c}")])
            VOP("scalar_tensor_tensor", lnsink, src, 1.0, src, ALU.mult,
                ALU.mult, accum_out=rb[:, 1:2], w=[mk(f"blnrow{c-1}")])
            sv = snap("VE")
            VOP("tensor_scalar", rb[:, 2:3], rb[:, 0:1], 1.0 / D, None,
                ALU.mult, w=[sv])
            VOP("tensor_tensor", rb[:, 3:4], rb[:, 2:3], rb[:, 2:3], ALU.mult)
            VOP("tensor_scalar", rb[:, 4:5], rb[:, 1:2], 1.0 / D, None,
                ALU.mult)
            VOP("tensor_tensor", rb[:, 4:5], rb[:, 4:5], rb[:, 3:4],
                ALU.subtract)
            VOP("tensor_scalar", rb[:, 4:5], rb[:, 4:5], 1e-5, None, ALU.add)
            SOP(rb[:, 4:5], rb[:, 4:5], AF.Ln, w=[snap("VE"), mk(f"bse{c-1}")])
            SOP(rb[:, 5:6], rb[:, 4:5], AF.Exp, scale=-0.5)
            VOP("tensor_tensor", rb[:, 6:7], rb[:, 2:3], rb[:, 5:6], ALU.mult,
                w=[snap("SE")])
            VOP("tensor_scalar_mul", rb[:, 6:7], rb[:, 6:7], -1.0)
            mark(f"blnrow{c}", "VE")
            SOP(ln2tm, src, AF.Identity, bias=rb[:, 6:7], scale=rb[:, 5:6],
                w=[snap("VE"), mk(f"ffnve{c-1}")])
            sev = snap("SE")
            if c == 0:
                DMA("LDX", lfv[:, :, 15:16],
                    ln2b_dram.rearrange("a (j p) -> p j a", p=128),
                    w=[mk("b2bnc"), mk(f"ffngp{c-1}")],
                    allow_slow_non_contiguous=True)
                hv = [snap("LDX")]
            else:
                VOP("tensor_copy", lfv[:, :, 15:16], lfv[:, :, 143:144],
                    w=[mk(f"ffngp{c-1}"), mk(f"ffnmm{c-1}")])
                hv = [snap("VE")]
            PET(lfv[:, :, 16:144],
                [ln2tm[:, j * 128:(j + 1) * 128] for j in range(8)],
                [bkB[0], bkB[1]],
                w=[sev, mk(f"ffnmm{c-1}"), mk(f"rve{c-1}")] + hv,
                evac_w=[mk(f"ffngp{c-1}")])
            if c == 14:
                VOP("tensor_copy", ln2h15[:, 0:8],
                    lfv[:, :, 143:144].squeeze(2), w=[snap("VE")])
            mark(f"bln{c}", "VE")
            mark("blnL", "VE")

        def passb_ffn_mix(c, x1src):
            VOP("tensor_tensor", dmv, lfv[:, :, 16:144], lfv[:, :, 15:143],
                ALU.subtract, w=[mk(f"bln{c}"), mk(f"ffngp{c-1}")])
            VOP("tensor_tensor", xrv, dmv,
                murep.unsqueeze(2).broadcast_to([128, 8, 128]), ALU.mult,
                w=[mk(f"ffnmm{c-1}")])
            VOP("tensor_tensor", xrv, xrv, lfv[:, :, 15:143], ALU.add)
            GOP("tensor_tensor", xkv, dmv,
                mukrep.unsqueeze(2).broadcast_to([128, 8, 128]), ALU.mult,
                w=[snap("VE"), mk(f"ffnmm{c-1}")])
            GOP("tensor_tensor", xkv, xkv, lfv[:, :, 15:143], ALU.add)
            mark(f"ffngp{c}", "GP")
            mark(f"ffnmixv{c}", "VE")

        def passb_ffn_rest(c, x1src):
            svx = mk(f"ffnmixv{c}")
            ggx = mk(f"ffngp{c}")
            for dg in range(2):
                bank = bkB[dg]
                for cc in range(8):
                    MMOP(bank, xrv[:, cc, :],
                         wrv[:, cc, dg * 512:dg * 512 + 512],
                         start=(cc == 0), stop=(cc == 7),
                         w=[svx, mk(f"blnve{c}") if False else mk(f"bln{c}")]
                         if (dg == 0 and cc == 0) else ())
            mmv = snap("MM")
            SOP(r_bf[:, 0:512], bkB[0], AF.Sigmoid,
                w=[mmv, mk(f"vmm{c-1}")])
            SOP(r_bf[:, 512:1024], bkB[1], AF.Sigmoid)
            mark(f"rse{c}", "SE")
            for dg in range(2):
                bank = bkB[2 + dg]
                for cc in range(8):
                    MMOP(bank, xkv[:, cc, :],
                         wkv[:, cc, dg * 512:dg * 512 + 512],
                         start=(cc == 0), stop=(cc == 7),
                         w=[ggx, mk(f"kse{c-1}"), mk(f"kve{c-1}")]
                         if (dg == 0 and cc == 0) else ())
            mmv = snap("MM")
            SOP(krelu[:, 0:512], bkB[2], AF.Relu, w=[mmv])
            SOP(krelu[:, 512:1024], bkB[3], AF.Relu)
            SOP(kact, krelu, AF.Square)
            mark(f"kse{c}", "SE")
            sev = snap("SE")
            PET(kTv, [kact[:, j * 128:(j + 1) * 128] for j in range(8)],
                [bkB[2], bkB[3]], w=[sev, mk(f"vmm{c-1}")])
            mark(f"kpet{c}", "MM")
            mark(f"kve{c}", "VE")
            tpk = snap("VE")
            for dg in range(2):
                bank = bkB[dg]
                for cc in range(8):
                    MMOP(bank, kTv[:, cc, :],
                         wvv[:, cc, dg * 512:dg * 512 + 512],
                         start=(cc == 0), stop=(cc == 7),
                         w=[tpk, mk(f"rse{c}")]
                         if (dg == 0 and cc == 0) else ())
            mark(f"vmm{c}", "MM")
            mmv = snap("MM")
            for dg in range(2):
                VOP("scalar_tensor_tensor", outb[:, dg * 512:dg * 512 + 512],
                    bkB[dg], 1.0, r_bf[:, dg * 512:dg * 512 + 512],
                    ALU.mult, ALU.mult,
                    w=[mmv, mk(f"rse{c}"), mk(f"osto{c-1}")]
                    if dg == 0 else ())
            VOP("tensor_tensor", outb, outb, x1src, ALU.add)
            mark(f"ffnve{c}", "VE")
            mark(f"rve{c}", "VE")
            DMA("STO", out_d[c * 128:(c + 1) * 128, :], outb, w=[snap("VE")])
            mark(f"osto{c}", "STO")
            mark(f"ffnmm{c}", "MM")
            mark(f"bse{c}", "SE")

        def passb_ffn(c, x1src):
            passb_ffn_mix(c, x1src)
            passb_ffn_rest(c, x1src)

        passb_attn_pre(15)
        passb_attn_mm(15)
        VOP("tensor_copy", x1c15, x1b3, w=[mk("attnveL")])
        DMA("BNC", gin_x1, x1b3[127:128, :], w=[snap("VE")])
        E("gpsimd", lambda e: e.collective_compute(
            "AllGather", ALU.bypass, [list(range(8))],
            [gin_x1.opt()], [gout_x1.opt()]),
          w=[snap("BNC"), snap("VE")], i="CC")
        DMA("LDX", sbgx, gout_x1, w=[snap("CC"), mk("q2mmL"),
                                     mk("oppveL")])
        lgx = snap("LDX")
        MMOP(psA[0:1, 1024:1536], selprev, sbgx[:, 0:512],
             w=[lgx, snap("VE"), snap("SE")])
        MMOP(psA[0:1, 1536:2048], selprev, sbgx[:, 512:1024])
        mark("q2mmL", "MM")
        VOP("tensor_copy", x1brow[0:1, 0:512], psA[0:1, 1024:1536],
            w=[snap("MM"), snap("STO")])
        VOP("tensor_copy", x1brow[0:1, 512:1024], psA[0:1, 1536:2048])
        VOP("tensor_reduce", rowb[0:1, 16:17], x1brow, AX.X, ALU.add,
            w=[snap("VE")])
        VOP("scalar_tensor_tensor", t1b[0:1, :], x1brow, 1.0, x1brow,
            ALU.mult, ALU.mult, accum_out=rowb[0:1, 17:18])
        sv = snap("VE")
        VOP("tensor_scalar", rowb[0:1, 18:19], rowb[0:1, 16:17], 1.0 / D,
            None, ALU.mult, w=[sv])
        VOP("tensor_tensor", rowb[0:1, 19:20], rowb[0:1, 18:19],
            rowb[0:1, 18:19], ALU.mult)
        VOP("tensor_scalar", rowb[0:1, 20:21], rowb[0:1, 17:18], 1.0 / D,
            None, ALU.mult)
        VOP("tensor_tensor", rowb[0:1, 20:21], rowb[0:1, 20:21],
            rowb[0:1, 19:20], ALU.subtract)
        VOP("tensor_scalar", rowb[0:1, 20:21], rowb[0:1, 20:21], 1e-5, None,
            ALU.add)
        SOP(rowb[0:1, 20:21], rowb[0:1, 20:21], AF.Ln, w=[snap("VE")])
        SOP(rowb[0:1, 21:22], rowb[0:1, 20:21], AF.Exp, scale=-0.5)
        VOP("tensor_tensor", rowb[0:1, 22:23], rowb[0:1, 18:19],
            rowb[0:1, 21:22], ALU.mult, w=[snap("SE")])
        VOP("tensor_scalar_mul", rowb[0:1, 22:23], rowb[0:1, 22:23], -1.0)
        VOP("tensor_scalar", ln2brow, x1brow, rowb[0:1, 21:22],
            rowb[0:1, 22:23], ALU.mult, ALU.add)
        VOP("tensor_copy", ln2bb, ln2brow)
        DMA("BNC", ln2b_dram, ln2bb, w=[snap("VE")])
        mark("b2bnc", "BNC")

        passb_attn_pre(0)
        for c in range(15):
            passb_attn_mm(c)
            passb_ln2(c, x1bv[c % 2])
            passb_ffn_mix(c, x1bv[c % 2])
            if c < 14:
                passb_attn_pre_a(c + 1)
            passb_ffn_rest(c, x1bv[c % 2])
            if c < 14:
                passb_attn_pre_b(c + 1)
        passb_ln2(15, x1c15)
        passb_ffn(15, x1c15)

        E("sync", lambda e: e.nop(), w=[snap("STO"), snap("BNC"),
                                        snap("VE"), snap("MM"), snap("SE"),
                                        snap("GP")])

        block = stack.enter_context(nc.Block())
        SELFW = {"vector": "VE", "scalar": "SE", "gpsimd": "GP"}

        def replay(name):
            own = SELFW.get(name)

            def go(eng):
                done = 0
                for wl, fn, pend in SCR[name]:
                    if fn is None:
                        for (n, v) in wl:
                            eng.wait_ge(semh[n], v)
                        continue
                    if own is not None and done > 0:
                        eng.wait_ge(semh[own], done)
                    for (n, v) in wl:
                        if v > 0:
                            eng.wait_ge(semh[n], v)
                    inst = fn(eng)
                    if pend is not None:
                        inst.then_inc(semh[pend[0]], pend[1])
                        if own is not None and pend[0] == own:
                            done += pend[1]
            return go

        block.sync(replay("sync"))
        block.tensor(replay("tensor"))
        block.vector(replay("vector"))
        block.scalar(replay("scalar"))
        block.gpsimd(replay("gpsimd"))

    return nc


_NC_CACHE = [None]


def _host_inputs(inputs):
    f = np.float32
    x = np.ascontiguousarray(
        inputs["L0_attn_hidden_states"].reshape(16384, D)).astype(f)
    conv_w = np.asarray(inputs["L0_attn_conv_w"], f)
    conv_b = np.asarray(inputs["L0_attn_conv_b"], f)
    cdiag = np.zeros((4, 18, 128, 128), f)
    idx = np.arange(128)
    for k in range(4):
        for j in range(18):
            cdiag[k, j, idx, idx] = conv_w[j * 128:(j + 1) * 128, k]
    sel8 = np.zeros((8, 1024), f)
    for g in range(8):
        sel8[g, g * 128:(g + 1) * 128] = 1.0
    base = {
        "wint": np.ascontiguousarray(np.asarray(inputs["L0_attn_in_proj_w"], f).T),
        "woutt": np.ascontiguousarray(np.asarray(inputs["L0_attn_out_proj_w"], f).T),
        "wrt": np.ascontiguousarray(np.asarray(inputs["L0_ffn_W_r"], f).T),
        "wkt": np.ascontiguousarray(np.asarray(inputs["L0_ffn_W_k"], f).T),
        "wvt": np.ascontiguousarray(np.asarray(inputs["L0_ffn_W_v"], f).T),
        "cdiag": cdiag,
        "cbias_fm": np.ascontiguousarray(conv_b.reshape(18, 128).T),
        "nwrep": np.ascontiguousarray(
            np.broadcast_to(np.asarray(inputs["L0_attn_norm_w"], f), (128, D))),
        "onwrep": np.ascontiguousarray(
            np.broadcast_to(np.asarray(inputs["L0_attn_out_norm_w"], f),
                            (128, DI))),
        "murep": np.ascontiguousarray(
            np.asarray(inputs["L0_ffn_mu_r"], f).reshape(8, 128).T),
        "mukrep": np.ascontiguousarray(
            np.asarray(inputs["L0_ffn_mu_k"], f).reshape(8, 128).T),
        "dtb": np.tile(np.asarray(inputs["L0_attn_dt_bias"],
                                  f).reshape(32, 1), (2, 1)),
        "alog": np.tile(np.asarray(inputs["L0_attn_A_log"],
                                   f).reshape(32, 1), (2, 1)),
        "dprep": np.ascontiguousarray(
            np.broadcast_to(np.asarray(inputs["L0_attn_Dp"], f), (128, 32))),
        "ones1": np.ones((1, 128), f),
        "onescol": np.ones((128, 1), f),
        "i128": np.eye(128, dtype=f),
        "u01": np.triu(np.ones((128, 128), f)),
        "sel8": sel8,
    }
    in_maps = []
    for c in range(NCORES):
        m = dict(base)
        xt = np.zeros((NT + 1, 128, D), f)
        s = c * T
        if c == 0:
            xt.reshape(-1, D)[128:] = x[0:T]
        else:
            xt.reshape(-1, D)[:] = x[s - 128:s + T]
        m["x0"] = xt
        sel = np.zeros((8, 1), f)
        if c > 0:
            sel[c - 1, 0] = 1.0
        m["selprev"] = sel
        bt = np.zeros((8, 8), f)
        for k in range(8):
            for j in range(8):
                if j < k < c:
                    bt[k, j] = 1.0
        m["between"] = bt
        mj = np.zeros((8, 1), f)
        mj[:c, 0] = 1.0
        m["maskj"] = mj
        in_maps.append(m)
    return in_maps


def kernel(**inputs):
    if _NC_CACHE[0] is None:
        _NC_CACHE[0] = build_program()
    nc = _NC_CACHE[0]
    in_maps = _host_inputs(inputs)
    res = run_bass_kernel_spmd(nc, in_maps, core_ids=list(range(NCORES)))
    out = np.concatenate([np.asarray(r["out"]) for r in res.results], axis=0)
    return out.reshape(1, 16384, D).astype(np.float32)


# version 31
# speedup vs baseline: 1.0087x; 1.0046x over previous
import os
from contextlib import ExitStack

import numpy as np
import ml_dtypes

import concourse.bass as bass
import concourse.mybir as mybir
from concourse.bass_utils import run_bass_kernel_spmd

F32 = mybir.dt.float32
BF16 = mybir.dt.bfloat16
AF = mybir.ActivationFunctionType
ALU = mybir.AluOpType
AX = mybir.AxisListType
BFNP = ml_dtypes.bfloat16

NCORES = 8
T = 2048
NT = 16
D = 1024
DI = 2048
DIP = 4384
NH = 32
MAGIC = float(np.float32(1.5 * 2 ** 23))

RACE = os.environ.get("KMK_RACE", "0") == "1"
ENGS = ("sync", "tensor", "vector", "scalar", "gpsimd")


def bcast2(ap2d, reps):
    return ap2d.unsqueeze(2).broadcast_to([ap2d.shape[0], ap2d.shape[1], reps])


def build_program():
    nc = bass.Bass("TRN2", target_bir_lowering=False,
                   detect_race_conditions=RACE)

    def din(name, shape, dt=F32):
        return nc.dram_tensor(name, list(shape), dt, kind="ExternalInput").ap()

    x0_d = din("x0", (NT + 1, 128, D))
    wint_d = din("wint", (D, DIP))
    woutt_d = din("woutt", (DI, D))
    wrt_d = din("wrt", (D, D))
    wkt_d = din("wkt", (D, D))
    wvt_d = din("wvt", (D, D))
    cdiag_d = din("cdiag", (4, 18, 128, 128))
    cbias_fm_d = din("cbias_fm", (128, 18))
    nwrep_d = din("nwrep", (128, D))
    onwrep_d = din("onwrep", (128, DI))
    murep_d = din("murep", (128, 8))
    mukrep_d = din("mukrep", (128, 8))
    dtb_d = din("dtb", (64, 1))
    alog_d = din("alog", (64, 1))
    dprep_d = din("dprep", (128, 32))
    ones1_d = din("ones1", (1, 128))
    onescol_d = din("onescol", (128, 1))
    i128_d = din("i128", (128, 128))
    u01_d = din("u01", (128, 128))
    sel8_d = din("sel8", (8, 1024))
    selprev_d = din("selprev", (8, 1))
    between_d = din("between", (8, 8))
    maskj_d = din("maskj", (8, 1))

    out_d = nc.dram_tensor("out", [T, D], F32, kind="ExternalOutput").ap()

    yp_dram = nc.dram_tensor("yp_dram", [NT, 128, DI], F32).ap()
    sz_dram = nc.dram_tensor("sz_dram", [NT, 128, DI], BF16).ap()
    c_dram = nc.dram_tensor("c_dram", [NT, 128, 128], BF16).ap()
    acs_dram = nc.dram_tensor("acs_dram", [2, 32, 128], F32).ap()
    w8_dram = nc.dram_tensor("w8_dram", [8, 32], F32).ap()
    ln2b_dram = nc.dram_tensor("ln2b_dram", [1, D], BF16).ap()
    gin_S = nc.dram_tensor("gin_S", [128, DI], F32).ap()
    gout_S = nc.dram_tensor("gout_S", [8 * 128, DI], F32).ap()
    gin_as = nc.dram_tensor("gin_as", [1, 32], F32).ap()
    gout_as = nc.dram_tensor("gout_as", [8, 32], F32).ap()
    gin_x1 = nc.dram_tensor("gin_x1", [1, D], F32).ap()
    gout_x1 = nc.dram_tensor("gout_x1", [8, D], F32).ap()

    stack = ExitStack()
    with stack:
        AR = 53000
        arena_t = stack.enter_context(nc.sbuf_tensor("arena", [128, AR], F32))
        arena = arena_t.ap()
        off = [0]

        def af(cols, parts=128, align=16):
            off[0] = (off[0] + align - 1) // align * align
            o = off[0]
            off[0] += cols
            assert off[0] <= AR, f"arena overflow {off[0]}"
            return arena[0:parts, o:o + cols]

        def abf(cols_bf, parts=128, align=16):
            assert cols_bf % 2 == 0
            off[0] = (off[0] + align - 1) // align * align
            o = off[0]
            off[0] += cols_bf // 2
            assert off[0] <= AR, f"arena overflow {off[0]}"
            return arena[0:128, o:o + cols_bf // 2].bitcast(BF16)[0:parts, :]

        cb_fm = af(18)
        nwrep = af(D)
        onwrep = af(DI)
        murep = af(8)
        mukrep = af(8)
        dprep = af(32)
        dtb = af(1, 64)
        alogc = af(1, 64)
        nega = af(1, 64)
        ones1 = af(128, 1)
        onescol = af(1)
        u01b = abf(128)
        sel8b = af(1024, 8)
        mu1_st = af(NT + 1)
        r1_st = af(NT + 1)
        g_st = af(NT * 32)
        run_row = af(32, 1)
        rowb = af(32)
        rowdec = af(32, 1)
        wpp = af(32)
        u01f = af(128)
        i128 = af(128)
        ln2h15 = abf(16, 128)
        x0t = af(2 * D)
        selprev = af(1, 8)
        between = af(8, 8)
        maskj = af(1, 8)

        wbase = off[0]
        w_in_bf = abf(8 * DIP)
        wq_v = w_in_bf.rearrange("p (c d) -> p c d", c=8)
        cdiag_bf = abf(4 * 18 * 128)
        cdv = cdiag_bf.rearrange("p (k j t) -> p k j t", k=4, j=18)
        passA_wend = off[0]
        off[0] = wbase
        woutt_bf = abf(16 * D)
        wo_v = woutt_bf.rearrange("p (c d) -> p c d", c=16)
        wr_bf = abf(8 * D)
        wrv = wr_bf.rearrange("p (c d) -> p c d", c=8)
        wk_bf = abf(8 * D)
        wkv = wk_bf.rearrange("p (c d) -> p c d", c=8)
        wv_bf = abf(8 * D)
        wvv = wv_bf.rearrange("p (c d) -> p c d", c=8)
        off[0] = max(off[0], passA_wend)

        tbase = off[0]
        ppA = af(D)
        ppB = af(D)
        qblock = af(1536)
        q1 = qblock[:, 0:512].bitcast(BF16)
        q1T2 = [qblock[:, 512:1024].bitcast(BF16),
                qblock[:, 1024:1536].bitcast(BF16)]
        q1Tv2 = [t.rearrange("p (j t) -> p j t", j=8) for t in q1T2]
        szt = abf(DI)
        xdp = szt
        dt_tm2 = af(128)
        xbcg = abf(2 * 2304)
        xbv = xbcg.rearrange("p (b c) -> p b c", b=2)
        convin = abf(18 * 272)
        cvv = convin.rearrange("p (j t) -> p j t", j=18)
        xsil = abf(16 * 256)
        xsv = xsil.rearrange("p (j t) -> p j t", j=16)
        bcfm = abf(2 * 256)
        bcv = bcfm.rearrange("p (j t) -> p j t", j=2)
        x_tmP = [abf(DI), abf(DI)]
        xtvP = [t.rearrange("p (j t) -> p j t", j=16) for t in x_tmP]
        u_t = abf(DI)
        m_t = abf(2 * 512)
        m_v = m_t.rearrange("p (b d) -> p b d", b=2)
        pm_bfP = [abf(128), abf(128)]
        b_tmP = [abf(128), abf(128)]
        y_p = af(DI)
        acs8P = [af(512, 8), af(512, 8)]
        asum_row = af(32, 1)
        dtx_fm = af(128, 64)
        sp1_fm = af(128, 64)
        sp2_fm = af(128, 64)
        dta_fm = af(128, 64)
        a_fm = sp1_fm
        acs_fm = dtx_fm
        acs_tm = af(32)
        nacs_tmP = [af(32), af(32)]
        dta_tmP = [af(32), af(32)]
        gtmp = af(32)
        eacsP = [af(32), af(32)]
        ds_tmP = [af(32), af(32)]
        dec_repP = [af(32), af(32)]
        ones_a = af(128, 64)
        pstate = af(DI)
        pstate_bf = abf(DI)
        passA_tend = off[0]
        off[0] = tbase
        ypB = af(DI)
        stmp = ypB
        tcorr = af(DI)
        szB = abf(DI)
        cfmB = abf(128)
        qb2 = af(4096)
        q2 = qb2[:, 0:1024].bitcast(BF16)
        q2T = qb2[:, 1024:2048].bitcast(BF16)
        q2Tv = q2T.rearrange("p (j t) -> p j t", j=16)
        x1b2 = qb2[:, 2048:4096]
        x1bv = [x1b2[:, 0:1024], x1b2[:, 1024:2048]]
        x1b3 = af(D)
        t1b = af(D)
        sjbuf = qb2[:, 0:4096]
        sjv = sjbuf.rearrange("p (b d) -> p b d", b=2)
        sbgx = qb2[0:8, 0:1024]
        x1brow = qb2[0:1, 1024:2048]
        wflat = qb2[0:1, 3584:3840]
        ln2tm = abf(D)
        ln2fm = abf(8 * 144)
        lfv = ln2fm.rearrange("p (j t) -> p j t", j=8)
        dmix = ln2tm
        dmv = dmix.rearrange("p (j t) -> p j t", j=8)
        xr_fm = abf(8 * 128)
        xrv = xr_fm.rearrange("p (j t) -> p j t", j=8)
        xk_fm = abf(8 * 128)
        xkv = xk_fm.rearrange("p (j t) -> p j t", j=8)
        r_bf = abf(D)
        kact = abf(D)
        kab = af(512)
        sb8_32 = kab[0:8, 0:32]
        sbw8 = kab[0:8, 32:64]
        wrep = kab[:, 256:512]
        kactT = abf(8 * 128)
        kTv = kactT.rearrange("p (j t) -> p j t", j=8)
        outb = af(D)
        sin_bf = abf(DI)
        x1c15 = af(D)
        ln2brow = ppB[0:1, 0:D]
        ln2bb = ppB[0:1, 0:512].bitcast(BF16) if False else None
        krelu = af(D)
        lnsink = af(D)
        off[0] = max(off[0], passA_tend)
        stg0 = arena[:, tbase:tbase + DIP]
        stg1 = arena[:, tbase + DIP:tbase + 2 * DIP]
        stgs = [stg0, stg1]
        partial = arena[:, tbase + 2 * DIP:tbase + 2 * DIP + 32]
        cdiag_f = arena[:, tbase + 2 * DIP + 32:
                        tbase + 2 * DIP + 32 + 2 * 18 * 128]
        cfv = cdiag_f.rearrange("p (k j t) -> p k j t", k=2, j=18)
        ln2bb = tcorr[0:1, 0:512].bitcast(BF16)[0:1, 0:D]

        psA_t = stack.enter_context(nc.psum_tensor("psA", [128, 2048], F32))
        psB_t = stack.enter_context(nc.psum_tensor("psB", [128, 2048], F32))
        psA = psA_t.ap()
        psB = psB_t.ap()
        bkA = [psA[:, i * 512:(i + 1) * 512] for i in range(4)]
        bkB = [psB[:, i * 512:(i + 1) * 512] for i in range(4)]

        semh = {}
        cnt = {}
        for n in ["LDW", "LDX", "BNC", "ACS", "STO", "MM", "VE", "SE", "GP",
                  "CC"]:
            semh[n] = stack.enter_context(nc.semaphore(n))
            cnt[n] = 0
        SCR = {e: [] for e in ENGS}
        DMASEMS = {"LDW", "LDX", "BNC", "ACS", "STO"}
        dma_maxreq = {n: 0 for n in DMASEMS}
        dma_drained = {n: 0 for n in DMASEMS}
        M = {}

        def E(eng, fn, w=(), i=None, ik=1):
            wl = []
            for x in w:
                t = x if isinstance(x, tuple) else (x, cnt[x])
                if t[0] in DMASEMS:
                    t = (t[0], cnt[t[0]])
                    if t[1] > dma_maxreq[t[0]]:
                        dma_maxreq[t[0]] = t[1]
                wl.append(t)
            if i in DMASEMS and dma_maxreq[i] > dma_drained[i]:
                tgt = dma_maxreq[i]
                SCR[eng].append(([(i, tgt)], None, None))
                dma_drained[i] = tgt
            if i is not None:
                cnt[i] += ik
            SCR[eng].append((wl, fn, (i, ik) if i is not None else None))

        def snap(n):
            return (n, cnt[n])

        def mark(key, sem):
            M[key] = (sem, cnt[sem])

        def mk(key):
            return M.get(key, ("VE", 0))

        def DMA(sem, outap, inap, w=(), **kw):
            E("sync", lambda e, o=outap, ii=inap, k=kw:
              e.dma_start(out=o, in_=ii, **k), w=w, i=sem, ik=16)

        def VOP(meth, *args, w=(), **kw):
            E("vector", lambda e, a=args, k=kw: getattr(e, meth)(*a, **k),
              w=w, i="VE")

        def GOP(meth, *args, w=(), **kw):
            E("gpsimd", lambda e, a=args, k=kw: getattr(e, meth)(*a, **k),
              w=w, i="GP")

        def SOP(*args, w=(), **kw):
            E("scalar", lambda e, a=args, k=kw: e.activation(*a, **k),
              w=w, i="SE")

        def MMOP(out, lhsT, rhs, start=True, stop=True, w=()):
            E("tensor", lambda e, o=out, l=lhsT, r=rhs, s1=start, s2=stop:
              e.matmul(o, l, r, start=s1, stop=s2), w=w, i="MM")

        def TROP(out, inp, ident, w=()):
            E("tensor", lambda e, o=out, ii=inp, dd=ident: e.transpose(o, ii, dd),
              w=w, i="MM")

        i128b = u01f.bitcast(BF16)[:, 0:128]

        def PET(dst3, srcs, banks, w=(), evac_w=()):
            n = len(srcs)
            i = 0
            r = 0
            firstw = list(w)
            evs = []
            while i < n:
                m = min(8, n - i)
                bank = banks[r % len(banks)].bitcast(BF16)
                b3 = bank.rearrange("p (k t) -> p k t", k=8)
                roundw = list(firstw) + [snap("VE")]
                firstw = []
                for k in range(m):
                    E("tensor",
                      lambda e, ss=srcs[i + k], bb=bank, kk=k:
                      e.transpose(bb[:, kk * 128:(kk + 1) * 128], ss, i128b),
                      w=roundw if k == 0 else (), i="MM")
                mmv = snap("MM")
                VOP("tensor_copy", dst3[:, i:i + m, :], b3[:, 0:m, :],
                    w=[mmv] + list(evac_w))
                evs.append(snap("VE"))
                i += m
                r += 1

        def rsqrt_rows(dst, src, eps, w=()):
            VOP("tensor_scalar", src, src, eps, None, ALU.add, w=w)
            SOP(src, src, AF.Ln, w=[snap("VE")])
            SOP(dst, src, AF.Exp, scale=-0.5)

        def recip_rows(dst, src, w=()):
            SOP(src, src, AF.Ln, w=w)
            SOP(dst, src, AF.Exp, scale=-1.0)

        for dst, src in [(cb_fm, cbias_fm_d), (nwrep, nwrep_d),
                         (onwrep, onwrep_d), (murep, murep_d),
                         (mukrep, mukrep_d), (dprep, dprep_d), (dtb, dtb_d),
                         (alogc, alog_d), (ones1, ones1_d),
                         (onescol, onescol_d), (i128, i128_d), (u01f, u01_d),
                         (sel8b, sel8_d), (selprev, selprev_d),
                         (between, between_d), (maskj, maskj_d)]:
            DMA("LDW", dst, src)
        ld0 = snap("LDW")
        VOP("tensor_copy", u01b[:], u01f[:], w=[ld0])
        VOP("tensor_copy", i128b, i128, w=[ld0])
        VOP("memset", ones_a[:], 1.0)
        VOP("memset", run_row[:], 0.0)
        VOP("memset", pstate[:], 0.0)
        VOP("memset", pstate_bf[:], 0.0)
        SOP(nega[:], alogc[:], AF.Exp, w=[ld0])
        VOP("tensor_scalar_mul", nega[:], nega[:], -1.0, w=[snap("SE")])
        for half in range(2):
            DMA("LDW", cfv,
                cdiag_d.rearrange("k j a t -> a k j t")[:, 2 * half:2 * half + 2],
                w=[snap("VE")])
            VOP("tensor_copy", cdv[:, 2 * half:2 * half + 2], cfv,
                w=[snap("LDW")])
        wiv = wint_d.rearrange("(ch cl) d -> cl ch d", cl=128)
        for cc in range(8):
            DMA("LDW", stgs[cc % 2], wiv[:, cc, :], w=[snap("VE")])
            VOP("tensor_reduce", partial[:, cc:cc + 1], stgs[cc % 2][:, :],
                AX.X, ALU.add, apply_absolute_value=True, w=[snap("LDW")])
        VOP("tensor_reduce", partial[:, 8:9], partial[:, 0:8], AX.X, ALU.add,
            w=[snap("VE")])
        MMOP(psA[0:1, 0:1], onescol, partial[:, 8:9], w=[snap("VE")])
        VOP("tensor_scalar", rowb[0:1, 0:1], psA[0:1, 0:1], 1.0 / (DIP * D),
            1e-5, ALU.mult, ALU.max, w=[snap("MM")])
        recip_rows(rowb[0:1, 1:2], rowb[0:1, 0:1], w=[snap("VE")])
        MMOP(psA[:, 8:9], ones1, rowb[0:1, 1:2], w=[snap("SE")])
        VOP("tensor_copy", rowb[:, 2:3], psA[:, 8:9], w=[snap("MM")])
        for cc in range(8):
            DMA("LDW", stgs[cc % 2], wiv[:, cc, :], w=[snap("VE")])
            VOP("tensor_scalar", stgs[cc % 2][:, :], stgs[cc % 2][:, :],
                rowb[:, 2:3], MAGIC, ALU.mult, ALU.add, w=[snap("LDW")])
            VOP("tensor_scalar", stgs[cc % 2][:, :], stgs[cc % 2][:, :],
                MAGIC - 1.0, MAGIC, ALU.max, ALU.subtract)
            VOP("tensor_scalar", wq_v[:, cc, :], stgs[cc % 2][:, :],
                1.0, None, ALU.min)
        mark("wprep", "VE")

        ipsevs = {}

        def prepipe(tau):
            c = tau
            slot = (tau + 1) % 2
            par = (tau + 1) % 2
            rb = rowb[:, par * 16:par * 16 + 16]
            xt = x0t.rearrange("p (b d) -> p b d", b=2)[:, slot, :]
            DMA("LDX", xt, x0_d[tau + 1],
                w=[mk(f"ppve{c-2}"), mk(f"ppse{c-2}")])
            lx = snap("LDX")
            st = tau + 1
            VOP("tensor_reduce", rb[:, 0:1], xt, AX.X, ALU.add,
                w=[lx, mk(f"ppse{c-1}"), mk(f"ppgp{c-1}"),
                   mk(f"ipse{c-2}")])
            VOP("scalar_tensor_tensor", ppA[:, :], xt, 1.0, xt,
                ALU.mult, ALU.mult, accum_out=rb[:, 1:2])
            sv = snap("VE")
            VOP("tensor_scalar", mu1_st[:, st:st + 1], rb[:, 0:1], 1.0 / D,
                None, ALU.mult, w=[sv])
            VOP("tensor_tensor", rb[:, 2:3], mu1_st[:, st:st + 1],
                mu1_st[:, st:st + 1], ALU.mult)
            VOP("tensor_scalar", rb[:, 3:4], rb[:, 1:2], 1.0 / D, None,
                ALU.mult)
            VOP("tensor_tensor", rb[:, 3:4], rb[:, 3:4], rb[:, 2:3],
                ALU.subtract)
            sv = snap("VE")
            VOP("tensor_scalar", rb[:, 4:5], rb[:, 3:4], 1e-5, None, ALU.add)
            SOP(rb[:, 4:5], rb[:, 4:5], AF.Ln, w=[snap("VE")])
            SOP(r1_st[:, st:st + 1], rb[:, 4:5], AF.Exp, scale=-0.5)
            se = snap("SE")
            VOP("tensor_tensor", rb[:, 5:6], r1_st[:, st:st + 1],
                r1_st[:, st:st + 1], ALU.mult, w=[se])
            VOP("tensor_tensor", rb[:, 5:6], rb[:, 5:6], rb[:, 3:4], ALU.mult)
            VOP("tensor_scalar", rb[:, 5:6], rb[:, 5:6], 1e-6, None, ALU.add)
            SOP(rb[:, 5:6], rb[:, 5:6], AF.Ln, w=[snap("VE")])
            SOP(rb[:, 6:7], rb[:, 5:6], AF.Exp, scale=-0.5)
            VOP("tensor_tensor", rb[:, 7:8], r1_st[:, st:st + 1], rb[:, 6:7],
                ALU.mult, w=[snap("SE")])
            VOP("tensor_tensor", rb[:, 8:9], mu1_st[:, st:st + 1], rb[:, 7:8],
                ALU.mult)
            VOP("tensor_scalar_mul", rb[:, 8:9], rb[:, 8:9], -1.0)
            SOP(ppA[:, :], xt, AF.Identity, bias=rb[:, 8:9], scale=rb[:, 7:8],
                w=[snap("VE")])
            GOP("tensor_tensor", ppB[:, :], ppA[:, :], nwrep, ALU.mult,
                w=[snap("SE")])
            gv = snap("GP")
            mark(f"ppgp{c}", "GP")
            VOP("tensor_reduce", rb[:, 0:1], ppB[:, :], AX.X, ALU.add, w=[gv])
            VOP("scalar_tensor_tensor", ppA[:, :], ppB[:, :], 1.0, ppB[:, :],
                ALU.mult, ALU.mult, accum_out=rb[:, 1:2])
            sv = snap("VE")
            VOP("tensor_scalar", rb[:, 2:3], rb[:, 0:1], 1.0 / D, None,
                ALU.mult, w=[sv])
            VOP("tensor_tensor", rb[:, 3:4], rb[:, 2:3], rb[:, 2:3], ALU.mult)
            VOP("tensor_scalar", rb[:, 4:5], rb[:, 1:2], 1.0 / D, None,
                ALU.mult)
            VOP("tensor_tensor", rb[:, 4:5], rb[:, 4:5], rb[:, 3:4],
                ALU.subtract)
            VOP("tensor_scalar", rb[:, 4:5], rb[:, 4:5], 1e-5, None, ALU.add)
            SOP(rb[:, 4:5], rb[:, 4:5], AF.Ln, w=[snap("VE")])
            SOP(rb[:, 5:6], rb[:, 4:5], AF.Exp, scale=-0.5)
            VOP("tensor_tensor", rb[:, 9:10], rb[:, 2:3], rb[:, 5:6],
                ALU.mult, w=[snap("SE")])
            VOP("tensor_scalar_mul", rb[:, 9:10], rb[:, 9:10], -1.0)
            SOP(ppA[:, :], ppB[:, :], AF.Identity, bias=rb[:, 9:10],
                scale=rb[:, 5:6], w=[snap("VE")])
            VOP("tensor_reduce", rb[:, 10:11], ppA[:, :], AX.X, ALU.max,
                apply_absolute_value=True, w=[snap("SE")])
            sv = snap("VE")
            VOP("tensor_scalar", rb[:, 10:11], rb[:, 10:11], 1e-5, None,
                ALU.max, w=[sv])
            VOP("tensor_scalar", rb[:, 13:14], rb[:, 10:11], 1.0 / 127.0,
                None, ALU.mult)
            VOP("tensor_copy", rb[:, 11:12], rb[:, 10:11])
            SOP(rb[:, 11:12], rb[:, 11:12], AF.Ln, w=[snap("VE")])
            SOP(rb[:, 12:13], rb[:, 11:12], AF.Exp, scale=-1.0)
            VOP("tensor_scalar", rb[:, 12:13], rb[:, 12:13], 127.0, None,
                ALU.mult, w=[snap("SE")])
            VOP("tensor_scalar", ppB[:, :], ppA[:, :], rb[:, 12:13], MAGIC,
                ALU.mult, ALU.add, w=[mk(f"q1mm{c-1}")])
            VOP("tensor_scalar", ppB[:, :], ppB[:, :], MAGIC - 128.0, MAGIC,
                ALU.max, ALU.subtract)
            VOP("tensor_scalar", q1[:, :], ppB[:, :], 127.0, None, ALU.min)
            mark(f"ppve{c}", "VE")
            mark(f"ppse{c}", "SE")
            qv = snap("VE")
            PET(q1Tv2[(tau + 16) % 2],
                [q1[:, j * 128:(j + 1) * 128] for j in range(8)],
                [bkA[0]], w=[qv, mk(f"ipse{c-1}"), mk(f"q1mm{c-2}")])
            mark(f"q1mm{c}", "MM")
            mark(f"q1ve{c}", "VE")

        def in_proj_dg(tau, dg):
            c = tau
            par = (tau + 1) % 2
            rb = rowb[:, par * 16:par * 16 + 16]
            recip = rb[:, 13:14]
            tpq = mk(f"q1ve{c}")
            q1Tv = q1Tv2[(tau + 16) % 2]
            xv = xbv[:, tau % 2, :]
            g = tau // 2
            sevs = ipsevs[tau]
            d0 = dg * 512
            dw = min(512, DIP - d0)
            bank = bkA[dg % 2][:, 0:dw]
            if dg < 2:
                wmm = [tpq, mk(f"ipse{c-1}")]
            else:
                wmm = [tpq, sevs[dg - 2]]
            for ccx in range(8):
                MMOP(bank, q1Tv[:, ccx, :], wq_v[:, ccx, d0:d0 + dw],
                     start=(ccx == 0), stop=(ccx == 7),
                     w=wmm if ccx == 0 else ())
            mmv = snap("MM")
            if dg < 4:
                SOP(szt[:, d0:d0 + 512], bank, AF.Identity, bias=0.0,
                    scale=recip,
                    w=[mmv, mk(f"sz{c-1}"),
                       mk(f"xdpve{((c - 4) // 2) * 2 - 1}")]
                    if dg == 0 else [mmv])
                if dg == 3:
                    SOP(szt[:, :], szt[:, :], AF.Silu)
            elif dg < 8:
                SOP(xv[:, d0 - DI:d0 - DI + 512], bank, AF.Identity,
                    bias=0.0, scale=recip,
                    w=[mmv, mk(f"cvpet{g-1}")] if dg == 4 else [mmv])
            else:
                SOP(xv[:, 2048:2304], bank[:, 0:256], AF.Identity,
                    bias=0.0, scale=recip, w=[mmv])
                SOP(dt_tm2[:, (tau % 4) * 32:(tau % 4) * 32 + 32],
                    bank[:, 256:288], AF.Identity, bias=0.0, scale=recip,
                    w=[mk(f"dtmm{c-4}")])
            sevs.append(snap("SE"))
            if dg == 3:
                DMA("BNC", sz_dram[tau], szt[:, :], w=[snap("SE")])
                mark(f"sz{c}", "BNC")
            if dg == 8:
                mark(f"ipse{c}", "SE")

        def ip_fillers(tau):
            return [(lambda dg=dg: in_proj_dg(tau, dg)) for dg in range(4, 9)]

        def in_proj(tau, part=None):
            c = tau
            par = (tau + 1) % 2
            rb = rowb[:, par * 16:par * 16 + 16]
            recip = rb[:, 13:14]
            tpq = mk(f"q1ve{c}")
            q1Tv = q1Tv2[(tau + 16) % 2]
            if tau == -1:
                for j in range(18):
                    bank = bkA[j % 2]
                    wmm = [tpq] if j < 2 else [tpq, snap("SE")]
                    for ccx in range(8):
                        MMOP(bank[:, 0:128],
                             wq_v[:, ccx, DI + j * 128:DI + (j + 1) * 128],
                             q1Tv[:, ccx, :], start=(ccx == 0), stop=(ccx == 7),
                             w=wmm if ccx == 0 else ())
                    SOP(cvv[:, j, 13:16], bank[:, 125:128], AF.Identity,
                        bias=0.0, scale=recip, w=[snap("MM")])
                mark("ipse-1", "SE")
                mark("cvse-1", "SE")
                return
            if part in (0, None):
                ipsevs[tau] = []
            dgr = (range(9) if part is None else
                   (range(0, 4) if part == 0 else range(4, 9)))
            for dg in dgr:
                in_proj_dg(tau, dg)

        cvsevs = {}

        def conv_pet(g):
            PET(cvv[:, :, 16:16 + 128],
                [xbv[:, 0, j * 128:(j + 1) * 128] for j in range(18)],
                [bkA[2], bkA[3]],
                w=[mk(f"ipse{2*g}"), mk(f"halo{g-1}"), mk(f"cvse{g-1}"),
                   mk(f"fve{2*g-1}")])
            PET(cvv[:, :, 16 + 128:16 + 256],
                [xbv[:, 1, j * 128:(j + 1) * 128] for j in range(18)],
                [bkA[2], bkA[3]], w=[mk(f"ipse{2*g+1}")])
            mark(f"cvpet{g}", "MM")
            mark(f"cvpve{g}", "VE")

        def conv_j(g, j):
            tpx = mk(f"cvpve{g}")
            sevs = cvsevs[g]
            bank = bkA[2 + (j % 2)][:, 0:256]
            wmm = ([tpx, mk(f"cvse{g-1}"), mk(f"fve{2*g-1}")]
                   if j < 2 else [tpx, sevs[j - 2]])
            for k in range(4):
                MMOP(bank, cdv[:, k, j, :], cvv[:, j, 13 + k:13 + k + 256],
                     start=(k == 0), stop=(k == 3),
                     w=wmm if k == 0 else ())
            mmv = snap("MM")
            if j < 16:
                SOP(xsv[:, j, :], bank, AF.Identity, bias=cb_fm[:, j:j + 1],
                    scale=1.0,
                    w=[mmv, mk(f"fmm{2*g-1}"), mk(f"cbnc{2*g-1}"),
                       mk(f"rtmm{2*g-1}")]
                    if j == 0 else [mmv])
            else:
                SOP(bcv[:, j - 16, :], bank, AF.Identity,
                    bias=cb_fm[:, j:j + 1], scale=1.0, w=[mmv])
            sevs.append(snap("SE"))
            if j == 17:
                SOP(xsil[:, :], xsil[:, :], AF.Silu)
                SOP(bcfm[:, :], bcfm[:, :], AF.Silu)
                mark(f"cvse{g}", "SE")
                mark(f"cvmm{g}", "MM")
                GOP("tensor_copy", cvv[:, :, 13:16], cvv[:, :, 269:272],
                    w=[snap("SE"), mk(f"cvmm{g}")])
                mark(f"halo{g}", "GP")

        def cv_fillers(g):
            cvsevs[g] = []
            out = []
            for j0 in range(0, 18, 3):
                out.append(lambda j0=j0: [conv_j(g, j)
                                          for j in range(j0, j0 + 3)])
            return out

        def conv_mm(g, part=None):
            if part in (0, None):
                cvsevs[g] = []
            jr = (range(18) if part is None else
                  (range(0, 9) if part == 0 else range(9, 18)))
            for j in jr:
                conv_j(g, j)

        def ssd_front1(c):
            par = c % 2
            g = c // 2
            cse = mk(f"cvse{g}")
            if par == 0:
                pbanks = [bkB[2]]
                pw = [cse, mk(f"ydve{c-1}")]
            else:
                pbanks = [bkA[2], bkA[3]]
                pw = [cse, mk(f"cvpve{g+1}")]
            PET(xtvP[par],
                [xsv[:, j, par * 128:(par + 1) * 128] for j in range(16)],
                pbanks, w=pw, evac_w=[mk(f"xdpgp{c-2}")])
            PET(b_tmP[par].rearrange("p (a t) -> p a t", a=1),
                [bcv[:, 0, par * 128:(par + 1) * 128]],
                pbanks, w=[mk(f"smm{c-1}")], evac_w=[mk(f"smm{c-2}")])
            DMA("BNC", c_dram[c], bcv[:, 1, par * 128:(par + 1) * 128],
                w=[cse])
            mark(f"cbnc{c}", "BNC")
            if par == 0:
                TROP(psB[0:64, 0:128],
                     dt_tm2[:, (c % 4) * 32:(c % 4) * 32 + 64],
                     i128, w=[mk(f"ipse{c}"), mk(f"ipse{c+1}"),
                              mk(f"b0ve{c-1}")])
                mark(f"dtmm{c}", "MM")
                M[f"dtmm{c+1}"] = M[f"dtmm{c}"]
                VOP("tensor_copy", dtx_fm, psB[0:64, 0:128],
                    w=[snap("MM"), mk(f"acs{c-1}"), mk(f"fmm{c-1}")])
                SOP(dtx_fm, dtx_fm, AF.Identity, bias=dtb, scale=1.0,
                    w=[snap("VE")])
                se = snap("SE")
                VOP("tensor_scalar_mul", sp1_fm, dtx_fm, -1.0, w=[se])
                VOP("tensor_tensor", sp1_fm, dtx_fm, sp1_fm, ALU.max)
                SOP(sp2_fm, sp1_fm, AF.Exp, bias=0.0, scale=-1.0,
                    w=[snap("VE")])
                VOP("tensor_scalar", sp2_fm, sp2_fm, 1.0, None, ALU.add,
                    w=[snap("SE")])
                SOP(sp2_fm, sp2_fm, AF.Ln, bias=0.0, scale=1.0,
                    w=[snap("VE")])
                se = snap("SE")
                VOP("tensor_scalar", dta_fm, dtx_fm, 0.0, None, ALU.max,
                    w=[se, mk(f"fmm{c-1}")])
                VOP("tensor_tensor", dta_fm, dta_fm, sp2_fm, ALU.add)
                VOP("tensor_scalar", a_fm, dta_fm, nega, None, ALU.mult)
                VOP("tensor_tensor_scan", acs_fm, ones_a, a_fm, 0.0,
                    ALU.mult, ALU.add, w=[snap("VE")])
                mark(f"scan{c}", "VE")
                M[f"scan{c+1}"] = M[f"scan{c}"]
            sva = mk(f"scan{c}")
            DMA("ACS", acs_dram[par],
                acs_fm[par * 32:(par + 1) * 32, :],
                w=[sva, mk(f"acs{c-2}")])
            DMA("ACS", acs8P[par],
                acs_dram[par].rearrange("h t -> (h t)").rearrange(
                    "(g x) -> g x", g=8),
                w=[snap("ACS"), mk(f"rtmm{c-2}")])
            DMA("ACS", asum_row,
                acs_dram[par].rearrange("h t -> t h")[127:128, :],
                w=[mk(f"fmm{c-1}"), mk(f"fse{c-1}")],
                allow_slow_non_contiguous=True)
            mark(f"acs8{c}", "ACS")
            mark(f"acs{c}", "ACS")

        def ssd_front2(c):
            par = c % 2
            g = c // 2
            cse = mk(f"cvse{g}")
            sva = mk(f"scan{c}")
            acsf = mk(f"acs8{c}")
            idn = i128[par * 32:(par + 1) * 32, par * 32:(par + 1) * 32]
            TROP(psB[0:128, 128:160], acs_fm[par * 32:(par + 1) * 32, :],
                 idn, w=[sva])
            TROP(psB[0:128, 160:192], dta_fm[par * 32:(par + 1) * 32, :],
                 idn)
            mmv = snap("MM")
            VOP("tensor_copy", acs_tm, psB[:, 128:160],
                w=[mmv, mk(f"fse{c-1}")])
            VOP("tensor_scalar_mul", nacs_tmP[par], acs_tm, -1.0,
                w=[mk(f"rtse{c-2}")])
            VOP("tensor_copy", dta_tmP[par], psB[:, 160:192],
                w=[mk(f"umgp{c-2}")])
            svt = snap("VE")
            MMOP(psB[:, 192:224], ones1, run_row[0:1, 0:32], w=[svt])
            VOP("tensor_tensor", gtmp, acs_tm, psB[:, 192:224], ALU.add,
                w=[snap("MM"), mk(f"fse{c-1}")])
            SOP(g_st[:, 32 * c:32 * c + 32], gtmp, AF.Exp, w=[snap("VE")])
            SOP(eacsP[par], acs_tm, AF.Exp, w=[mk(f"yove{c-2}")])
            MMOP(psB[:, 224:256], ones1, asum_row[0:1, 0:32],
                 w=[acsf, snap("SE")])
            VOP("tensor_tensor", gtmp, psB[:, 224:256], acs_tm, ALU.subtract,
                w=[snap("MM"), snap("SE")])
            SOP(ds_tmP[par], gtmp, AF.Exp,
                w=[snap("VE"), mk(f"uvgp{c-2}")])
            SOP(rowdec, asum_row, AF.Exp, w=[acsf, mk(f"fmm{c-1}")])
            MMOP(psB[:, 256:288], ones1, rowdec, w=[snap("SE")])
            VOP("tensor_copy", dec_repP[par], psB[:, 256:288],
                w=[snap("MM"), mk(f"psgp{c-2}")])
            VOP("tensor_tensor", run_row, run_row, asum_row, ALU.add,
                w=[snap("MM"), acsf])
            MMOP(psB[:, 288:416], bcv[:, 0, par * 128:(par + 1) * 128],
                 bcv[:, 1, par * 128:(par + 1) * 128],
                 w=[cse, snap("VE")])
            VOP("tensor_tensor", pm_bfP[par], psB[:, 288:416], u01b, ALU.mult,
                w=[snap("MM")])
            mark(f"b0ve{c}", "VE")
            mark(f"fve{c}", "VE")
            mark(f"fmm{c}", "MM")
            mark(f"fse{c}", "SE")

        def ssd_back1(c):
            par = c % 2
            g = c // 2
            yoev = []
            for q in range(4):
                bank = bkB[1 if q % 2 == 0 else 3]
                if q == 0:
                    wmm = [mk(f"cvse{g}"), mk(f"sve{c-1}"), mk(f"rtse{c-1}"),
                           mk(f"pbfve{c-1}"), mk("wprep")]
                elif q >= 2:
                    wmm = [yoev[q - 2]]
                else:
                    wmm = ()
                MMOP(bank, bcv[:, 1, par * 128:(par + 1) * 128],
                     pstate_bf[:, q * 512:(q + 1) * 512], w=wmm)
                VOP("tensor_tensor",
                    y_p[:, q * 512:(q + 1) * 512].rearrange(
                        "p (h q2) -> p h q2", h=8),
                    bank.rearrange("p (h q2) -> p h q2", h=8),
                    bcast2(eacsP[par][:, q * 8:(q + 1) * 8], 64), ALU.mult,
                    w=[snap("MM"), mk(f"ypbnc{c-1}"), mk(f"fse{c}")]
                    if q == 0 else [snap("MM")])
                yoev.append(snap("VE"))
            mark(f"yove{c}", "VE")
            GOP("tensor_tensor", u_t.rearrange("p (h q) -> p h q", h=32),
                x_tmP[par].rearrange("p (h q) -> p h q", h=32),
                bcast2(dta_tmP[par], 64), ALU.mult,
                w=[mk(f"fve{c}"), mk(f"smm{c-1}")])
            mark(f"umgp{c}", "GP")

        def ssd_back2(c, fillers=()):
            par = c % 2
            fillers = list(fillers)
            acsf = mk(f"acs8{c}")
            tau_last = min(2 * (c // 2) + 5, 15)
            GOP("tensor_tensor", xdp.rearrange("p (h q) -> p h q", h=32),
                x_tmP[par].rearrange("p (h q) -> p h q", h=32),
                bcast2(dprep, 64), ALU.mult,
                w=[mk(f"fve{c}"), mk(f"xdpve{c-1}"), mk(f"sz{tau_last}")])
            mark(f"xdpgp{c}", "GP")
            GOP("tensor_tensor", y_p, y_p, xdp, ALU.add,
                w=[mk(f"yove{c}")])
            mark(f"xdpve{c}", "GP")
            grp_se = []
            grp_ye = []
            grp_sv = []
            grp_sttv = []

            def lam_rt(g8):
                bank = bkB[1 if g8 % 2 == 0 else 3]
                MMOP(bank, sel8b[:, g8 * 128:(g8 + 1) * 128], acs8P[par],
                     w=[acsf, mk(f"yove{c}"), mk(f"rtse{c-1}")] if g8 < 2 else
                     [acsf, grp_se[g8 - 2]])
                mmv = snap("MM")
                ew = [mmv, mk(f"stt{c-1}")] if g8 < 2 else [mmv, grp_sv[g8 - 2]]
                for hh in range(4):
                    SOP(m_v[:, g8 % 2, hh * 128:(hh + 1) * 128],
                        bank[:, hh * 128:(hh + 1) * 128], AF.Exp,
                        bias=nacs_tmP[par][:, g8 * 4 + hh:g8 * 4 + hh + 1],
                        scale=1.0, w=ew if hh == 0 else ())
                VOP("scalar_tensor_tensor",
                    m_v[:, g8 % 2, :].rearrange("p (h t) -> p h t", h=4),
                    m_v[:, g8 % 2, :].rearrange("p (h t) -> p h t", h=4),
                    1.0,
                    pm_bfP[par].unsqueeze(1).broadcast_to([128, 4, 128]),
                    ALU.min, ALU.mult, w=[snap("SE"), mk(f"b0ve{c}")])
                grp_se.append(snap("SE"))
                grp_sv.append(snap("VE"))
                grp_sttv.append(snap("VE"))

            def lam_yd(g8):
                ydbank = bkB[2][:, (g8 % 2) * 256:(g8 % 2) * 256 + 256]
                ydw = [grp_sttv[g8]]
                if g8 >= 2:
                    ydw.append(grp_ye[g8 - 2])
                else:
                    ydw.append(mk(f"fve{c}"))
                    ydw.append(mk(f"umgp{c}"))
                for hh in range(4):
                    h = g8 * 4 + hh
                    MMOP(ydbank[:, hh * 64:(hh + 1) * 64],
                         m_v[:, g8 % 2, hh * 128:(hh + 1) * 128],
                         u_t[:, h * 64:(h + 1) * 64],
                         w=ydw if hh == 0 else ())
                VOP("tensor_tensor", y_p[:, g8 * 256:(g8 + 1) * 256],
                    y_p[:, g8 * 256:(g8 + 1) * 256], ydbank, ALU.add,
                    w=[snap("MM"), mk(f"xdpve{c}")] if g8 == 0
                    else [snap("MM")])
                grp_ye.append(snap("VE"))

            for g8 in range(8):
                lam_rt(g8)
                if fillers:
                    fillers.pop(0)()
                lam_yd(g8)
            while fillers:
                fillers.pop(0)()
            mark(f"rtse{c}", "SE")
            mark(f"stt{c}", "VE")
            mark(f"ydve{c}", "VE")
            mark(f"rtmm{c}", "MM")

        def ssd_back3(c):
            par = c % 2
            GOP("tensor_tensor", u_t.rearrange("p (h q) -> p h q", h=32),
                u_t.rearrange("p (h q) -> p h q", h=32),
                bcast2(ds_tmP[par], 64), ALU.mult, w=[mk(f"rtmm{c}")])
            mark(f"uvgp{c}", "GP")
            GOP("tensor_tensor", pstate.rearrange("p (h q) -> p h q", h=32),
                pstate.rearrange("p (h q) -> p h q", h=32),
                bcast2(dec_repP[par], 64), ALU.mult,
                w=[mk(f"pbfve{c-1}"), mk(f"fve{c}")])
            mark(f"psgp{c}", "GP")
            stev = []
            for q in range(4):
                bank = bkB[1 if q % 2 == 0 else 3]
                if q == 0:
                    wmm = [mk(f"uvgp{c}"), mk(f"rtse{c}")]
                elif q >= 2:
                    wmm = [stev[q - 2]]
                else:
                    wmm = ()
                MMOP(bank, b_tmP[par], u_t[:, q * 512:(q + 1) * 512], w=wmm)
                VOP("tensor_tensor", pstate[:, q * 512:(q + 1) * 512],
                    pstate[:, q * 512:(q + 1) * 512], bank, ALU.add,
                    w=[snap("MM"), snap("GP")] if q == 0 else [snap("MM")])
                stev.append(snap("VE"))
            mark(f"smm{c}", "MM")
            mark(f"sve{c}", "VE")
            VOP("tensor_copy", pstate_bf, pstate, w=[snap("VE")])
            mark(f"pbfve{c}", "VE")
            DMA("BNC", yp_dram[c], y_p, w=[snap("VE")])
            mark(f"ypbnc{c}", "BNC")

        wov = woutt_d.rearrange("(ch cl) d -> cl ch d", cl=128)
        wstB = [ppA, ppB]

        def w1_thunks():
            out = []
            first = [True]

            def cast2(wvd, wv_, cc):
                for q in range(2):
                    w = ([mk("q1ve15"), mk("ppve15"), mk("ppgp15"),
                          mk("ppse15"), mk("ipse15")]
                         if first[0] else [snap("VE")])
                    first[0] = False
                    DMA("LDX", wstB[(cc + q) % 2], wvd[:, cc + q, :], w=w)
                    VOP("tensor_copy", wv_[:, cc + q, :],
                        wstB[(cc + q) % 2], w=[snap("LDX")])
            for wd, wv_ in [(wrt_d, wrv), (wkt_d, wkv)]:
                wvd = wd.rearrange("(ch cl) d -> cl ch d", cl=128)
                for cc in range(0, 8, 2):
                    out.append(lambda wvd=wvd, wv_=wv_, cc=cc:
                               cast2(wvd, wv_, cc))
            return out

        def w1b_thunks():
            out = []
            first = [True]

            def cast2(cc):
                wvd = wvt_d.rearrange("(ch cl) d -> cl ch d", cl=128)
                for q in range(2):
                    DMA("LDX", wstB[(cc + q) % 2], wvd[:, cc + q, :],
                        w=[snap("VE")])
                    w = ([snap("LDX"), mk("cvmm7")] if first[0]
                         else [snap("LDX")])
                    first[0] = False
                    VOP("tensor_copy", wvv[:, cc + q, :],
                        wstB[(cc + q) % 2], w=w)
            return [lambda cc=cc: cast2(cc) for cc in range(0, 8, 2)]

        def w2_thunks():
            out = []

            def red2(cc):
                for q in range(2):
                    DMA("LDX", wstB[(cc + q) % 2], wov[:, cc + q, :],
                        w=[snap("VE")])
                    VOP("tensor_reduce", wpp[:, cc + q:cc + q + 1],
                        wstB[(cc + q) % 2], AX.X, ALU.add,
                        apply_absolute_value=True, w=[snap("LDX")])

            def tail():
                VOP("tensor_reduce", wpp[:, 16:17], wpp[:, 0:16], AX.X,
                    ALU.add, w=[snap("VE")])
                MMOP(psA[0:1, 0:1], onescol, wpp[:, 16:17],
                     w=[snap("VE"), mk("ipse15"), mk("q1mm15")])
                VOP("tensor_scalar", wpp[0:1, 17:18], psA[0:1, 0:1],
                    1.0 / (DI * D), 1e-5, ALU.mult, ALU.max, w=[snap("MM")])
                recip_rows(wpp[0:1, 18:19], wpp[0:1, 17:18], w=[snap("VE")])
                MMOP(psA[:, 8:9], ones1, wpp[0:1, 18:19], w=[snap("SE")])
                VOP("tensor_copy", wpp[:, 2:3], psA[:, 8:9], w=[snap("MM")])
            for cc in range(0, 16, 2):
                out.append(lambda cc=cc: red2(cc))
            out.append(tail)
            return out

        def w3_thunks():
            first = [True]

            def quant2(cc):
                for q in range(2):
                    DMA("LDX", wstB[(cc + q) % 2], wov[:, cc + q, :],
                        w=[snap("VE")])
                    w = ([snap("LDX"), mk("cvmm7"), mk("ipse15")]
                         if first[0] else [snap("LDX")])
                    first[0] = False
                    VOP("tensor_scalar", wstB[(cc + q) % 2],
                        wstB[(cc + q) % 2], wpp[:, 2:3], MAGIC,
                        ALU.mult, ALU.add, w=w)
                    VOP("tensor_scalar", wstB[(cc + q) % 2],
                        wstB[(cc + q) % 2], MAGIC - 1.0, MAGIC,
                        ALU.max, ALU.subtract)
                    VOP("tensor_scalar", wo_v[:, cc + q, :],
                        wstB[(cc + q) % 2], 1.0, None, ALU.min)
            return [lambda cc=cc: quant2(cc) for cc in range(0, 16, 2)]

        prepipe(-1)
        in_proj(-1)
        prepipe(0)
        in_proj(0)
        prepipe(1)
        in_proj(1)
        conv_pet(0)
        conv_mm(0)
        prepipe(2)
        in_proj(2)
        prepipe(3)
        in_proj(3)
        prepipe(4)
        prepipe(5)
        w3t = None
        for g in range(8):
            if g == 7:
                w3t = w3_thunks()
            ssd_front1(2 * g)
            if g < 6:
                in_proj(2 * g + 4, 0)
            ssd_front2(2 * g)
            ssd_back1(2 * g)
            if g < 7:
                conv_pet(g + 1)
            if g < 6:
                in_proj(2 * g + 4, 1)
            ssd_front1(2 * g + 1)
            if g < 6:
                in_proj(2 * g + 5, 0)
            ssd_back2(2 * g,
                      ip_fillers(2 * g + 5) if g < 6 else
                      (w1_thunks() if g == 6 else
                       w1b_thunks() + w3t[0:4]))
            ssd_front2(2 * g + 1)
            ssd_back3(2 * g)
            ssd_back1(2 * g + 1)
            ssd_back2(2 * g + 1,
                      cv_fillers(g + 1) if g < 7 else w3t[4:8])
            if g == 6:
                for th in w2_thunks():
                    th()
            ssd_back3(2 * g + 1)
            if g < 5:
                prepipe(2 * g + 6)
                prepipe(2 * g + 7)

        DMA("BNC", gin_S, pstate, w=[mk("pbfve15"), snap("VE")])
        DMA("BNC", gin_as, run_row[0:1, 0:32], w=[snap("VE")])
        bncv = snap("BNC")
        E("gpsimd", lambda e: e.collective_compute(
            "AllGather", ALU.bypass, [list(range(8))],
            [gin_S.opt()], [gout_S.opt()]), w=[bncv], i="CC")
        E("gpsimd", lambda e: e.collective_compute(
            "AllGather", ALU.bypass, [list(range(8))],
            [gin_as.opt()], [gout_as.opt()]), i="CC")
        ccv = snap("CC")

        DMA("LDX", sb8_32, gout_as, w=[ccv])
        MMOP(psA[0:8, 32:64], between, sb8_32, w=[snap("LDX"), snap("VE")])
        SOP(sbw8, psA[0:8, 32:64], AF.Exp, w=[snap("MM")])
        VOP("tensor_scalar", sbw8, sbw8, maskj, None, ALU.mult,
            w=[snap("SE")])
        DMA("BNC", w8_dram, sbw8, w=[snap("VE")])
        DMA("BNC", wflat, w8_dram.rearrange("j h -> (j h)").unsqueeze(0),
            w=[snap("BNC")])
        MMOP(psA[:, 256:512], ones1, wflat, w=[snap("BNC")])
        VOP("tensor_copy", wrep, psA[:, 256:512], w=[snap("MM")])
        VOP("memset", stmp[:, :], 0.0)
        gsv = gout_S.rearrange("(r p) d -> r p d", p=128)
        smul = []
        for j in range(8):
            DMA("LDX", sjv[:, j % 2, :], gsv[j],
                w=[ccv, snap("VE")] if j < 2 else [ccv, smul[j - 2]])
            VOP("tensor_tensor", tcorr.rearrange("p (h q) -> p h q", h=32),
                sjv[:, j % 2, :].rearrange("p (h q) -> p h q", h=32),
                bcast2(wrep[:, j * 32:(j + 1) * 32], 64), ALU.mult,
                w=[snap("LDX")])
            smul.append(snap("VE"))
            VOP("tensor_tensor", stmp, stmp, tcorr, ALU.add)
        VOP("tensor_copy", sin_bf, stmp, w=[snap("VE")])
        mark("sinve", "VE")

        def passb_attn_pre_a(c):
            par = c % 2
            DMA("LDX", ypB, yp_dram[c],
                w=[mk("sinve"), mk("attnveL"), mk("attnseL")])
            DMA("LDX", szB, sz_dram[c])
            DMA("LDX", cfmB, c_dram[c])
            xt = x0t.rearrange("p (b d) -> p b d", b=2)[:, 0, :]
            DMA("LDX", xt, x0_d[c + 1], w=[mk("attnseL")])
            lx = snap("LDX")
            ycev = []
            for q in range(4):
                bank = bkA[q % 2]
                if q == 0:
                    wmm = [lx, mk("sinve"), mk("oppveL"), mk("q2mmL")]
                elif q >= 2:
                    wmm = [ycev[q - 2]]
                else:
                    wmm = ()
                MMOP(bank, cfmB, sin_bf[:, q * 512:(q + 1) * 512], w=wmm)
                VOP("tensor_tensor",
                    tcorr[:, q * 512:(q + 1) * 512].rearrange(
                        "p (h q2) -> p h q2", h=8),
                    bank.rearrange("p (h q2) -> p h q2", h=8),
                    bcast2(g_st[:, 32 * c + q * 8:32 * c + (q + 1) * 8], 64),
                    ALU.mult,
                    w=[snap("MM"), ("BNC", cnt["BNC"]), mk("blnL"),
                       mk("qgpL")] if q == 0
                    else [snap("MM")])
                ycev.append(snap("VE"))
            VOP("tensor_tensor", ypB, ypB, tcorr, ALU.add, w=[lx])
            VOP("tensor_tensor", ypB, ypB, szB, ALU.mult)
            VOP("scalar_tensor_tensor", tcorr, ypB, 1.0, ypB, ALU.mult,
                ALU.mult, accum_out=rowb[:, 0:1])
            sv = snap("VE")
            VOP("tensor_scalar", rowb[:, 1:2], rowb[:, 0:1], 1.0 / DI, None,
                ALU.mult, w=[sv])
            VOP("tensor_scalar", rowb[:, 1:2], rowb[:, 1:2], 1e-6, None,
                ALU.add)
            SOP(rowb[:, 1:2], rowb[:, 1:2], AF.Ln, w=[snap("VE")])
            SOP(rowb[:, 2:3], rowb[:, 1:2], AF.Exp, scale=-0.5)
            VOP("scalar_tensor_tensor", tcorr, ypB, rowb[:, 2:3], onwrep,
                ALU.mult, ALU.mult, w=[snap("SE")])

        def passb_attn_pre_b(c):
            par = c % 2
            VOP("tensor_reduce", rowb[:, 3:4], tcorr, AX.X, ALU.add)
            VOP("scalar_tensor_tensor", ypB, tcorr, 1.0, tcorr, ALU.mult,
                ALU.mult, accum_out=rowb[:, 4:5])
            sv = snap("VE")
            VOP("tensor_scalar", rowb[:, 5:6], rowb[:, 3:4], 1.0 / DI, None,
                ALU.mult, w=[sv])
            VOP("tensor_tensor", rowb[:, 6:7], rowb[:, 5:6], rowb[:, 5:6],
                ALU.mult)
            VOP("tensor_scalar", rowb[:, 7:8], rowb[:, 4:5], 1.0 / DI, None,
                ALU.mult)
            VOP("tensor_tensor", rowb[:, 7:8], rowb[:, 7:8], rowb[:, 6:7],
                ALU.subtract)
            VOP("tensor_scalar", rowb[:, 7:8], rowb[:, 7:8], 1e-5, None,
                ALU.add)
            SOP(rowb[:, 7:8], rowb[:, 7:8], AF.Ln, w=[snap("VE")])
            SOP(rowb[:, 8:9], rowb[:, 7:8], AF.Exp, scale=-0.5)
            VOP("tensor_tensor", rowb[:, 9:10], rowb[:, 5:6], rowb[:, 8:9],
                ALU.mult, w=[snap("SE")])
            VOP("tensor_scalar_mul", rowb[:, 9:10], rowb[:, 9:10], -1.0)
            SOP(ypB, tcorr, AF.Identity, bias=rowb[:, 9:10],
                scale=rowb[:, 8:9], w=[snap("VE")])
            VOP("tensor_reduce", rowb[:, 10:11], ypB, AX.X, ALU.max,
                apply_absolute_value=True, w=[snap("SE")])
            sv = snap("VE")
            VOP("tensor_scalar", rowb[:, 10:11], rowb[:, 10:11], 1e-5, None,
                ALU.max, w=[sv])
            VOP("tensor_scalar", rowb[:, 13:14], rowb[:, 10:11], 1.0 / 127.0,
                None, ALU.mult)
            VOP("tensor_copy", rowb[:, 11:12], rowb[:, 10:11])
            SOP(rowb[:, 11:12], rowb[:, 11:12], AF.Ln, w=[snap("VE")])
            SOP(rowb[:, 12:13], rowb[:, 11:12], AF.Exp, scale=-1.0)
            VOP("tensor_scalar", rowb[:, 12:13], rowb[:, 12:13], 127.0, None,
                ALU.mult, w=[snap("SE")])
            VOP("tensor_scalar", tcorr, ypB, rowb[:, 12:13], MAGIC, ALU.mult,
                ALU.add, w=[mk("q2mmL")])
            VOP("tensor_scalar", tcorr, tcorr, MAGIC - 128.0, MAGIC, ALU.max,
                ALU.subtract)
            VOP("tensor_scalar", q2, tcorr, 127.0, None, ALU.min)
            mark("qgpL", "VE")
            qv = snap("VE")
            PET(q2Tv, [q2[:, j * 128:(j + 1) * 128] for j in range(16)],
                [bkA[0], bkA[1]], w=[qv, mk("oppveL")])
            mark("q2mmL", "MM")
            mark("apveL", "VE")
            mark(f"apve{c}", "VE")
            mark("q2tpqL", "VE")

        def passb_attn_pre(c):
            passb_attn_pre_a(c)
            passb_attn_pre_b(c)

        def passb_attn_mm(c):
            par = c % 2
            x1b = x1bv[par] if c != 15 else x1b3
            xt = x0t.rearrange("p (b d) -> p b d", b=2)[:, 0, :]
            lx = ("LDX", cnt["LDX"])
            tpq = mk("q2tpqL")
            for dg in range(2):
                bank = bkA[dg]
                for cc2 in range(16):
                    MMOP(bank, q2Tv[:, cc2, :],
                         wo_v[:, cc2, dg * 512:dg * 512 + 512],
                         start=(cc2 == 0), stop=(cc2 == 15),
                         w=[tpq] if (dg == 0 and cc2 == 0) else ())
            mmv = snap("MM")
            VOP("tensor_scalar", rowb[:, 14:15], r1_st[:, c + 1:c + 2], 1.0,
                None, ALU.add, w=[mmv])
            VOP("tensor_tensor", rowb[:, 15:16], mu1_st[:, c + 1:c + 2],
                r1_st[:, c + 1:c + 2], ALU.mult)
            VOP("tensor_scalar_mul", rowb[:, 15:16], rowb[:, 15:16], -1.0)
            SOP(t1b, xt, AF.Identity, bias=rowb[:, 15:16],
                scale=rowb[:, 14:15],
                w=[snap("VE"), lx, mk("blnL")])
            sev = snap("SE")
            for dg in range(2):
                VOP("scalar_tensor_tensor", x1b[:, dg * 512:dg * 512 + 512],
                    psA[:, dg * 512:dg * 512 + 512], rowb[:, 13:14],
                    t1b[:, dg * 512:dg * 512 + 512], ALU.mult, ALU.add,
                    w=[sev, mmv, mk(f"ffnve{c-2}")]
                    if dg == 0 else ())
            mark("attnveL", "VE")
            mark("attnseL", "SE")
            mark("oppveL", "VE")
            mark(f"attnve{c}", "VE")

        def passb_ln2(c, src):
            rb = rowb[:, 16:32]
            VOP("tensor_reduce", rb[:, 0:1], src, AX.X, ALU.add,
                w=[mk(f"attnve{c}")])
            VOP("scalar_tensor_tensor", lnsink, src, 1.0, src, ALU.mult,
                ALU.mult, accum_out=rb[:, 1:2], w=[mk(f"blnrow{c-1}")])
            sv = snap("VE")
            VOP("tensor_scalar", rb[:, 2:3], rb[:, 0:1], 1.0 / D, None,
                ALU.mult, w=[sv])
            VOP("tensor_tensor", rb[:, 3:4], rb[:, 2:3], rb[:, 2:3], ALU.mult)
            VOP("tensor_scalar", rb[:, 4:5], rb[:, 1:2], 1.0 / D, None,
                ALU.mult)
            VOP("tensor_tensor", rb[:, 4:5], rb[:, 4:5], rb[:, 3:4],
                ALU.subtract)
            VOP("tensor_scalar", rb[:, 4:5], rb[:, 4:5], 1e-5, None, ALU.add)
            SOP(rb[:, 4:5], rb[:, 4:5], AF.Ln, w=[snap("VE"), mk(f"bse{c-1}")])
            SOP(rb[:, 5:6], rb[:, 4:5], AF.Exp, scale=-0.5)
            VOP("tensor_tensor", rb[:, 6:7], rb[:, 2:3], rb[:, 5:6], ALU.mult,
                w=[snap("SE")])
            VOP("tensor_scalar_mul", rb[:, 6:7], rb[:, 6:7], -1.0)
            mark(f"blnrow{c}", "VE")
            SOP(ln2tm, src, AF.Identity, bias=rb[:, 6:7], scale=rb[:, 5:6],
                w=[snap("VE"), mk(f"ffnve{c-1}")])
            sev = snap("SE")
            if c == 0:
                DMA("LDX", lfv[:, :, 15:16],
                    ln2b_dram.rearrange("a (j p) -> p j a", p=128),
                    w=[mk("b2bnc"), mk(f"ffngp{c-1}")],
                    allow_slow_non_contiguous=True)
                hv = [snap("LDX")]
            else:
                VOP("tensor_copy", lfv[:, :, 15:16], lfv[:, :, 143:144],
                    w=[mk(f"ffngp{c-1}"), mk(f"ffnmm{c-1}")])
                hv = [snap("VE")]
            PET(lfv[:, :, 16:144],
                [ln2tm[:, j * 128:(j + 1) * 128] for j in range(8)],
                [bkB[0], bkB[1]],
                w=[sev, mk(f"ffnmm{c-1}"), mk(f"rve{c-1}")] + hv,
                evac_w=[mk(f"ffngp{c-1}")])
            if c == 14:
                VOP("tensor_copy", ln2h15[:, 0:8],
                    lfv[:, :, 143:144].squeeze(2), w=[snap("VE")])
            mark(f"bln{c}", "VE")
            mark("blnL", "VE")

        def passb_ffn_mix(c, x1src):
            VOP("tensor_tensor", dmv, lfv[:, :, 16:144], lfv[:, :, 15:143],
                ALU.subtract, w=[mk(f"bln{c}"), mk(f"ffngp{c-1}")])
            VOP("tensor_tensor", xrv, dmv,
                murep.unsqueeze(2).broadcast_to([128, 8, 128]), ALU.mult,
                w=[mk(f"ffnmm{c-1}")])
            VOP("tensor_tensor", xrv, xrv, lfv[:, :, 15:143], ALU.add)
            GOP("tensor_tensor", xkv, dmv,
                mukrep.unsqueeze(2).broadcast_to([128, 8, 128]), ALU.mult,
                w=[snap("VE"), mk(f"ffnmm{c-1}")])
            GOP("tensor_tensor", xkv, xkv, lfv[:, :, 15:143], ALU.add)
            mark(f"ffngp{c}", "GP")
            mark(f"ffnmixv{c}", "VE")

        def passb_ffn_rest(c, x1src):
            svx = mk(f"ffnmixv{c}")
            ggx = mk(f"ffngp{c}")
            for dg in range(2):
                bank = bkB[dg]
                for cc in range(8):
                    MMOP(bank, xrv[:, cc, :],
                         wrv[:, cc, dg * 512:dg * 512 + 512],
                         start=(cc == 0), stop=(cc == 7),
                         w=[svx, mk(f"blnve{c}") if False else mk(f"bln{c}")]
                         if (dg == 0 and cc == 0) else ())
            mmv = snap("MM")
            SOP(r_bf[:, 0:512], bkB[0], AF.Sigmoid,
                w=[mmv, mk(f"vmm{c-1}")])
            SOP(r_bf[:, 512:1024], bkB[1], AF.Sigmoid)
            mark(f"rse{c}", "SE")
            for dg in range(2):
                bank = bkB[2 + dg]
                for cc in range(8):
                    MMOP(bank, xkv[:, cc, :],
                         wkv[:, cc, dg * 512:dg * 512 + 512],
                         start=(cc == 0), stop=(cc == 7),
                         w=[ggx, mk(f"kse{c-1}"), mk(f"kve{c-1}")]
                         if (dg == 0 and cc == 0) else ())
            mmv = snap("MM")
            SOP(krelu[:, 0:512], bkB[2], AF.Relu, w=[mmv])
            SOP(krelu[:, 512:1024], bkB[3], AF.Relu)
            SOP(kact, krelu, AF.Square)
            mark(f"kse{c}", "SE")
            sev = snap("SE")
            PET(kTv, [kact[:, j * 128:(j + 1) * 128] for j in range(8)],
                [bkB[2], bkB[3]], w=[sev, mk(f"vmm{c-1}")])
            mark(f"kpet{c}", "MM")
            mark(f"kve{c}", "VE")
            tpk = snap("VE")
            for dg in range(2):
                bank = bkB[dg]
                for cc in range(8):
                    MMOP(bank, kTv[:, cc, :],
                         wvv[:, cc, dg * 512:dg * 512 + 512],
                         start=(cc == 0), stop=(cc == 7),
                         w=[tpk, mk(f"rse{c}")]
                         if (dg == 0 and cc == 0) else ())
            mark(f"vmm{c}", "MM")
            mmv = snap("MM")
            for dg in range(2):
                VOP("scalar_tensor_tensor", outb[:, dg * 512:dg * 512 + 512],
                    bkB[dg], 1.0, r_bf[:, dg * 512:dg * 512 + 512],
                    ALU.mult, ALU.mult,
                    w=[mmv, mk(f"rse{c}"), mk(f"osto{c-1}")]
                    if dg == 0 else ())
            VOP("tensor_tensor", outb, outb, x1src, ALU.add)
            mark(f"ffnve{c}", "VE")
            mark(f"rve{c}", "VE")
            DMA("STO", out_d[c * 128:(c + 1) * 128, :], outb, w=[snap("VE")])
            mark(f"osto{c}", "STO")
            mark(f"ffnmm{c}", "MM")
            mark(f"bse{c}", "SE")

        def passb_ffn(c, x1src):
            passb_ffn_mix(c, x1src)
            passb_ffn_rest(c, x1src)

        passb_attn_pre(15)
        passb_attn_mm(15)
        VOP("tensor_copy", x1c15, x1b3, w=[mk("attnveL")])
        DMA("BNC", gin_x1, x1b3[127:128, :], w=[snap("VE")])
        E("gpsimd", lambda e: e.collective_compute(
            "AllGather", ALU.bypass, [list(range(8))],
            [gin_x1.opt()], [gout_x1.opt()]),
          w=[snap("BNC"), snap("VE")], i="CC")
        DMA("LDX", sbgx, gout_x1, w=[snap("CC"), mk("q2mmL"),
                                     mk("oppveL")])
        lgx = snap("LDX")
        MMOP(psA[0:1, 1024:1536], selprev, sbgx[:, 0:512],
             w=[lgx, snap("VE"), snap("SE")])
        MMOP(psA[0:1, 1536:2048], selprev, sbgx[:, 512:1024])
        mark("q2mmL", "MM")
        VOP("tensor_copy", x1brow[0:1, 0:512], psA[0:1, 1024:1536],
            w=[snap("MM"), snap("STO")])
        VOP("tensor_copy", x1brow[0:1, 512:1024], psA[0:1, 1536:2048])
        VOP("tensor_reduce", rowb[0:1, 16:17], x1brow, AX.X, ALU.add,
            w=[snap("VE")])
        VOP("scalar_tensor_tensor", t1b[0:1, :], x1brow, 1.0, x1brow,
            ALU.mult, ALU.mult, accum_out=rowb[0:1, 17:18])
        sv = snap("VE")
        VOP("tensor_scalar", rowb[0:1, 18:19], rowb[0:1, 16:17], 1.0 / D,
            None, ALU.mult, w=[sv])
        VOP("tensor_tensor", rowb[0:1, 19:20], rowb[0:1, 18:19],
            rowb[0:1, 18:19], ALU.mult)
        VOP("tensor_scalar", rowb[0:1, 20:21], rowb[0:1, 17:18], 1.0 / D,
            None, ALU.mult)
        VOP("tensor_tensor", rowb[0:1, 20:21], rowb[0:1, 20:21],
            rowb[0:1, 19:20], ALU.subtract)
        VOP("tensor_scalar", rowb[0:1, 20:21], rowb[0:1, 20:21], 1e-5, None,
            ALU.add)
        SOP(rowb[0:1, 20:21], rowb[0:1, 20:21], AF.Ln, w=[snap("VE")])
        SOP(rowb[0:1, 21:22], rowb[0:1, 20:21], AF.Exp, scale=-0.5)
        VOP("tensor_tensor", rowb[0:1, 22:23], rowb[0:1, 18:19],
            rowb[0:1, 21:22], ALU.mult, w=[snap("SE")])
        VOP("tensor_scalar_mul", rowb[0:1, 22:23], rowb[0:1, 22:23], -1.0)
        VOP("tensor_scalar", ln2brow, x1brow, rowb[0:1, 21:22],
            rowb[0:1, 22:23], ALU.mult, ALU.add)
        VOP("tensor_copy", ln2bb, ln2brow)
        DMA("BNC", ln2b_dram, ln2bb, w=[snap("VE")])
        mark("b2bnc", "BNC")

        passb_attn_pre(0)
        for c in range(15):
            passb_attn_mm(c)
            passb_ln2(c, x1bv[c % 2])
            passb_ffn_mix(c, x1bv[c % 2])
            if c < 14:
                passb_attn_pre_a(c + 1)
            passb_ffn_rest(c, x1bv[c % 2])
            if c < 14:
                passb_attn_pre_b(c + 1)
        passb_ln2(15, x1c15)
        passb_ffn(15, x1c15)

        E("sync", lambda e: e.nop(), w=[snap("STO"), snap("BNC"),
                                        snap("VE"), snap("MM"), snap("SE"),
                                        snap("GP")])

        block = stack.enter_context(nc.Block())
        SELFW = {"vector": "VE", "scalar": "SE", "gpsimd": "GP"}

        def replay(name):
            own = SELFW.get(name)

            def go(eng):
                done = 0
                for wl, fn, pend in SCR[name]:
                    if fn is None:
                        for (n, v) in wl:
                            eng.wait_ge(semh[n], v)
                        continue
                    if own is not None and done > 0:
                        eng.wait_ge(semh[own], done)
                    for (n, v) in wl:
                        if v > 0:
                            eng.wait_ge(semh[n], v)
                    inst = fn(eng)
                    if pend is not None:
                        inst.then_inc(semh[pend[0]], pend[1])
                        if own is not None and pend[0] == own:
                            done += pend[1]
            return go

        block.sync(replay("sync"))
        block.tensor(replay("tensor"))
        block.vector(replay("vector"))
        block.scalar(replay("scalar"))
        block.gpsimd(replay("gpsimd"))

    return nc


_NC_CACHE = [None]


def _host_inputs(inputs):
    f = np.float32
    x = np.ascontiguousarray(
        inputs["L0_attn_hidden_states"].reshape(16384, D)).astype(f)
    conv_w = np.asarray(inputs["L0_attn_conv_w"], f)
    conv_b = np.asarray(inputs["L0_attn_conv_b"], f)
    cdiag = np.zeros((4, 18, 128, 128), f)
    idx = np.arange(128)
    for k in range(4):
        for j in range(18):
            cdiag[k, j, idx, idx] = conv_w[j * 128:(j + 1) * 128, k]
    sel8 = np.zeros((8, 1024), f)
    for g in range(8):
        sel8[g, g * 128:(g + 1) * 128] = 1.0
    base = {
        "wint": np.ascontiguousarray(np.asarray(inputs["L0_attn_in_proj_w"], f).T),
        "woutt": np.ascontiguousarray(np.asarray(inputs["L0_attn_out_proj_w"], f).T),
        "wrt": np.ascontiguousarray(np.asarray(inputs["L0_ffn_W_r"], f).T),
        "wkt": np.ascontiguousarray(np.asarray(inputs["L0_ffn_W_k"], f).T),
        "wvt": np.ascontiguousarray(np.asarray(inputs["L0_ffn_W_v"], f).T),
        "cdiag": cdiag,
        "cbias_fm": np.ascontiguousarray(conv_b.reshape(18, 128).T),
        "nwrep": np.ascontiguousarray(
            np.broadcast_to(np.asarray(inputs["L0_attn_norm_w"], f), (128, D))),
        "onwrep": np.ascontiguousarray(
            np.broadcast_to(np.asarray(inputs["L0_attn_out_norm_w"], f),
                            (128, DI))),
        "murep": np.ascontiguousarray(
            np.asarray(inputs["L0_ffn_mu_r"], f).reshape(8, 128).T),
        "mukrep": np.ascontiguousarray(
            np.asarray(inputs["L0_ffn_mu_k"], f).reshape(8, 128).T),
        "dtb": np.tile(np.asarray(inputs["L0_attn_dt_bias"],
                                  f).reshape(32, 1), (2, 1)),
        "alog": np.tile(np.asarray(inputs["L0_attn_A_log"],
                                   f).reshape(32, 1), (2, 1)),
        "dprep": np.ascontiguousarray(
            np.broadcast_to(np.asarray(inputs["L0_attn_Dp"], f), (128, 32))),
        "ones1": np.ones((1, 128), f),
        "onescol": np.ones((128, 1), f),
        "i128": np.eye(128, dtype=f),
        "u01": np.triu(np.ones((128, 128), f)),
        "sel8": sel8,
    }
    in_maps = []
    for c in range(NCORES):
        m = dict(base)
        xt = np.zeros((NT + 1, 128, D), f)
        s = c * T
        if c == 0:
            xt.reshape(-1, D)[128:] = x[0:T]
        else:
            xt.reshape(-1, D)[:] = x[s - 128:s + T]
        m["x0"] = xt
        sel = np.zeros((8, 1), f)
        if c > 0:
            sel[c - 1, 0] = 1.0
        m["selprev"] = sel
        bt = np.zeros((8, 8), f)
        for k in range(8):
            for j in range(8):
                if j < k < c:
                    bt[k, j] = 1.0
        m["between"] = bt
        mj = np.zeros((8, 1), f)
        mj[:c, 0] = 1.0
        m["maskj"] = mj
        in_maps.append(m)
    return in_maps


def kernel(**inputs):
    if _NC_CACHE[0] is None:
        _NC_CACHE[0] = build_program()
    nc = _NC_CACHE[0]
    in_maps = _host_inputs(inputs)
    res = run_bass_kernel_spmd(nc, in_maps, core_ids=list(range(NCORES)))
    out = np.concatenate([np.asarray(r["out"]) for r in res.results], axis=0)
    return out.reshape(1, 16384, D).astype(np.float32)


# version 32 (final)
# speedup vs baseline: 1.0219x; 1.0131x over previous
import os
from contextlib import ExitStack

import numpy as np
import ml_dtypes

import concourse.bass as bass
import concourse.mybir as mybir
from concourse.bass_utils import run_bass_kernel_spmd

F32 = mybir.dt.float32
BF16 = mybir.dt.bfloat16
AF = mybir.ActivationFunctionType
ALU = mybir.AluOpType
AX = mybir.AxisListType
BFNP = ml_dtypes.bfloat16

NCORES = 8
T = 2048
NT = 16
D = 1024
DI = 2048
DIP = 4384
NH = 32
MAGIC = float(np.float32(1.5 * 2 ** 23))

RACE = os.environ.get("KMK_RACE", "0") == "1"
ENGS = ("sync", "tensor", "vector", "scalar", "gpsimd")


def bcast2(ap2d, reps):
    return ap2d.unsqueeze(2).broadcast_to([ap2d.shape[0], ap2d.shape[1], reps])


def build_program():
    nc = bass.Bass("TRN2", target_bir_lowering=False,
                   detect_race_conditions=RACE)

    def din(name, shape, dt=F32):
        return nc.dram_tensor(name, list(shape), dt, kind="ExternalInput").ap()

    x0_d = din("x0", (NT + 1, 128, D))
    wint_d = din("wint", (D, DIP))
    woutt_d = din("woutt", (DI, D))
    wrt_d = din("wrt", (D, D))
    wkt_d = din("wkt", (D, D))
    wvt_d = din("wvt", (D, D))
    cdiag_d = din("cdiag", (4, 18, 128, 128))
    cbias_fm_d = din("cbias_fm", (128, 18))
    nwrep_d = din("nwrep", (128, D))
    onwrep_d = din("onwrep", (128, DI))
    murep_d = din("murep", (128, 8))
    mukrep_d = din("mukrep", (128, 8))
    dtb_d = din("dtb", (64, 1))
    alog_d = din("alog", (64, 1))
    dprep_d = din("dprep", (128, 32))
    ones1_d = din("ones1", (1, 128))
    onescol_d = din("onescol", (128, 1))
    i128_d = din("i128", (128, 128))
    u01_d = din("u01", (128, 128))
    sel8_d = din("sel8", (8, 1024))
    selprev_d = din("selprev", (8, 1))
    between_d = din("between", (8, 8))
    maskj_d = din("maskj", (8, 1))

    out_d = nc.dram_tensor("out", [T, D], F32, kind="ExternalOutput").ap()

    yp_dram = nc.dram_tensor("yp_dram", [NT, 128, DI], F32).ap()
    sz_dram = nc.dram_tensor("sz_dram", [NT, 128, DI], BF16).ap()
    c_dram = nc.dram_tensor("c_dram", [NT, 128, 128], BF16).ap()
    acs_dram = nc.dram_tensor("acs_dram", [2, 32, 128], F32).ap()
    w8_dram = nc.dram_tensor("w8_dram", [8, 32], F32).ap()
    ln2b_dram = nc.dram_tensor("ln2b_dram", [1, D], BF16).ap()
    gin_S = nc.dram_tensor("gin_S", [128, DI], F32).ap()
    gout_S = nc.dram_tensor("gout_S", [8 * 128, DI], F32).ap()
    gin_as = nc.dram_tensor("gin_as", [1, 32], F32).ap()
    gout_as = nc.dram_tensor("gout_as", [8, 32], F32).ap()
    gin_x1 = nc.dram_tensor("gin_x1", [1, D], F32).ap()
    gout_x1 = nc.dram_tensor("gout_x1", [8, D], F32).ap()

    stack = ExitStack()
    with stack:
        AR = 53000
        arena_t = stack.enter_context(nc.sbuf_tensor("arena", [128, AR], F32))
        arena = arena_t.ap()
        off = [0]

        def af(cols, parts=128, align=16):
            off[0] = (off[0] + align - 1) // align * align
            o = off[0]
            off[0] += cols
            assert off[0] <= AR, f"arena overflow {off[0]}"
            return arena[0:parts, o:o + cols]

        def abf(cols_bf, parts=128, align=16):
            assert cols_bf % 2 == 0
            off[0] = (off[0] + align - 1) // align * align
            o = off[0]
            off[0] += cols_bf // 2
            assert off[0] <= AR, f"arena overflow {off[0]}"
            return arena[0:128, o:o + cols_bf // 2].bitcast(BF16)[0:parts, :]

        cb_fm = af(18)
        nwrep = af(D)
        onwrep = af(DI)
        murep = af(8)
        mukrep = af(8)
        dprep = af(32)
        dtb = af(1, 64)
        alogc = af(1, 64)
        nega = af(1, 64)
        ones1 = af(128, 1)
        onescol = af(1)
        u01b = abf(128)
        sel8b = af(1024, 8)
        mu1_st = af(NT + 1)
        r1_st = af(NT + 1)
        g_st = af(NT * 32)
        run_row = af(32, 1)
        rowb = af(32)
        rowdec = af(32, 1)
        wpp = af(32)
        u01f = af(128)
        i128 = af(128)
        ln2h15 = abf(16, 128)
        x0t = af(2 * D)
        selprev = af(1, 8)
        between = af(8, 8)
        maskj = af(1, 8)

        wbase = off[0]
        w_in_bf = abf(8 * DIP)
        wq_v = w_in_bf.rearrange("p (c d) -> p c d", c=8)
        cdiag_bf = abf(4 * 18 * 128)
        cdv = cdiag_bf.rearrange("p (k j t) -> p k j t", k=4, j=18)
        passA_wend = off[0]
        off[0] = wbase
        woutt_bf = abf(16 * D)
        wo_v = woutt_bf.rearrange("p (c d) -> p c d", c=16)
        wr_bf = abf(8 * D)
        wrv = wr_bf.rearrange("p (c d) -> p c d", c=8)
        wk_bf = abf(8 * D)
        wkv = wk_bf.rearrange("p (c d) -> p c d", c=8)
        wv_bf = abf(8 * D)
        wvv = wv_bf.rearrange("p (c d) -> p c d", c=8)
        off[0] = max(off[0], passA_wend)

        tbase = off[0]
        ppA = af(D)
        ppB = af(D)
        qblock = af(1536)
        q1 = qblock[:, 0:512].bitcast(BF16)
        q1T2 = [qblock[:, 512:1024].bitcast(BF16),
                qblock[:, 1024:1536].bitcast(BF16)]
        q1Tv2 = [t.rearrange("p (j t) -> p j t", j=8) for t in q1T2]
        szt = abf(DI)
        xdp = szt
        dt_tm2 = af(128)
        xbcg = abf(2 * 2304)
        xbv = xbcg.rearrange("p (b c) -> p b c", b=2)
        convin = abf(18 * 272)
        cvv = convin.rearrange("p (j t) -> p j t", j=18)
        xsil = abf(16 * 256)
        xsv = xsil.rearrange("p (j t) -> p j t", j=16)
        bcfm = abf(2 * 256)
        bcv = bcfm.rearrange("p (j t) -> p j t", j=2)
        x_tmP = [abf(DI), abf(DI)]
        xtvP = [t.rearrange("p (j t) -> p j t", j=16) for t in x_tmP]
        u_t = abf(DI)
        m_t = abf(2 * 512)
        m_v = m_t.rearrange("p (b d) -> p b d", b=2)
        pm_bfP = [abf(128), abf(128)]
        b_tmP = [abf(128), abf(128)]
        y_p = af(DI)
        acs8P = [af(512, 8), af(512, 8)]
        asum_row = af(32, 1)
        dtx_fm = af(128, 64)
        sp1_fm = af(128, 64)
        sp2_fm = af(128, 64)
        dta_fm = af(128, 64)
        a_fm = sp1_fm
        acs_fm = dtx_fm
        acs_tm = af(32)
        nacs_tmP = [af(32), af(32)]
        dta_tmP = [af(32), af(32)]
        gtmp = af(32)
        eacsP = [af(32), af(32)]
        ds_tmP = [af(32), af(32)]
        dec_repP = [af(32), af(32)]
        ones_a = af(128, 64)
        pstate = af(DI)
        pstate_bf = abf(DI)
        passA_tend = off[0]
        off[0] = tbase
        ypB = af(DI)
        stmp = ypB
        tcorr = af(DI)
        szB = abf(DI)
        cfmB = abf(128)
        qb2 = af(4096)
        q2 = qb2[:, 0:1024].bitcast(BF16)
        q2T = qb2[:, 1024:2048].bitcast(BF16)
        q2Tv = q2T.rearrange("p (j t) -> p j t", j=16)
        x1b2 = qb2[:, 2048:4096]
        x1bv = [x1b2[:, 0:1024], x1b2[:, 1024:2048]]
        x1b3 = af(D)
        t1b = af(D)
        sjbuf = qb2[:, 0:4096]
        sjv = sjbuf.rearrange("p (b d) -> p b d", b=2)
        sbgx = qb2[0:8, 0:1024]
        x1brow = qb2[0:1, 1024:2048]
        wflat = qb2[0:1, 3584:3840]
        ln2tm = abf(D)
        ln2fm = abf(8 * 144)
        lfv = ln2fm.rearrange("p (j t) -> p j t", j=8)
        dmix = ln2tm
        dmv = dmix.rearrange("p (j t) -> p j t", j=8)
        xr_fm = abf(8 * 128)
        xrv = xr_fm.rearrange("p (j t) -> p j t", j=8)
        xk_fm = abf(8 * 128)
        xkv = xk_fm.rearrange("p (j t) -> p j t", j=8)
        r_bf = abf(D)
        kact = abf(D)
        kab = af(512)
        sb8_32 = kab[0:8, 0:32]
        sbw8 = kab[0:8, 32:64]
        wrep = kab[:, 256:512]
        kactT = abf(8 * 128)
        kTv = kactT.rearrange("p (j t) -> p j t", j=8)
        outb = af(D)
        sin_bf = abf(DI)
        x1c15 = af(D)
        ln2brow = ppB[0:1, 0:D]
        ln2bb = ppB[0:1, 0:512].bitcast(BF16) if False else None
        krelu = af(D)
        lnsink = af(D)
        off[0] = max(off[0], passA_tend)
        stg0 = arena[:, tbase:tbase + DIP]
        stg1 = arena[:, tbase + DIP:tbase + 2 * DIP]
        stgs = [stg0, stg1]
        partial = arena[:, tbase + 2 * DIP:tbase + 2 * DIP + 32]
        cdiag_f = arena[:, tbase + 2 * DIP + 32:
                        tbase + 2 * DIP + 32 + 2 * 18 * 128]
        cfv = cdiag_f.rearrange("p (k j t) -> p k j t", k=2, j=18)
        ln2bb = tcorr[0:1, 0:512].bitcast(BF16)[0:1, 0:D]

        psA_t = stack.enter_context(nc.psum_tensor("psA", [128, 2048], F32))
        psB_t = stack.enter_context(nc.psum_tensor("psB", [128, 2048], F32))
        psA = psA_t.ap()
        psB = psB_t.ap()
        bkA = [psA[:, i * 512:(i + 1) * 512] for i in range(4)]
        bkB = [psB[:, i * 512:(i + 1) * 512] for i in range(4)]

        semh = {}
        cnt = {}
        for n in ["LDW", "LDX", "BNC", "ACS", "STO", "MM", "VE", "SE", "GP",
                  "CC"]:
            semh[n] = stack.enter_context(nc.semaphore(n))
            cnt[n] = 0
        SCR = {e: [] for e in ENGS}
        DMASEMS = {"LDW", "LDX", "BNC", "ACS", "STO"}
        dma_maxreq = {n: 0 for n in DMASEMS}
        dma_drained = {n: 0 for n in DMASEMS}
        M = {}

        def E(eng, fn, w=(), i=None, ik=1):
            wl = []
            for x in w:
                t = x if isinstance(x, tuple) else (x, cnt[x])
                if t[0] in DMASEMS:
                    t = (t[0], cnt[t[0]])
                    if t[1] > dma_maxreq[t[0]]:
                        dma_maxreq[t[0]] = t[1]
                wl.append(t)
            if i in DMASEMS and dma_maxreq[i] > dma_drained[i]:
                tgt = dma_maxreq[i]
                SCR[eng].append(([(i, tgt)], None, None))
                dma_drained[i] = tgt
            if i is not None:
                cnt[i] += ik
            SCR[eng].append((wl, fn, (i, ik) if i is not None else None))

        def snap(n):
            return (n, cnt[n])

        def mark(key, sem):
            M[key] = (sem, cnt[sem])

        def mk(key):
            return M.get(key, ("VE", 0))

        def DMA(sem, outap, inap, w=(), **kw):
            E("sync", lambda e, o=outap, ii=inap, k=kw:
              e.dma_start(out=o, in_=ii, **k), w=w, i=sem, ik=16)

        def VOP(meth, *args, w=(), **kw):
            E("vector", lambda e, a=args, k=kw: getattr(e, meth)(*a, **k),
              w=w, i="VE")

        def GOP(meth, *args, w=(), **kw):
            E("gpsimd", lambda e, a=args, k=kw: getattr(e, meth)(*a, **k),
              w=w, i="GP")

        def SOP(*args, w=(), **kw):
            E("scalar", lambda e, a=args, k=kw: e.activation(*a, **k),
              w=w, i="SE")

        def MMOP(out, lhsT, rhs, start=True, stop=True, w=()):
            E("tensor", lambda e, o=out, l=lhsT, r=rhs, s1=start, s2=stop:
              e.matmul(o, l, r, start=s1, stop=s2), w=w, i="MM")

        def TROP(out, inp, ident, w=()):
            E("tensor", lambda e, o=out, ii=inp, dd=ident: e.transpose(o, ii, dd),
              w=w, i="MM")

        i128b = u01f.bitcast(BF16)[:, 0:128]

        def PET(dst3, srcs, banks, w=(), evac_w=()):
            n = len(srcs)
            i = 0
            r = 0
            firstw = list(w)
            evs = []
            while i < n:
                m = min(8, n - i)
                bank = banks[r % len(banks)].bitcast(BF16)
                b3 = bank.rearrange("p (k t) -> p k t", k=8)
                roundw = list(firstw) + [snap("VE")]
                firstw = []
                for k in range(m):
                    E("tensor",
                      lambda e, ss=srcs[i + k], bb=bank, kk=k:
                      e.transpose(bb[:, kk * 128:(kk + 1) * 128], ss, i128b),
                      w=roundw if k == 0 else (), i="MM")
                mmv = snap("MM")
                VOP("tensor_copy", dst3[:, i:i + m, :], b3[:, 0:m, :],
                    w=[mmv] + list(evac_w))
                evs.append(snap("VE"))
                i += m
                r += 1

        def rsqrt_rows(dst, src, eps, w=()):
            VOP("tensor_scalar", src, src, eps, None, ALU.add, w=w)
            SOP(src, src, AF.Ln, w=[snap("VE")])
            SOP(dst, src, AF.Exp, scale=-0.5)

        def recip_rows(dst, src, w=()):
            SOP(src, src, AF.Ln, w=w)
            SOP(dst, src, AF.Exp, scale=-1.0)

        for dst, src in [(cb_fm, cbias_fm_d), (nwrep, nwrep_d),
                         (onwrep, onwrep_d), (murep, murep_d),
                         (mukrep, mukrep_d), (dprep, dprep_d), (dtb, dtb_d),
                         (alogc, alog_d), (ones1, ones1_d),
                         (onescol, onescol_d), (i128, i128_d), (u01f, u01_d),
                         (sel8b, sel8_d), (selprev, selprev_d),
                         (between, between_d), (maskj, maskj_d)]:
            DMA("LDW", dst, src)
        ld0 = snap("LDW")
        VOP("tensor_copy", u01b[:], u01f[:], w=[ld0])
        VOP("tensor_copy", i128b, i128, w=[ld0])
        VOP("memset", ones_a[:], 1.0)
        VOP("memset", run_row[:], 0.0)
        VOP("memset", pstate[:], 0.0)
        VOP("memset", pstate_bf[:], 0.0)
        SOP(nega[:], alogc[:], AF.Exp, w=[ld0])
        VOP("tensor_scalar_mul", nega[:], nega[:], -1.0, w=[snap("SE")])
        for half in range(2):
            DMA("LDW", cfv,
                cdiag_d.rearrange("k j a t -> a k j t")[:, 2 * half:2 * half + 2],
                w=[snap("VE")])
            VOP("tensor_copy", cdv[:, 2 * half:2 * half + 2], cfv,
                w=[snap("LDW")])
        wiv = wint_d.rearrange("(ch cl) d -> cl ch d", cl=128)
        for cc in range(8):
            DMA("LDW", stgs[cc % 2], wiv[:, cc, :], w=[snap("VE")])
            VOP("tensor_reduce", partial[:, cc:cc + 1], stgs[cc % 2][:, :],
                AX.X, ALU.add, apply_absolute_value=True, w=[snap("LDW")])
        VOP("tensor_reduce", partial[:, 8:9], partial[:, 0:8], AX.X, ALU.add,
            w=[snap("VE")])
        MMOP(psA[0:1, 0:1], onescol, partial[:, 8:9], w=[snap("VE")])
        VOP("tensor_scalar", rowb[0:1, 0:1], psA[0:1, 0:1], 1.0 / (DIP * D),
            1e-5, ALU.mult, ALU.max, w=[snap("MM")])
        recip_rows(rowb[0:1, 1:2], rowb[0:1, 0:1], w=[snap("VE")])
        MMOP(psA[:, 8:9], ones1, rowb[0:1, 1:2], w=[snap("SE")])
        VOP("tensor_copy", rowb[:, 2:3], psA[:, 8:9], w=[snap("MM")])
        for cc in range(8):
            DMA("LDW", stgs[cc % 2], wiv[:, cc, :], w=[snap("VE")])
            VOP("tensor_scalar", stgs[cc % 2][:, :], stgs[cc % 2][:, :],
                rowb[:, 2:3], MAGIC, ALU.mult, ALU.add, w=[snap("LDW")])
            VOP("tensor_scalar", stgs[cc % 2][:, :], stgs[cc % 2][:, :],
                MAGIC - 1.0, MAGIC, ALU.max, ALU.subtract)
            VOP("tensor_scalar", wq_v[:, cc, :], stgs[cc % 2][:, :],
                1.0, None, ALU.min)
        mark("wprep", "VE")

        ipsevs = {}

        def prepipe(tau):
            c = tau
            slot = (tau + 1) % 2
            par = (tau + 1) % 2
            rb = rowb[:, par * 16:par * 16 + 16]
            xt = x0t.rearrange("p (b d) -> p b d", b=2)[:, slot, :]
            DMA("LDX", xt, x0_d[tau + 1],
                w=[mk(f"ppve{c-2}"), mk(f"ppse{c-2}")])
            lx = snap("LDX")
            st = tau + 1
            VOP("tensor_reduce", rb[:, 0:1], xt, AX.X, ALU.add,
                w=[lx, mk(f"ppse{c-1}"), mk(f"ppgp{c-1}"),
                   mk(f"ipse{c-2}")])
            VOP("scalar_tensor_tensor", ppA[:, :], xt, 1.0, xt,
                ALU.mult, ALU.mult, accum_out=rb[:, 1:2])
            sv = snap("VE")
            VOP("tensor_scalar", mu1_st[:, st:st + 1], rb[:, 0:1], 1.0 / D,
                None, ALU.mult, w=[sv])
            VOP("tensor_tensor", rb[:, 2:3], mu1_st[:, st:st + 1],
                mu1_st[:, st:st + 1], ALU.mult)
            VOP("tensor_scalar", rb[:, 3:4], rb[:, 1:2], 1.0 / D, None,
                ALU.mult)
            VOP("tensor_tensor", rb[:, 3:4], rb[:, 3:4], rb[:, 2:3],
                ALU.subtract)
            sv = snap("VE")
            VOP("tensor_scalar", rb[:, 4:5], rb[:, 3:4], 1e-5, None, ALU.add)
            SOP(rb[:, 4:5], rb[:, 4:5], AF.Ln, w=[snap("VE")])
            SOP(r1_st[:, st:st + 1], rb[:, 4:5], AF.Exp, scale=-0.5)
            se = snap("SE")
            VOP("tensor_tensor", rb[:, 5:6], r1_st[:, st:st + 1],
                r1_st[:, st:st + 1], ALU.mult, w=[se])
            VOP("tensor_tensor", rb[:, 5:6], rb[:, 5:6], rb[:, 3:4], ALU.mult)
            VOP("tensor_scalar", rb[:, 5:6], rb[:, 5:6], 1e-6, None, ALU.add)
            SOP(rb[:, 5:6], rb[:, 5:6], AF.Ln, w=[snap("VE")])
            SOP(rb[:, 6:7], rb[:, 5:6], AF.Exp, scale=-0.5)
            VOP("tensor_tensor", rb[:, 7:8], r1_st[:, st:st + 1], rb[:, 6:7],
                ALU.mult, w=[snap("SE")])
            VOP("tensor_tensor", rb[:, 8:9], mu1_st[:, st:st + 1], rb[:, 7:8],
                ALU.mult)
            VOP("tensor_scalar_mul", rb[:, 8:9], rb[:, 8:9], -1.0)
            SOP(ppA[:, :], xt, AF.Identity, bias=rb[:, 8:9], scale=rb[:, 7:8],
                w=[snap("VE")])
            GOP("tensor_tensor", ppB[:, :], ppA[:, :], nwrep, ALU.mult,
                w=[snap("SE")])
            gv = snap("GP")
            mark(f"ppgp{c}", "GP")
            VOP("tensor_reduce", rb[:, 0:1], ppB[:, :], AX.X, ALU.add, w=[gv])
            VOP("scalar_tensor_tensor", ppA[:, :], ppB[:, :], 1.0, ppB[:, :],
                ALU.mult, ALU.mult, accum_out=rb[:, 1:2])
            sv = snap("VE")
            VOP("tensor_scalar", rb[:, 2:3], rb[:, 0:1], 1.0 / D, None,
                ALU.mult, w=[sv])
            VOP("tensor_tensor", rb[:, 3:4], rb[:, 2:3], rb[:, 2:3], ALU.mult)
            VOP("tensor_scalar", rb[:, 4:5], rb[:, 1:2], 1.0 / D, None,
                ALU.mult)
            VOP("tensor_tensor", rb[:, 4:5], rb[:, 4:5], rb[:, 3:4],
                ALU.subtract)
            VOP("tensor_scalar", rb[:, 4:5], rb[:, 4:5], 1e-5, None, ALU.add)
            SOP(rb[:, 4:5], rb[:, 4:5], AF.Ln, w=[snap("VE")])
            SOP(rb[:, 5:6], rb[:, 4:5], AF.Exp, scale=-0.5)
            VOP("tensor_tensor", rb[:, 9:10], rb[:, 2:3], rb[:, 5:6],
                ALU.mult, w=[snap("SE")])
            VOP("tensor_scalar_mul", rb[:, 9:10], rb[:, 9:10], -1.0)
            SOP(ppA[:, :], ppB[:, :], AF.Identity, bias=rb[:, 9:10],
                scale=rb[:, 5:6], w=[snap("VE")])
            VOP("tensor_reduce", rb[:, 10:11], ppA[:, :], AX.X, ALU.max,
                apply_absolute_value=True, w=[snap("SE")])
            sv = snap("VE")
            VOP("tensor_scalar", rb[:, 10:11], rb[:, 10:11], 1e-5, None,
                ALU.max, w=[sv])
            VOP("tensor_scalar", rb[:, 13:14], rb[:, 10:11], 1.0 / 127.0,
                None, ALU.mult)
            VOP("tensor_copy", rb[:, 11:12], rb[:, 10:11])
            SOP(rb[:, 11:12], rb[:, 11:12], AF.Ln, w=[snap("VE")])
            SOP(rb[:, 12:13], rb[:, 11:12], AF.Exp, scale=-1.0)
            VOP("tensor_scalar", rb[:, 12:13], rb[:, 12:13], 127.0, None,
                ALU.mult, w=[snap("SE")])
            VOP("tensor_scalar", ppB[:, :], ppA[:, :], rb[:, 12:13], MAGIC,
                ALU.mult, ALU.add, w=[mk(f"q1mm{c-1}")])
            VOP("tensor_scalar", ppB[:, :], ppB[:, :], MAGIC - 128.0, MAGIC,
                ALU.max, ALU.subtract)
            VOP("tensor_scalar", q1[:, :], ppB[:, :], 127.0, None, ALU.min)
            mark(f"ppve{c}", "VE")
            mark(f"ppse{c}", "SE")
            qv = snap("VE")
            PET(q1Tv2[(tau + 16) % 2],
                [q1[:, j * 128:(j + 1) * 128] for j in range(8)],
                [bkA[0]], w=[qv, mk(f"ipse{c-1}"), mk(f"q1mm{c-2}")])
            mark(f"q1mm{c}", "MM")
            mark(f"q1ve{c}", "VE")

        def in_proj_dg(tau, dg):
            c = tau
            par = (tau + 1) % 2
            rb = rowb[:, par * 16:par * 16 + 16]
            recip = rb[:, 13:14]
            tpq = mk(f"q1ve{c}")
            q1Tv = q1Tv2[(tau + 16) % 2]
            xv = xbv[:, tau % 2, :]
            g = tau // 2
            sevs = ipsevs[tau]
            d0 = dg * 512
            dw = min(512, DIP - d0)
            bank = bkA[dg % 2][:, 0:dw]
            if dg < 2:
                wmm = [tpq, mk(f"ipse{c-1}")]
            else:
                wmm = [tpq, sevs[dg - 2]]
            for ccx in range(8):
                MMOP(bank, q1Tv[:, ccx, :], wq_v[:, ccx, d0:d0 + dw],
                     start=(ccx == 0), stop=(ccx == 7),
                     w=wmm if ccx == 0 else ())
            mmv = snap("MM")
            if dg < 4:
                SOP(szt[:, d0:d0 + 512], bank, AF.Identity, bias=0.0,
                    scale=recip,
                    w=[mmv, mk(f"sz{c-1}"),
                       mk(f"xdpve{((c - 4) // 2) * 2 - 1}")]
                    if dg == 0 else [mmv])
                if dg == 3:
                    SOP(szt[:, :], szt[:, :], AF.Silu)
            elif dg < 8:
                SOP(xv[:, d0 - DI:d0 - DI + 512], bank, AF.Identity,
                    bias=0.0, scale=recip,
                    w=[mmv, mk(f"cvpet{g-1}")] if dg == 4 else [mmv])
            else:
                SOP(xv[:, 2048:2304], bank[:, 0:256], AF.Identity,
                    bias=0.0, scale=recip, w=[mmv])
                SOP(dt_tm2[:, (tau % 4) * 32:(tau % 4) * 32 + 32],
                    bank[:, 256:288], AF.Identity, bias=0.0, scale=recip,
                    w=[mk(f"dtmm{c-4}")])
            sevs.append(snap("SE"))
            if dg == 3:
                DMA("BNC", sz_dram[tau], szt[:, :], w=[snap("SE")])
                mark(f"sz{c}", "BNC")
            if dg == 8:
                mark(f"ipse{c}", "SE")

        def ip_fillers(tau):
            return [(lambda dg=dg: in_proj_dg(tau, dg)) for dg in range(4, 9)]

        def in_proj(tau, part=None):
            c = tau
            par = (tau + 1) % 2
            rb = rowb[:, par * 16:par * 16 + 16]
            recip = rb[:, 13:14]
            tpq = mk(f"q1ve{c}")
            q1Tv = q1Tv2[(tau + 16) % 2]
            if tau == -1:
                for j in range(18):
                    bank = bkA[j % 2]
                    wmm = [tpq] if j < 2 else [tpq, snap("SE")]
                    for ccx in range(8):
                        MMOP(bank[:, 0:128],
                             wq_v[:, ccx, DI + j * 128:DI + (j + 1) * 128],
                             q1Tv[:, ccx, :], start=(ccx == 0), stop=(ccx == 7),
                             w=wmm if ccx == 0 else ())
                    SOP(cvv[:, j, 13:16], bank[:, 125:128], AF.Identity,
                        bias=0.0, scale=recip, w=[snap("MM")])
                mark("ipse-1", "SE")
                mark("cvse-1", "SE")
                return
            if part in (0, None):
                ipsevs[tau] = []
            dgr = (range(9) if part is None else
                   (range(0, 4) if part == 0 else range(4, 9)))
            for dg in dgr:
                in_proj_dg(tau, dg)

        cvsevs = {}

        def conv_pet(g):
            PET(cvv[:, :, 16:16 + 128],
                [xbv[:, 0, j * 128:(j + 1) * 128] for j in range(18)],
                [bkA[2], bkA[3]],
                w=[mk(f"ipse{2*g}"), mk(f"halo{g-1}"), mk(f"cvse{g-1}"),
                   mk(f"fve{2*g-1}")])
            PET(cvv[:, :, 16 + 128:16 + 256],
                [xbv[:, 1, j * 128:(j + 1) * 128] for j in range(18)],
                [bkA[2], bkA[3]], w=[mk(f"ipse{2*g+1}")])
            mark(f"cvpet{g}", "MM")
            mark(f"cvpve{g}", "VE")

        def conv_j(g, j):
            tpx = mk(f"cvpve{g}")
            sevs = cvsevs[g]
            bank = bkA[2 + (j % 2)][:, 0:256]
            wmm = ([tpx, mk(f"cvse{g-1}"), mk(f"fve{2*g-1}")]
                   if j < 2 else [tpx, sevs[j - 2]])
            for k in range(4):
                MMOP(bank, cdv[:, k, j, :], cvv[:, j, 13 + k:13 + k + 256],
                     start=(k == 0), stop=(k == 3),
                     w=wmm if k == 0 else ())
            mmv = snap("MM")
            if j < 16:
                SOP(xsv[:, j, :], bank, AF.Identity, bias=cb_fm[:, j:j + 1],
                    scale=1.0,
                    w=[mmv, mk(f"fmm{2*g-1}"), mk(f"cbnc{2*g-1}"),
                       mk(f"rtmm{2*g-1}")]
                    if j == 0 else [mmv])
            else:
                SOP(bcv[:, j - 16, :], bank, AF.Identity,
                    bias=cb_fm[:, j:j + 1], scale=1.0, w=[mmv])
            sevs.append(snap("SE"))
            if j == 17:
                SOP(xsil[:, :], xsil[:, :], AF.Silu)
                SOP(bcfm[:, :], bcfm[:, :], AF.Silu)
                mark(f"cvse{g}", "SE")
                mark(f"cvmm{g}", "MM")
                GOP("tensor_copy", cvv[:, :, 13:16], cvv[:, :, 269:272],
                    w=[snap("SE"), mk(f"cvmm{g}")])
                mark(f"halo{g}", "GP")

        def cv_fillers(g):
            cvsevs[g] = []
            out = []
            for j0 in range(0, 18, 3):
                out.append(lambda j0=j0: [conv_j(g, j)
                                          for j in range(j0, j0 + 3)])
            return out

        def conv_mm(g, part=None):
            if part in (0, None):
                cvsevs[g] = []
            jr = (range(18) if part is None else
                  (range(0, 9) if part == 0 else range(9, 18)))
            for j in jr:
                conv_j(g, j)

        def ssd_front1(c):
            par = c % 2
            g = c // 2
            cse = mk(f"cvse{g}")
            if par == 0:
                pbanks = [bkB[2]]
                pw = [cse, mk(f"ydve{c-1}")]
            else:
                pbanks = [bkA[2], bkA[3]]
                pw = [cse, mk(f"cvpve{g+1}")]
            PET(xtvP[par],
                [xsv[:, j, par * 128:(par + 1) * 128] for j in range(16)],
                pbanks, w=pw, evac_w=[mk(f"xdpgp{c-2}")])
            PET(b_tmP[par].rearrange("p (a t) -> p a t", a=1),
                [bcv[:, 0, par * 128:(par + 1) * 128]],
                pbanks, w=[mk(f"smm{c-1}")], evac_w=[mk(f"smm{c-2}")])
            DMA("BNC", c_dram[c], bcv[:, 1, par * 128:(par + 1) * 128],
                w=[cse])
            mark(f"cbnc{c}", "BNC")
            if par == 0:
                TROP(psB[0:64, 0:128],
                     dt_tm2[:, (c % 4) * 32:(c % 4) * 32 + 64],
                     i128, w=[mk(f"ipse{c}"), mk(f"ipse{c+1}"),
                              mk(f"b0ve{c-1}")])
                mark(f"dtmm{c}", "MM")
                M[f"dtmm{c+1}"] = M[f"dtmm{c}"]
                VOP("tensor_copy", dtx_fm, psB[0:64, 0:128],
                    w=[snap("MM"), mk(f"acs{c-1}"), mk(f"fmm{c-1}")])
                SOP(dtx_fm, dtx_fm, AF.Identity, bias=dtb, scale=1.0,
                    w=[snap("VE")])
                se = snap("SE")
                VOP("tensor_scalar_mul", sp1_fm, dtx_fm, -1.0, w=[se])
                VOP("tensor_tensor", sp1_fm, dtx_fm, sp1_fm, ALU.max)
                SOP(sp2_fm, sp1_fm, AF.Exp, bias=0.0, scale=-1.0,
                    w=[snap("VE")])
                VOP("tensor_scalar", sp2_fm, sp2_fm, 1.0, None, ALU.add,
                    w=[snap("SE")])
                SOP(sp2_fm, sp2_fm, AF.Ln, bias=0.0, scale=1.0,
                    w=[snap("VE")])
                se = snap("SE")
                VOP("tensor_scalar", dta_fm, dtx_fm, 0.0, None, ALU.max,
                    w=[se, mk(f"fmm{c-1}")])
                VOP("tensor_tensor", dta_fm, dta_fm, sp2_fm, ALU.add)
                VOP("tensor_scalar", a_fm, dta_fm, nega, None, ALU.mult)
                VOP("tensor_tensor_scan", acs_fm, ones_a, a_fm, 0.0,
                    ALU.mult, ALU.add, w=[snap("VE")])
                mark(f"scan{c}", "VE")
                M[f"scan{c+1}"] = M[f"scan{c}"]
            sva = mk(f"scan{c}")
            DMA("ACS", acs_dram[par],
                acs_fm[par * 32:(par + 1) * 32, :],
                w=[sva, mk(f"acs{c-2}")])
            DMA("ACS", acs8P[par],
                acs_dram[par].rearrange("h t -> (h t)").rearrange(
                    "(g x) -> g x", g=8),
                w=[snap("ACS"), mk(f"rtmm{c-2}")])
            DMA("ACS", asum_row,
                acs_dram[par].rearrange("h t -> t h")[127:128, :],
                w=[mk(f"fmm{c-1}"), mk(f"fse{c-1}")],
                allow_slow_non_contiguous=True)
            mark(f"acs8{c}", "ACS")
            mark(f"acs{c}", "ACS")

        def ssd_front2(c):
            par = c % 2
            g = c // 2
            cse = mk(f"cvse{g}")
            sva = mk(f"scan{c}")
            acsf = mk(f"acs8{c}")
            idn = i128[par * 32:(par + 1) * 32, par * 32:(par + 1) * 32]
            TROP(psB[0:128, 128:160], acs_fm[par * 32:(par + 1) * 32, :],
                 idn, w=[sva])
            TROP(psB[0:128, 160:192], dta_fm[par * 32:(par + 1) * 32, :],
                 idn)
            mmv = snap("MM")
            VOP("tensor_copy", acs_tm, psB[:, 128:160],
                w=[mmv, mk(f"fse{c-1}")])
            VOP("tensor_scalar_mul", nacs_tmP[par], acs_tm, -1.0,
                w=[mk(f"rtse{c-2}")])
            VOP("tensor_copy", dta_tmP[par], psB[:, 160:192],
                w=[mk(f"umgp{c-2}")])
            svt = snap("VE")
            MMOP(psB[:, 192:224], ones1, run_row[0:1, 0:32], w=[svt])
            VOP("tensor_tensor", gtmp, acs_tm, psB[:, 192:224], ALU.add,
                w=[snap("MM"), mk(f"fse{c-1}")])
            SOP(g_st[:, 32 * c:32 * c + 32], gtmp, AF.Exp, w=[snap("VE")])
            SOP(eacsP[par], acs_tm, AF.Exp, w=[mk(f"yove{c-2}")])
            MMOP(psB[:, 224:256], ones1, asum_row[0:1, 0:32],
                 w=[acsf, snap("SE")])
            VOP("tensor_tensor", gtmp, psB[:, 224:256], acs_tm, ALU.subtract,
                w=[snap("MM"), snap("SE")])
            SOP(ds_tmP[par], gtmp, AF.Exp,
                w=[snap("VE"), mk(f"uvgp{c-2}")])
            SOP(rowdec, asum_row, AF.Exp, w=[acsf, mk(f"fmm{c-1}")])
            MMOP(psB[:, 256:288], ones1, rowdec, w=[snap("SE")])
            VOP("tensor_copy", dec_repP[par], psB[:, 256:288],
                w=[snap("MM"), mk(f"psgp{c-2}")])
            VOP("tensor_tensor", run_row, run_row, asum_row, ALU.add,
                w=[snap("MM"), acsf])
            MMOP(psB[:, 288:416], bcv[:, 0, par * 128:(par + 1) * 128],
                 bcv[:, 1, par * 128:(par + 1) * 128],
                 w=[cse, snap("VE")])
            VOP("tensor_tensor", pm_bfP[par], psB[:, 288:416], u01b, ALU.mult,
                w=[snap("MM")])
            mark(f"b0ve{c}", "VE")
            mark(f"fve{c}", "VE")
            mark(f"fmm{c}", "MM")
            mark(f"fse{c}", "SE")

        def ssd_back1(c):
            par = c % 2
            g = c // 2
            yoev = []
            for q in range(4):
                bank = bkB[1 if q % 2 == 0 else 3]
                if q == 0:
                    wmm = [mk(f"cvse{g}"), mk(f"sve{c-1}"), mk(f"rtse{c-1}"),
                           mk(f"pbfve{c-1}"), mk("wprep")]
                elif q >= 2:
                    wmm = [yoev[q - 2]]
                else:
                    wmm = ()
                MMOP(bank, bcv[:, 1, par * 128:(par + 1) * 128],
                     pstate_bf[:, q * 512:(q + 1) * 512], w=wmm)
                VOP("tensor_tensor",
                    y_p[:, q * 512:(q + 1) * 512].rearrange(
                        "p (h q2) -> p h q2", h=8),
                    bank.rearrange("p (h q2) -> p h q2", h=8),
                    bcast2(eacsP[par][:, q * 8:(q + 1) * 8], 64), ALU.mult,
                    w=[snap("MM"), mk(f"ypbnc{c-1}"), mk(f"fse{c}")]
                    if q == 0 else [snap("MM")])
                yoev.append(snap("VE"))
            mark(f"yove{c}", "VE")
            GOP("tensor_tensor", u_t.rearrange("p (h q) -> p h q", h=32),
                x_tmP[par].rearrange("p (h q) -> p h q", h=32),
                bcast2(dta_tmP[par], 64), ALU.mult,
                w=[mk(f"fve{c}"), mk(f"smm{c-1}")])
            mark(f"umgp{c}", "GP")

        def ssd_back2(c, fillers=()):
            par = c % 2
            fillers = list(fillers)
            acsf = mk(f"acs8{c}")
            tau_last = min(2 * (c // 2) + 5, 15)
            GOP("tensor_tensor", xdp.rearrange("p (h q) -> p h q", h=32),
                x_tmP[par].rearrange("p (h q) -> p h q", h=32),
                bcast2(dprep, 64), ALU.mult,
                w=[mk(f"fve{c}"), mk(f"xdpve{c-1}"), mk(f"sz{tau_last}")])
            mark(f"xdpgp{c}", "GP")
            VOP("tensor_tensor", y_p, y_p, xdp, ALU.add,
                w=[snap("GP"), mk(f"yove{c}")])
            mark(f"xdpve{c}", "VE")
            grp_se = []
            grp_ye = []
            grp_sv = []
            grp_sttv = []

            def lam_rt(g8):
                bank = bkB[1 if g8 % 2 == 0 else 3]
                MMOP(bank, sel8b[:, g8 * 128:(g8 + 1) * 128], acs8P[par],
                     w=[acsf, mk(f"yove{c}"), mk(f"rtse{c-1}")] if g8 < 2 else
                     [acsf, grp_se[g8 - 2]])
                mmv = snap("MM")
                ew = [mmv, mk(f"stt{c-1}")] if g8 < 2 else [mmv, grp_sv[g8 - 2]]
                for hh in range(4):
                    SOP(m_v[:, g8 % 2, hh * 128:(hh + 1) * 128],
                        bank[:, hh * 128:(hh + 1) * 128], AF.Exp,
                        bias=nacs_tmP[par][:, g8 * 4 + hh:g8 * 4 + hh + 1],
                        scale=1.0, w=ew if hh == 0 else ())
                VOP("scalar_tensor_tensor",
                    m_v[:, g8 % 2, :].rearrange("p (h t) -> p h t", h=4),
                    m_v[:, g8 % 2, :].rearrange("p (h t) -> p h t", h=4),
                    1.0,
                    pm_bfP[par].unsqueeze(1).broadcast_to([128, 4, 128]),
                    ALU.min, ALU.mult, w=[snap("SE"), mk(f"b0ve{c}")])
                grp_se.append(snap("SE"))
                grp_sv.append(snap("VE"))
                grp_sttv.append(snap("VE"))

            def lam_yd(g8):
                ydbank = bkB[2][:, (g8 % 2) * 256:(g8 % 2) * 256 + 256]
                ydw = [grp_sttv[g8]]
                if g8 >= 2:
                    ydw.append(grp_ye[g8 - 2])
                else:
                    ydw.append(mk(f"fve{c}"))
                    ydw.append(mk(f"umgp{c}"))
                for hh in range(4):
                    h = g8 * 4 + hh
                    MMOP(ydbank[:, hh * 64:(hh + 1) * 64],
                         m_v[:, g8 % 2, hh * 128:(hh + 1) * 128],
                         u_t[:, h * 64:(h + 1) * 64],
                         w=ydw if hh == 0 else ())
                VOP("tensor_tensor", y_p[:, g8 * 256:(g8 + 1) * 256],
                    y_p[:, g8 * 256:(g8 + 1) * 256], ydbank, ALU.add,
                    w=[snap("MM")])
                grp_ye.append(snap("VE"))

            for g8 in range(8):
                lam_rt(g8)
                if fillers:
                    fillers.pop(0)()
                lam_yd(g8)
            while fillers:
                fillers.pop(0)()
            mark(f"rtse{c}", "SE")
            mark(f"stt{c}", "VE")
            mark(f"ydve{c}", "VE")
            mark(f"rtmm{c}", "MM")

        def ssd_back3(c):
            par = c % 2
            GOP("tensor_tensor", u_t.rearrange("p (h q) -> p h q", h=32),
                u_t.rearrange("p (h q) -> p h q", h=32),
                bcast2(ds_tmP[par], 64), ALU.mult, w=[mk(f"rtmm{c}")])
            mark(f"uvgp{c}", "GP")
            GOP("tensor_tensor", pstate.rearrange("p (h q) -> p h q", h=32),
                pstate.rearrange("p (h q) -> p h q", h=32),
                bcast2(dec_repP[par], 64), ALU.mult,
                w=[mk(f"pbfve{c-1}"), mk(f"fve{c}")])
            mark(f"psgp{c}", "GP")
            stev = []
            for q in range(4):
                bank = bkB[1 if q % 2 == 0 else 3]
                if q == 0:
                    wmm = [mk(f"uvgp{c}"), mk(f"rtse{c}")]
                elif q >= 2:
                    wmm = [stev[q - 2]]
                else:
                    wmm = ()
                MMOP(bank, b_tmP[par], u_t[:, q * 512:(q + 1) * 512], w=wmm)
                VOP("tensor_tensor", pstate[:, q * 512:(q + 1) * 512],
                    pstate[:, q * 512:(q + 1) * 512], bank, ALU.add,
                    w=[snap("MM"), snap("GP")] if q == 0 else [snap("MM")])
                stev.append(snap("VE"))
            mark(f"smm{c}", "MM")
            mark(f"sve{c}", "VE")
            VOP("tensor_copy", pstate_bf, pstate, w=[snap("VE")])
            mark(f"pbfve{c}", "VE")
            DMA("BNC", yp_dram[c], y_p,
                w=[mk(f"ydve{c}"), mk(f"xdpve{c}")])
            mark(f"ypbnc{c}", "BNC")

        wov = woutt_d.rearrange("(ch cl) d -> cl ch d", cl=128)
        wstB = [ppA, ppB]

        def w1_thunks():
            out = []
            first = [True]

            def cast2(wvd, wv_, cc):
                for q in range(2):
                    w = ([mk("q1ve15"), mk("ppve15"), mk("ppgp15"),
                          mk("ppse15"), mk("ipse15")]
                         if first[0] else [snap("VE")])
                    first[0] = False
                    DMA("LDX", wstB[(cc + q) % 2], wvd[:, cc + q, :], w=w)
                    VOP("tensor_copy", wv_[:, cc + q, :],
                        wstB[(cc + q) % 2], w=[snap("LDX")])
            for wd, wv_ in [(wrt_d, wrv), (wkt_d, wkv)]:
                wvd = wd.rearrange("(ch cl) d -> cl ch d", cl=128)
                for cc in range(0, 8, 2):
                    out.append(lambda wvd=wvd, wv_=wv_, cc=cc:
                               cast2(wvd, wv_, cc))
            return out

        def w1b_thunks():
            out = []
            first = [True]

            def cast2(cc):
                wvd = wvt_d.rearrange("(ch cl) d -> cl ch d", cl=128)
                for q in range(2):
                    DMA("LDX", wstB[(cc + q) % 2], wvd[:, cc + q, :],
                        w=[snap("VE")])
                    w = ([snap("LDX"), mk("cvmm7")] if first[0]
                         else [snap("LDX")])
                    first[0] = False
                    VOP("tensor_copy", wvv[:, cc + q, :],
                        wstB[(cc + q) % 2], w=w)
            return [lambda cc=cc: cast2(cc) for cc in range(0, 8, 2)]

        def w2_thunks():
            out = []

            def red2(cc):
                for q in range(2):
                    DMA("LDX", wstB[(cc + q) % 2], wov[:, cc + q, :],
                        w=[snap("VE")])
                    VOP("tensor_reduce", wpp[:, cc + q:cc + q + 1],
                        wstB[(cc + q) % 2], AX.X, ALU.add,
                        apply_absolute_value=True, w=[snap("LDX")])

            def tail():
                VOP("tensor_reduce", wpp[:, 16:17], wpp[:, 0:16], AX.X,
                    ALU.add, w=[snap("VE")])
                MMOP(psA[0:1, 0:1], onescol, wpp[:, 16:17],
                     w=[snap("VE"), mk("ipse15"), mk("q1mm15")])
                VOP("tensor_scalar", wpp[0:1, 17:18], psA[0:1, 0:1],
                    1.0 / (DI * D), 1e-5, ALU.mult, ALU.max, w=[snap("MM")])
                recip_rows(wpp[0:1, 18:19], wpp[0:1, 17:18], w=[snap("VE")])
                MMOP(psA[:, 8:9], ones1, wpp[0:1, 18:19], w=[snap("SE")])
                VOP("tensor_copy", wpp[:, 2:3], psA[:, 8:9], w=[snap("MM")])
            for cc in range(0, 16, 2):
                out.append(lambda cc=cc: red2(cc))
            out.append(tail)
            return out

        def w3_thunks():
            first = [True]

            def quant2(cc):
                for q in range(2):
                    DMA("LDX", wstB[(cc + q) % 2], wov[:, cc + q, :],
                        w=[snap("VE")])
                    w = ([snap("LDX"), mk("cvmm7"), mk("ipse15")]
                         if first[0] else [snap("LDX")])
                    first[0] = False
                    VOP("tensor_scalar", wstB[(cc + q) % 2],
                        wstB[(cc + q) % 2], wpp[:, 2:3], MAGIC,
                        ALU.mult, ALU.add, w=w)
                    VOP("tensor_scalar", wstB[(cc + q) % 2],
                        wstB[(cc + q) % 2], MAGIC - 1.0, MAGIC,
                        ALU.max, ALU.subtract)
                    VOP("tensor_scalar", wo_v[:, cc + q, :],
                        wstB[(cc + q) % 2], 1.0, None, ALU.min)
            return [lambda cc=cc: quant2(cc) for cc in range(0, 16, 2)]

        prepipe(-1)
        in_proj(-1)
        prepipe(0)
        in_proj(0)
        prepipe(1)
        in_proj(1)
        conv_pet(0)
        conv_mm(0)
        prepipe(2)
        in_proj(2)
        prepipe(3)
        in_proj(3)
        prepipe(4)
        prepipe(5)
        w3t = None
        for g in range(8):
            if g == 7:
                w3t = w3_thunks()
            ssd_front1(2 * g)
            if g < 6:
                in_proj(2 * g + 4, 0)
            ssd_front2(2 * g)
            ssd_back1(2 * g)
            if g < 7:
                conv_pet(g + 1)
            if g < 6:
                in_proj(2 * g + 4, 1)
            ssd_front1(2 * g + 1)
            if g < 6:
                in_proj(2 * g + 5, 0)
            ssd_back2(2 * g,
                      ip_fillers(2 * g + 5) if g < 6 else
                      (w1_thunks() if g == 6 else
                       w1b_thunks() + w3t[0:4]))
            ssd_front2(2 * g + 1)
            ssd_back3(2 * g)
            ssd_back1(2 * g + 1)
            ssd_back2(2 * g + 1,
                      cv_fillers(g + 1) if g < 7 else w3t[4:8])
            if g == 6:
                for th in w2_thunks():
                    th()
            ssd_back3(2 * g + 1)
            if g < 5:
                prepipe(2 * g + 6)
                prepipe(2 * g + 7)

        DMA("BNC", gin_S, pstate, w=[mk("pbfve15"), snap("VE")])
        DMA("BNC", gin_as, run_row[0:1, 0:32], w=[snap("VE")])
        bncv = snap("BNC")
        E("gpsimd", lambda e: e.collective_compute(
            "AllGather", ALU.bypass, [list(range(8))],
            [gin_S.opt()], [gout_S.opt()]), w=[bncv], i="CC")
        E("gpsimd", lambda e: e.collective_compute(
            "AllGather", ALU.bypass, [list(range(8))],
            [gin_as.opt()], [gout_as.opt()]), i="CC")
        ccv = snap("CC")

        DMA("LDX", sb8_32, gout_as, w=[ccv])
        MMOP(psA[0:8, 32:64], between, sb8_32, w=[snap("LDX"), snap("VE")])
        SOP(sbw8, psA[0:8, 32:64], AF.Exp, w=[snap("MM")])
        VOP("tensor_scalar", sbw8, sbw8, maskj, None, ALU.mult,
            w=[snap("SE")])
        DMA("BNC", w8_dram, sbw8, w=[snap("VE")])
        DMA("BNC", wflat, w8_dram.rearrange("j h -> (j h)").unsqueeze(0),
            w=[snap("BNC")])
        MMOP(psA[:, 256:512], ones1, wflat, w=[snap("BNC")])
        VOP("tensor_copy", wrep, psA[:, 256:512], w=[snap("MM")])
        VOP("memset", stmp[:, :], 0.0)
        gsv = gout_S.rearrange("(r p) d -> r p d", p=128)
        smul = []
        for j in range(8):
            DMA("LDX", sjv[:, j % 2, :], gsv[j],
                w=[ccv, snap("VE")] if j < 2 else [ccv, smul[j - 2]])
            VOP("tensor_tensor", tcorr.rearrange("p (h q) -> p h q", h=32),
                sjv[:, j % 2, :].rearrange("p (h q) -> p h q", h=32),
                bcast2(wrep[:, j * 32:(j + 1) * 32], 64), ALU.mult,
                w=[snap("LDX")])
            smul.append(snap("VE"))
            VOP("tensor_tensor", stmp, stmp, tcorr, ALU.add)
        VOP("tensor_copy", sin_bf, stmp, w=[snap("VE")])
        mark("sinve", "VE")

        def passb_attn_pre_a(c):
            par = c % 2
            DMA("LDX", ypB, yp_dram[c],
                w=[mk("sinve"), mk("attnveL"), mk("attnseL")])
            DMA("LDX", szB, sz_dram[c])
            DMA("LDX", cfmB, c_dram[c])
            xt = x0t.rearrange("p (b d) -> p b d", b=2)[:, 0, :]
            DMA("LDX", xt, x0_d[c + 1], w=[mk("attnseL")])
            lx = snap("LDX")
            ycev = []
            for q in range(4):
                bank = bkA[q % 2]
                if q == 0:
                    wmm = [lx, mk("sinve"), mk("oppveL"), mk("q2mmL")]
                elif q >= 2:
                    wmm = [ycev[q - 2]]
                else:
                    wmm = ()
                MMOP(bank, cfmB, sin_bf[:, q * 512:(q + 1) * 512], w=wmm)
                VOP("tensor_tensor",
                    tcorr[:, q * 512:(q + 1) * 512].rearrange(
                        "p (h q2) -> p h q2", h=8),
                    bank.rearrange("p (h q2) -> p h q2", h=8),
                    bcast2(g_st[:, 32 * c + q * 8:32 * c + (q + 1) * 8], 64),
                    ALU.mult,
                    w=[snap("MM"), ("BNC", cnt["BNC"]), mk("blnL"),
                       mk("qgpL")] if q == 0
                    else [snap("MM")])
                ycev.append(snap("VE"))
            VOP("tensor_tensor", ypB, ypB, tcorr, ALU.add, w=[lx])
            VOP("tensor_tensor", ypB, ypB, szB, ALU.mult)
            VOP("scalar_tensor_tensor", tcorr, ypB, 1.0, ypB, ALU.mult,
                ALU.mult, accum_out=rowb[:, 0:1])
            sv = snap("VE")
            VOP("tensor_scalar", rowb[:, 1:2], rowb[:, 0:1], 1.0 / DI, None,
                ALU.mult, w=[sv])
            VOP("tensor_scalar", rowb[:, 1:2], rowb[:, 1:2], 1e-6, None,
                ALU.add)
            SOP(rowb[:, 1:2], rowb[:, 1:2], AF.Ln, w=[snap("VE")])
            SOP(rowb[:, 2:3], rowb[:, 1:2], AF.Exp, scale=-0.5)
            VOP("scalar_tensor_tensor", tcorr, ypB, rowb[:, 2:3], onwrep,
                ALU.mult, ALU.mult, w=[snap("SE")])

        def passb_attn_pre_b(c):
            par = c % 2
            VOP("tensor_reduce", rowb[:, 3:4], tcorr, AX.X, ALU.add)
            VOP("scalar_tensor_tensor", ypB, tcorr, 1.0, tcorr, ALU.mult,
                ALU.mult, accum_out=rowb[:, 4:5])
            sv = snap("VE")
            VOP("tensor_scalar", rowb[:, 5:6], rowb[:, 3:4], 1.0 / DI, None,
                ALU.mult, w=[sv])
            VOP("tensor_tensor", rowb[:, 6:7], rowb[:, 5:6], rowb[:, 5:6],
                ALU.mult)
            VOP("tensor_scalar", rowb[:, 7:8], rowb[:, 4:5], 1.0 / DI, None,
                ALU.mult)
            VOP("tensor_tensor", rowb[:, 7:8], rowb[:, 7:8], rowb[:, 6:7],
                ALU.subtract)
            VOP("tensor_scalar", rowb[:, 7:8], rowb[:, 7:8], 1e-5, None,
                ALU.add)
            SOP(rowb[:, 7:8], rowb[:, 7:8], AF.Ln, w=[snap("VE")])
            SOP(rowb[:, 8:9], rowb[:, 7:8], AF.Exp, scale=-0.5)
            VOP("tensor_tensor", rowb[:, 9:10], rowb[:, 5:6], rowb[:, 8:9],
                ALU.mult, w=[snap("SE")])
            VOP("tensor_scalar_mul", rowb[:, 9:10], rowb[:, 9:10], -1.0)
            SOP(ypB, tcorr, AF.Identity, bias=rowb[:, 9:10],
                scale=rowb[:, 8:9], w=[snap("VE")])
            VOP("tensor_reduce", rowb[:, 10:11], ypB, AX.X, ALU.max,
                apply_absolute_value=True, w=[snap("SE")])
            sv = snap("VE")
            VOP("tensor_scalar", rowb[:, 10:11], rowb[:, 10:11], 1e-5, None,
                ALU.max, w=[sv])
            VOP("tensor_scalar", rowb[:, 13:14], rowb[:, 10:11], 1.0 / 127.0,
                None, ALU.mult)
            VOP("tensor_copy", rowb[:, 11:12], rowb[:, 10:11])
            SOP(rowb[:, 11:12], rowb[:, 11:12], AF.Ln, w=[snap("VE")])
            SOP(rowb[:, 12:13], rowb[:, 11:12], AF.Exp, scale=-1.0)
            VOP("tensor_scalar", rowb[:, 12:13], rowb[:, 12:13], 127.0, None,
                ALU.mult, w=[snap("SE")])
            VOP("tensor_scalar", tcorr, ypB, rowb[:, 12:13], MAGIC, ALU.mult,
                ALU.add, w=[mk("q2mmL")])
            VOP("tensor_scalar", tcorr, tcorr, MAGIC - 128.0, MAGIC, ALU.max,
                ALU.subtract)
            VOP("tensor_scalar", q2, tcorr, 127.0, None, ALU.min)
            mark("qgpL", "VE")
            qv = snap("VE")
            PET(q2Tv, [q2[:, j * 128:(j + 1) * 128] for j in range(16)],
                [bkA[0], bkA[1]], w=[qv, mk("oppveL")])
            mark("q2mmL", "MM")
            mark("apveL", "VE")
            mark(f"apve{c}", "VE")
            mark("q2tpqL", "VE")

        def passb_attn_pre(c):
            passb_attn_pre_a(c)
            passb_attn_pre_b(c)

        def passb_attn_mm(c):
            par = c % 2
            x1b = x1bv[par] if c != 15 else x1b3
            xt = x0t.rearrange("p (b d) -> p b d", b=2)[:, 0, :]
            lx = ("LDX", cnt["LDX"])
            tpq = mk("q2tpqL")
            for dg in range(2):
                bank = bkA[dg]
                for cc2 in range(16):
                    MMOP(bank, q2Tv[:, cc2, :],
                         wo_v[:, cc2, dg * 512:dg * 512 + 512],
                         start=(cc2 == 0), stop=(cc2 == 15),
                         w=[tpq] if (dg == 0 and cc2 == 0) else ())
            mmv = snap("MM")
            VOP("tensor_scalar", rowb[:, 14:15], r1_st[:, c + 1:c + 2], 1.0,
                None, ALU.add, w=[mmv])
            VOP("tensor_tensor", rowb[:, 15:16], mu1_st[:, c + 1:c + 2],
                r1_st[:, c + 1:c + 2], ALU.mult)
            VOP("tensor_scalar_mul", rowb[:, 15:16], rowb[:, 15:16], -1.0)
            SOP(t1b, xt, AF.Identity, bias=rowb[:, 15:16],
                scale=rowb[:, 14:15],
                w=[snap("VE"), lx, mk("blnL")])
            sev = snap("SE")
            for dg in range(2):
                VOP("scalar_tensor_tensor", x1b[:, dg * 512:dg * 512 + 512],
                    psA[:, dg * 512:dg * 512 + 512], rowb[:, 13:14],
                    t1b[:, dg * 512:dg * 512 + 512], ALU.mult, ALU.add,
                    w=[sev, mmv, mk(f"ffnve{c-2}")]
                    if dg == 0 else ())
            mark("attnveL", "VE")
            mark("attnseL", "SE")
            mark("oppveL", "VE")
            mark(f"attnve{c}", "VE")

        def passb_ln2(c, src):
            rb = rowb[:, 16:32]
            VOP("tensor_reduce", rb[:, 0:1], src, AX.X, ALU.add,
                w=[mk(f"attnve{c}")])
            VOP("scalar_tensor_tensor", lnsink, src, 1.0, src, ALU.mult,
                ALU.mult, accum_out=rb[:, 1:2], w=[mk(f"blnrow{c-1}")])
            sv = snap("VE")
            VOP("tensor_scalar", rb[:, 2:3], rb[:, 0:1], 1.0 / D, None,
                ALU.mult, w=[sv])
            VOP("tensor_tensor", rb[:, 3:4], rb[:, 2:3], rb[:, 2:3], ALU.mult)
            VOP("tensor_scalar", rb[:, 4:5], rb[:, 1:2], 1.0 / D, None,
                ALU.mult)
            VOP("tensor_tensor", rb[:, 4:5], rb[:, 4:5], rb[:, 3:4],
                ALU.subtract)
            VOP("tensor_scalar", rb[:, 4:5], rb[:, 4:5], 1e-5, None, ALU.add)
            SOP(rb[:, 4:5], rb[:, 4:5], AF.Ln, w=[snap("VE"), mk(f"bse{c-1}")])
            SOP(rb[:, 5:6], rb[:, 4:5], AF.Exp, scale=-0.5)
            VOP("tensor_tensor", rb[:, 6:7], rb[:, 2:3], rb[:, 5:6], ALU.mult,
                w=[snap("SE")])
            VOP("tensor_scalar_mul", rb[:, 6:7], rb[:, 6:7], -1.0)
            mark(f"blnrow{c}", "VE")
            SOP(ln2tm, src, AF.Identity, bias=rb[:, 6:7], scale=rb[:, 5:6],
                w=[snap("VE"), mk(f"ffnve{c-1}")])
            sev = snap("SE")
            if c == 0:
                DMA("LDX", lfv[:, :, 15:16],
                    ln2b_dram.rearrange("a (j p) -> p j a", p=128),
                    w=[mk("b2bnc"), mk(f"ffngp{c-1}")],
                    allow_slow_non_contiguous=True)
                hv = [snap("LDX")]
            else:
                VOP("tensor_copy", lfv[:, :, 15:16], lfv[:, :, 143:144],
                    w=[mk(f"ffngp{c-1}"), mk(f"ffnmm{c-1}")])
                hv = [snap("VE")]
            PET(lfv[:, :, 16:144],
                [ln2tm[:, j * 128:(j + 1) * 128] for j in range(8)],
                [bkB[0], bkB[1]],
                w=[sev, mk(f"ffnmm{c-1}"), mk(f"rve{c-1}")] + hv,
                evac_w=[mk(f"ffngp{c-1}")])
            if c == 14:
                VOP("tensor_copy", ln2h15[:, 0:8],
                    lfv[:, :, 143:144].squeeze(2), w=[snap("VE")])
            mark(f"bln{c}", "VE")
            mark("blnL", "VE")

        def passb_ffn_mix(c, x1src):
            VOP("tensor_tensor", dmv, lfv[:, :, 16:144], lfv[:, :, 15:143],
                ALU.subtract, w=[mk(f"bln{c}"), mk(f"ffngp{c-1}")])
            VOP("tensor_tensor", xrv, dmv,
                murep.unsqueeze(2).broadcast_to([128, 8, 128]), ALU.mult,
                w=[mk(f"ffnmm{c-1}")])
            VOP("tensor_tensor", xrv, xrv, lfv[:, :, 15:143], ALU.add)
            GOP("tensor_tensor", xkv, dmv,
                mukrep.unsqueeze(2).broadcast_to([128, 8, 128]), ALU.mult,
                w=[snap("VE"), mk(f"ffnmm{c-1}")])
            GOP("tensor_tensor", xkv, xkv, lfv[:, :, 15:143], ALU.add)
            mark(f"ffngp{c}", "GP")
            mark(f"ffnmixv{c}", "VE")

        def passb_ffn_rest(c, x1src):
            svx = mk(f"ffnmixv{c}")
            ggx = mk(f"ffngp{c}")
            for dg in range(2):
                bank = bkB[dg]
                for cc in range(8):
                    MMOP(bank, xrv[:, cc, :],
                         wrv[:, cc, dg * 512:dg * 512 + 512],
                         start=(cc == 0), stop=(cc == 7),
                         w=[svx, mk(f"blnve{c}") if False else mk(f"bln{c}")]
                         if (dg == 0 and cc == 0) else ())
            mmv = snap("MM")
            SOP(r_bf[:, 0:512], bkB[0], AF.Sigmoid,
                w=[mmv, mk(f"vmm{c-1}")])
            SOP(r_bf[:, 512:1024], bkB[1], AF.Sigmoid)
            mark(f"rse{c}", "SE")
            for dg in range(2):
                bank = bkB[2 + dg]
                for cc in range(8):
                    MMOP(bank, xkv[:, cc, :],
                         wkv[:, cc, dg * 512:dg * 512 + 512],
                         start=(cc == 0), stop=(cc == 7),
                         w=[ggx, mk(f"kse{c-1}"), mk(f"kve{c-1}")]
                         if (dg == 0 and cc == 0) else ())
            mmv = snap("MM")
            SOP(krelu[:, 0:512], bkB[2], AF.Relu, w=[mmv])
            SOP(krelu[:, 512:1024], bkB[3], AF.Relu)
            SOP(kact, krelu, AF.Square)
            mark(f"kse{c}", "SE")
            sev = snap("SE")
            PET(kTv, [kact[:, j * 128:(j + 1) * 128] for j in range(8)],
                [bkB[2], bkB[3]], w=[sev, mk(f"vmm{c-1}")])
            mark(f"kpet{c}", "MM")
            mark(f"kve{c}", "VE")
            tpk = snap("VE")
            for dg in range(2):
                bank = bkB[dg]
                for cc in range(8):
                    MMOP(bank, kTv[:, cc, :],
                         wvv[:, cc, dg * 512:dg * 512 + 512],
                         start=(cc == 0), stop=(cc == 7),
                         w=[tpk, mk(f"rse{c}")]
                         if (dg == 0 and cc == 0) else ())
            mark(f"vmm{c}", "MM")
            mmv = snap("MM")
            for dg in range(2):
                VOP("scalar_tensor_tensor", outb[:, dg * 512:dg * 512 + 512],
                    bkB[dg], 1.0, r_bf[:, dg * 512:dg * 512 + 512],
                    ALU.mult, ALU.mult,
                    w=[mmv, mk(f"rse{c}"), mk(f"osto{c-1}")]
                    if dg == 0 else ())
            VOP("tensor_tensor", outb, outb, x1src, ALU.add)
            mark(f"ffnve{c}", "VE")
            mark(f"rve{c}", "VE")
            DMA("STO", out_d[c * 128:(c + 1) * 128, :], outb, w=[snap("VE")])
            mark(f"osto{c}", "STO")
            mark(f"ffnmm{c}", "MM")
            mark(f"bse{c}", "SE")

        def passb_ffn(c, x1src):
            passb_ffn_mix(c, x1src)
            passb_ffn_rest(c, x1src)

        passb_attn_pre(15)
        passb_attn_mm(15)
        VOP("tensor_copy", x1c15, x1b3, w=[mk("attnveL")])
        DMA("BNC", gin_x1, x1b3[127:128, :], w=[snap("VE")])
        E("gpsimd", lambda e: e.collective_compute(
            "AllGather", ALU.bypass, [list(range(8))],
            [gin_x1.opt()], [gout_x1.opt()]),
          w=[snap("BNC"), snap("VE")], i="CC")
        DMA("LDX", sbgx, gout_x1, w=[snap("CC"), mk("q2mmL"),
                                     mk("oppveL")])
        lgx = snap("LDX")
        MMOP(psA[0:1, 1024:1536], selprev, sbgx[:, 0:512],
             w=[lgx, snap("VE"), snap("SE")])
        MMOP(psA[0:1, 1536:2048], selprev, sbgx[:, 512:1024])
        mark("q2mmL", "MM")
        VOP("tensor_copy", x1brow[0:1, 0:512], psA[0:1, 1024:1536],
            w=[snap("MM"), snap("STO")])
        VOP("tensor_copy", x1brow[0:1, 512:1024], psA[0:1, 1536:2048])
        VOP("tensor_reduce", rowb[0:1, 16:17], x1brow, AX.X, ALU.add,
            w=[snap("VE")])
        VOP("scalar_tensor_tensor", t1b[0:1, :], x1brow, 1.0, x1brow,
            ALU.mult, ALU.mult, accum_out=rowb[0:1, 17:18])
        sv = snap("VE")
        VOP("tensor_scalar", rowb[0:1, 18:19], rowb[0:1, 16:17], 1.0 / D,
            None, ALU.mult, w=[sv])
        VOP("tensor_tensor", rowb[0:1, 19:20], rowb[0:1, 18:19],
            rowb[0:1, 18:19], ALU.mult)
        VOP("tensor_scalar", rowb[0:1, 20:21], rowb[0:1, 17:18], 1.0 / D,
            None, ALU.mult)
        VOP("tensor_tensor", rowb[0:1, 20:21], rowb[0:1, 20:21],
            rowb[0:1, 19:20], ALU.subtract)
        VOP("tensor_scalar", rowb[0:1, 20:21], rowb[0:1, 20:21], 1e-5, None,
            ALU.add)
        SOP(rowb[0:1, 20:21], rowb[0:1, 20:21], AF.Ln, w=[snap("VE")])
        SOP(rowb[0:1, 21:22], rowb[0:1, 20:21], AF.Exp, scale=-0.5)
        VOP("tensor_tensor", rowb[0:1, 22:23], rowb[0:1, 18:19],
            rowb[0:1, 21:22], ALU.mult, w=[snap("SE")])
        VOP("tensor_scalar_mul", rowb[0:1, 22:23], rowb[0:1, 22:23], -1.0)
        VOP("tensor_scalar", ln2brow, x1brow, rowb[0:1, 21:22],
            rowb[0:1, 22:23], ALU.mult, ALU.add)
        VOP("tensor_copy", ln2bb, ln2brow)
        DMA("BNC", ln2b_dram, ln2bb, w=[snap("VE")])
        mark("b2bnc", "BNC")

        passb_attn_pre(0)
        for c in range(15):
            passb_attn_mm(c)
            passb_ln2(c, x1bv[c % 2])
            passb_ffn_mix(c, x1bv[c % 2])
            if c < 14:
                passb_attn_pre_a(c + 1)
            passb_ffn_rest(c, x1bv[c % 2])
            if c < 14:
                passb_attn_pre_b(c + 1)
        passb_ln2(15, x1c15)
        passb_ffn(15, x1c15)

        E("sync", lambda e: e.nop(), w=[snap("STO"), snap("BNC"),
                                        snap("VE"), snap("MM"), snap("SE"),
                                        snap("GP")])

        block = stack.enter_context(nc.Block())
        SELFW = {"vector": "VE", "scalar": "SE", "gpsimd": "GP"}

        def replay(name):
            own = SELFW.get(name)

            def go(eng):
                done = 0
                for wl, fn, pend in SCR[name]:
                    if fn is None:
                        for (n, v) in wl:
                            eng.wait_ge(semh[n], v)
                        continue
                    if own is not None and done > 0:
                        eng.wait_ge(semh[own], done)
                    for (n, v) in wl:
                        if v > 0:
                            eng.wait_ge(semh[n], v)
                    inst = fn(eng)
                    if pend is not None:
                        inst.then_inc(semh[pend[0]], pend[1])
                        if own is not None and pend[0] == own:
                            done += pend[1]
            return go

        block.sync(replay("sync"))
        block.tensor(replay("tensor"))
        block.vector(replay("vector"))
        block.scalar(replay("scalar"))
        block.gpsimd(replay("gpsimd"))

    return nc


_NC_CACHE = [None]


def _host_inputs(inputs):
    f = np.float32
    x = np.ascontiguousarray(
        inputs["L0_attn_hidden_states"].reshape(16384, D)).astype(f)
    conv_w = np.asarray(inputs["L0_attn_conv_w"], f)
    conv_b = np.asarray(inputs["L0_attn_conv_b"], f)
    cdiag = np.zeros((4, 18, 128, 128), f)
    idx = np.arange(128)
    for k in range(4):
        for j in range(18):
            cdiag[k, j, idx, idx] = conv_w[j * 128:(j + 1) * 128, k]
    sel8 = np.zeros((8, 1024), f)
    for g in range(8):
        sel8[g, g * 128:(g + 1) * 128] = 1.0
    base = {
        "wint": np.ascontiguousarray(np.asarray(inputs["L0_attn_in_proj_w"], f).T),
        "woutt": np.ascontiguousarray(np.asarray(inputs["L0_attn_out_proj_w"], f).T),
        "wrt": np.ascontiguousarray(np.asarray(inputs["L0_ffn_W_r"], f).T),
        "wkt": np.ascontiguousarray(np.asarray(inputs["L0_ffn_W_k"], f).T),
        "wvt": np.ascontiguousarray(np.asarray(inputs["L0_ffn_W_v"], f).T),
        "cdiag": cdiag,
        "cbias_fm": np.ascontiguousarray(conv_b.reshape(18, 128).T),
        "nwrep": np.ascontiguousarray(
            np.broadcast_to(np.asarray(inputs["L0_attn_norm_w"], f), (128, D))),
        "onwrep": np.ascontiguousarray(
            np.broadcast_to(np.asarray(inputs["L0_attn_out_norm_w"], f),
                            (128, DI))),
        "murep": np.ascontiguousarray(
            np.asarray(inputs["L0_ffn_mu_r"], f).reshape(8, 128).T),
        "mukrep": np.ascontiguousarray(
            np.asarray(inputs["L0_ffn_mu_k"], f).reshape(8, 128).T),
        "dtb": np.tile(np.asarray(inputs["L0_attn_dt_bias"],
                                  f).reshape(32, 1), (2, 1)),
        "alog": np.tile(np.asarray(inputs["L0_attn_A_log"],
                                   f).reshape(32, 1), (2, 1)),
        "dprep": np.ascontiguousarray(
            np.broadcast_to(np.asarray(inputs["L0_attn_Dp"], f), (128, 32))),
        "ones1": np.ones((1, 128), f),
        "onescol": np.ones((128, 1), f),
        "i128": np.eye(128, dtype=f),
        "u01": np.triu(np.ones((128, 128), f)),
        "sel8": sel8,
    }
    in_maps = []
    for c in range(NCORES):
        m = dict(base)
        xt = np.zeros((NT + 1, 128, D), f)
        s = c * T
        if c == 0:
            xt.reshape(-1, D)[128:] = x[0:T]
        else:
            xt.reshape(-1, D)[:] = x[s - 128:s + T]
        m["x0"] = xt
        sel = np.zeros((8, 1), f)
        if c > 0:
            sel[c - 1, 0] = 1.0
        m["selprev"] = sel
        bt = np.zeros((8, 8), f)
        for k in range(8):
            for j in range(8):
                if j < k < c:
                    bt[k, j] = 1.0
        m["between"] = bt
        mj = np.zeros((8, 1), f)
        mj[:c, 0] = 1.0
        m["maskj"] = mj
        in_maps.append(m)
    return in_maps


def kernel(**inputs):
    if _NC_CACHE[0] is None:
        _NC_CACHE[0] = build_program()
    nc = _NC_CACHE[0]
    in_maps = _host_inputs(inputs)
    res = run_bass_kernel_spmd(nc, in_maps, core_ids=list(range(NCORES)))
    out = np.concatenate([np.asarray(r["out"]) for r in res.results], axis=0)
    return out.reshape(1, 16384, D).astype(np.float32)
